# version 13
# baseline (speedup 1.0000x reference)
import math
import numpy as np
import ml_dtypes
import concourse.bass as bass
import concourse.mybir as mybir
from concourse.bass_utils import run_bass_kernel_spmd

F32 = mybir.dt.float32
BF16 = mybir.dt.bfloat16
AF = mybir.ActivationFunctionType
ALU = mybir.AluOpType
AX = mybir.AxisListType

T = 4096
D = 1024
H = 8
DV = 256
NE = 16
FE = 512
FS = 1024
SCALE = 1.0 / math.sqrt(128.0)
LAMBDA_INIT = 0.8 - 0.6 * math.exp(-0.3)
EPS = 1e-5

ENGINES = ("tensor", "scalar", "vector", "gpsimd", "sync")
SER_SC = False


DMA_POOL = 8


class Op:
    __slots__ = ("eng", "fn", "waits", "dma", "marked", "sem", "val", "pre", "tag")

    def __init__(self, eng, fn, waits, dma, tag):
        self.eng, self.fn, self.dma, self.tag = eng, fn, dma, tag
        self.waits = [w for w in waits if w is not None]
        self.marked = False
        self.sem = None
        self.val = None
        self.pre = None


class Prog:
    def __init__(self):
        self.ops = []

    def add(self, eng, fn, waits=(), dma=None, tag=""):
        op = Op(eng, fn, waits, dma, tag)
        self.ops.append(op)
        return op

    def resolve(self, nc, ctx):
        groups = sorted({op.dma for op in self.ops if op.dma is not None})
        sems = {}
        for e in ENGINES:
            sems["e_" + e] = ctx.enter_context(nc.semaphore("e_" + e))
        pool = {}
        for g in groups:
            pool[g] = [ctx.enter_context(nc.semaphore(f"d_{g}{i}")) for i in range(DMA_POOL)]
        CHAIN = ("vector", "scalar", "gpsimd")
        for op in self.ops:
            kept = []
            for w in op.waits:
                if w.dma is None and w.eng == op.eng and op.dma is None:
                    continue
                w.marked = True
                kept.append(w)
            op.waits = kept
        for op in self.ops:
            if op.eng in CHAIN and op.dma is None:
                op.marked = True
        counts = {("e_" + e): 0 for e in ENGINES}
        rr = {g: 0 for g in groups}
        dma_counts = {}
        for op in self.ops:
            if op.dma is not None:
                op.marked = True
                i = rr[op.dma]
                rr[op.dma] = (i + 1) % DMA_POOL
                sem = pool[op.dma][i]
                prevv = dma_counts.get(id(sem), 0)
                if prevv:
                    op.pre = (sem, prevv)
                op.sem = sem
                op.val = prevv + 16
                dma_counts[id(sem)] = op.val
            elif op.marked:
                cnt = counts["e_" + op.eng]
                if op.eng in CHAIN and cnt > 0:
                    op.pre = (sems["e_" + op.eng], cnt)
                counts["e_" + op.eng] = cnt + 1
                op.sem = sems["e_" + op.eng]
                op.val = cnt + 1
            elif op.eng in CHAIN:
                pass
        self.sems = sems
        self.drain_waits = []
        seen = set()
        for op in self.ops:
            if op.dma is not None and id(op.sem) not in seen:
                seen.add(id(op.sem))
        finals = {}
        for op in self.ops:
            if op.dma is not None:
                finals[id(op.sem)] = (op.sem, op.val)
        self.drain_waits = list(finals.values())

    def emit(self, eng_name, eng):
        watermark = {}
        for op in self.ops:
            if op.eng != eng_name:
                continue
            pend = list(op.waits)
            for w in pend:
                key = id(w.sem)
                if watermark.get(key, -1) >= w.val:
                    continue
                eng.wait_ge(w.sem, w.val)
                watermark[key] = w.val
            if op.pre is not None:
                sem, v = op.pre
                if watermark.get(id(sem), -1) < v:
                    eng.wait_ge(sem, v)
                    watermark[id(sem)] = v
            inst = op.fn(eng)
            if op.marked:
                inst.then_inc(op.sem, 16 if op.dma is not None else 1)
        if eng_name == "sync":
            for sem, v in self.drain_waits:
                if watermark.get(id(sem), -1) < v:
                    eng.wait_ge(sem, v)


def build_program(dbg=()):
    nc = bass.Bass("TRN2", target_bir_lowering=False)
    dt = nc.dram_tensor
    E = {}

    for n, s in (("XT", [D, T]), ("WqT", [128, 16384]), ("WkT", [128, 16384]),
                 ("WvT", [128, 16384]), ("WoT", [128, 16384]),
                 ("lq1", [H, 128]), ("lk1", [H, 128]), ("lq2", [H, 128]), ("lk2", [H, 128]),
                 ("rms_scale", [1, DV]), ("rw", [128, 128]),
                 ("gw", [NE, 128, 4096]), ("uw", [NE, 128, 4096]), ("dw", [NE, 128, 4096]),
                 ("sgw", [128, 8192]), ("suw", [128, 8192]), ("sdw", [128, 8192]),
                 ("identf", [128, 128]), ("onesf", [1, 128])):
        E[n] = dt(n, s, F32, kind="ExternalInput").ap()
    E["iotap"] = dt("iotap", [128, 1], F32, kind="ExternalInput").ap()
    E["eids"] = dt("eids", [1, NE], F32, kind="ExternalInput").ap()
    E["ltri"] = dt("ltri", [128, 128], F32, kind="ExternalInput").ap()
    E["YT"] = dt("YT", [T, D], F32, kind="ExternalOutput").ap()

    def scratch(name, shape, dtype):
        kind = "ExternalOutput" if name in dbg else "Internal"
        return dt(name, shape, dtype, kind=kind).ap()

    if "stop_cmp0" in dbg:
        E["DBG_CMP"] = dt("DBG_CMP", [128, 256], F32, kind="ExternalOutput").ap()
        E["DBG_AI"] = dt("DBG_AI", [128, 32], mybir.dt.int32, kind="ExternalOutput").ap()
    if "probe_u0" in dbg:
        E["DBG_P"] = dt("DBG_P", [128, 4096], BF16, kind="ExternalOutput").ap()
        E["DBG_E2"] = dt("DBG_E2", [128, 4096], BF16, kind="ExternalOutput").ap()
        E["DBG_DEN"] = dt("DBG_DEN", [128, 32], F32, kind="ExternalOutput").ap()
        E["DBG_PT"] = dt("DBG_PT", [128, 4096], BF16, kind="ExternalOutput").ap()
        E["DBG_ON"] = dt("DBG_ON", [128, 256], F32, kind="ExternalOutput").ap()
        E["DBG_OTM"] = dt("DBG_OTM", [128, 256], BF16, kind="ExternalOutput").ap()
        E["DBG_Q"] = dt("DBG_Q", [128, 8192], BF16, kind="ExternalOutput").ap()
        E["DBG_K"] = dt("DBG_K", [128, 8192], BF16, kind="ExternalOutput").ap()
        E["DBG_V"] = dt("DBG_V", [128, 8192], BF16, kind="ExternalOutput").ap()
    E["QTh"] = scratch("QTh", [2048, T], BF16)
    E["KTh"] = scratch("KTh", [2048, T], BF16)
    E["VH"] = scratch("VH", [T, 2048], BF16)
    E["OT"] = scratch("OT", [2048, T], BF16)
    E["X1F"] = scratch("X1F", [D, T], F32)
    E["X1B"] = scratch("X1B", [D, T], BF16)
    E["X1TM"] = scratch("X1TM", [T, D], BF16)
    E["XCOMP"] = scratch("XCOMP", [NE * 2048, D], BF16)
    E["MCOMP"] = scratch("MCOMP", [NE * 2048, 2], F32)
    E["RACC"] = scratch("RACC", [T, D], F32)
    E["SHFM"] = scratch("SHFM", [D, T], BF16)

    from contextlib import ExitStack
    with ExitStack() as ctx:
        E["AB"] = ctx.enter_context(nc.sbuf_tensor("AB", [128, 73728], BF16))
        E["AFr"] = ctx.enter_context(nc.sbuf_tensor("AFr", [128, 15488], F32))
        E["AI"] = ctx.enter_context(nc.sbuf_tensor("AI", [128, 32], mybir.dt.int32))
        E["S1p"] = ctx.enter_context(nc.psum_tensor("S1p", [128, 1024], F32))
        E["S2p"] = ctx.enter_context(nc.psum_tensor("S2p", [128, 1024], F32))
        E["trp0"] = ctx.enter_context(nc.psum_tensor("trp0", [128, 1024], BF16))
        E["trp1"] = ctx.enter_context(nc.psum_tensor("trp1", [128, 1024], BF16))
        E["ovp"] = ctx.enter_context(nc.psum_tensor("ovp", [128, 512], F32))
        E["pob"] = ctx.enter_context(nc.psum_tensor("pob", [128, 1024], BF16))

        p = Prog()
        emit_all(p, E, dbg)
        p.resolve(nc, ctx)

        with nc.Block() as block:
            @block.tensor
            def _(e):
                p.emit("tensor", e)

            @block.scalar
            def _(e):
                p.emit("scalar", e)

            @block.vector
            def _(e):
                p.emit("vector", e)

            @block.gpsimd
            def _(e):
                p.emit("gpsimd", e)

            @block.sync
            def _(e):
                p.emit("sync", e)
    return nc


def emit_all(p, E, dbg):
    AB, AFr, AI = E["AB"], E["AFr"], E["AI"]
    S1p, S2p, trp0, trp1, ovp, pob = E["S1p"], E["S2p"], E["trp0"], E["trp1"], E["ovp"], E["pob"]

    def ab(off, n, p0=0, np_=128):
        return AB[p0:p0 + np_, off:off + n]

    def af(off, n, p0=0, np_=128):
        return AFr[p0:p0 + np_, off:off + n]

    fIDENT, fONES, fRMSB, fLAM = 0, 128, 256, 512
    fBIG = 640
    bIDENT, bONES, bSEL = 0, 128, 256
    bBIG = bSEL + 2048

    fIOTA, fONC, fEIDS, fEIDB = 520, 521, 524, 544
    fLTRI = 15232
    ld_id = p.add("sync", lambda e: e.dma_start(out=af(fIDENT, 128), in_=E["identf"]), dma="win")
    ld_on = p.add("sync", lambda e: e.dma_start(out=af(fONES, 128, 0, 1), in_=E["onesf"]), dma="win")
    ld_io = p.add("sync", lambda e: e.dma_start(out=af(fIOTA, 1), in_=E["iotap"]), dma="win")
    ld_ei = p.add("sync", lambda e: e.dma_start(out=af(fEIDS, NE, 0, 1), in_=E["eids"]), dma="win")
    ld_idb = p.add("gpsimd", lambda e: e.dma_start(out=ab(bIDENT, 128), in_=E["identf"]), dma="wing")
    ms_ones = p.add("gpsimd", lambda e: e.memset(ab(bONES, 128), 1.0))
    ms_onc = p.add("vector", lambda e: e.memset(af(fONC, 1), 1.0))
    mm_ei = p.add("tensor", lambda e: e.matmul(ovp[:, 272:288], af(fONES, 128, 0, 1), af(fEIDS, NE, 0, 1), start=True, stop=True), waits=[ld_on, ld_ei])
    ev_ei = p.add("vector", lambda e: e.tensor_copy(af(fEIDB, NE), ovp[:, 272:288]), waits=[mm_ei])

    fSM = fBIG
    ld_rs = p.add("sync", lambda e: e.dma_start(out=af(fSM, DV, 0, 1), in_=E["rms_scale"]), dma="win")
    mm_rs = p.add("tensor", lambda e: e.matmul(ovp[:, 0:256], af(fONES, 128, 0, 1), af(fSM, DV, 0, 1), start=True, stop=True), waits=[ld_on, ld_rs])
    ev_rs = p.add("vector", lambda e: e.tensor_scalar(af(fRMSB, DV), ovp[:, 0:256], 1.0 - LAMBDA_INIT, None, ALU.mult), waits=[mm_rs])

    fL = fBIG + 256
    fLs = fL + 512
    ldl = []
    for i, t in enumerate(("lq1", "lk1", "lq2", "lk2")):
        ldl.append(p.add("sync", lambda e, t=t, i=i: e.dma_start(out=af(fL + 128 * i, 128, 0, 8), in_=E[t]), dma="win"))
    p.add("vector", lambda e: e.tensor_tensor(af(fL, 128, 0, 8), af(fL, 128, 0, 8), af(fL + 128, 128, 0, 8), ALU.mult), waits=ldl[:2])
    p.add("vector", lambda e: e.tensor_tensor(af(fL + 256, 128, 0, 8), af(fL + 256, 128, 0, 8), af(fL + 384, 128, 0, 8), ALU.mult), waits=ldl[2:])
    rs1 = p.add("vector", lambda e: e.tensor_reduce(af(fLs, 1, 0, 8), af(fL, 128, 0, 8), AX.X, ALU.add))
    rs2 = p.add("vector", lambda e: e.tensor_reduce(af(fLs + 1, 1, 0, 8), af(fL + 256, 128, 0, 8), AX.X, ALU.add))
    ex12 = p.add("scalar", lambda e: e.activation(af(fLs + 2, 2, 0, 8), af(fLs, 2, 0, 8), AF.Exp), waits=[rs1, rs2])
    p.add("vector", lambda e: e.tensor_tensor(af(fLs + 4, 1, 0, 8), af(fLs + 3, 1, 0, 8), af(fLs + 2, 1, 0, 8), ALU.subtract), waits=[ex12])
    nl2 = p.add("vector", lambda e: e.tensor_scalar(af(fLs + 5, 1, 0, 8), af(fLs + 4, 1, 0, 8), -LAMBDA_INIT, None, ALU.add))
    trl = p.add("tensor", lambda e: e.transpose(ovp[0:1, 256:264], af(fLs + 5, 1, 0, 8), af(fIDENT, 8, 0, 8)), waits=[nl2, ev_rs, ld_id])
    evl = p.add("vector", lambda e: e.tensor_copy(af(fLs + 6, 8, 0, 1), ovp[0:1, 256:264]), waits=[trl])
    mml = p.add("tensor", lambda e: e.matmul(ovp[:, 264:272], af(fONES, 128, 0, 1), af(fLs + 6, 8, 0, 1), start=True, stop=True), waits=[evl])
    nlam = p.add("vector", lambda e: e.tensor_copy(af(fLAM, 8), ovp[:, 264:272]), waits=[mml])

    bXTB = bBIG
    bW = bXTB + 32768
    bW2 = bW + 16384
    bSTG = bW2 + 16384
    assert bSTG + 4096 <= 73728

    xchunks = [p.add("gpsimd",
                     lambda e, c=c: e.dma_start(out=ab(bXTB + 4096 * c, 4096), in_=E["XT"][c * 128:(c + 1) * 128, :]),
                     dma="wing") for c in range(8)]
    ldwq = p.add("gpsimd", lambda e: e.dma_start(out=ab(bW, 16384), in_=E["WqT"]), dma="wing")
    ldwk = p.add("gpsimd", lambda e: e.dma_start(out=ab(bW2, 16384), in_=E["WkT"]), dma="wing")

    vh_r = E["VH"].rearrange("(t p) o -> p t o", p=128)

    def proj_fm(dst, woff, wait_w, tag, stg_prev=None, ev_seed=None):
        dmas = []
        prev_stg = stg_prev
        prev_ev = dict(ev_seed) if ev_seed else {}
        for m in range(16):
            evs = []
            for n in range(8):
                bank = (m * 8 + n) % 2
                psl = (S1p[:, 0:512], S1p[:, 512:1024])[bank]
                mms = []
                for c in range(8):
                    w = []
                    if c == 0:
                        w = [wait_w] + xchunks
                        if prev_ev.get(bank) is not None:
                            w.append(prev_ev[bank])
                    mms.append(p.add(
                        "tensor",
                        lambda e, m=m, n=n, c=c, psl=psl, woff=woff: e.matmul(
                            psl, ab(woff + 2048 * c + 128 * m, 128),
                            ab(bXTB + 4096 * c + 512 * n, 512),
                            start=(c == 0), stop=(c == 7)),
                        waits=w, tag=f"{tag}mm"))
                w2 = [mms[-1]]
                if n == 0:
                    if dmas:
                        w2.append(dmas[-1])
                    elif prev_stg is not None:
                        w2.append(prev_stg)
                ev = p.add("scalar", lambda e, n=n, psl=psl: e.copy(ab(bSTG + 512 * n, 512), psl), waits=w2)
                prev_ev[bank] = ev
                evs.append(ev)
            dmas.append(p.add(
                "sync",
                lambda e, m=m, dst=dst: e.dma_start(out=dst[m * 128:(m + 1) * 128, :], in_=ab(bSTG, 4096)),
                waits=evs, dma="out", tag=f"{tag}dma"))
        return dmas, prev_ev

    qdmas, qev = proj_fm(E["QTh"], bW, ldwq, "q")
    ldwv = p.add("gpsimd", lambda e: e.dma_start(out=ab(bW, 16384), in_=E["WvT"]), waits=[qdmas[-1]], dma="wing")
    kdmas, _ = proj_fm(E["KTh"], bW2, ldwk, "k", stg_prev=qdmas[-1], ev_seed=qev)

    vdmas = []
    prev_ev = {}
    for m in range(32):
        evs = []
        for gi in range(4):
            bank = (m * 4 + gi) % 2
            psl = (S2p[:, 0:512], S2p[:, 512:1024])[bank]
            mms = []
            for c in range(8):
                w = []
                if c == 0:
                    w = [ldwv]
                    if prev_ev.get(bank) is not None:
                        w.append(prev_ev[bank])
                mms.append(p.add(
                    "tensor",
                    lambda e, m=m, gi=gi, c=c, psl=psl: e.matmul(
                        psl, ab(bXTB + 4096 * c + 128 * m, 128),
                        ab(bW + 2048 * c + 512 * gi, 512),
                        start=(c == 0), stop=(c == 7)),
                    waits=w, tag="vmm"))
            w2 = [mms[-1]]
            if gi == 0:
                w2.append(vdmas[-1] if vdmas else kdmas[-1])
            ev = p.add("scalar", lambda e, gi=gi, psl=psl: e.copy(ab(bSTG + 512 * gi, 512), psl), waits=w2)
            prev_ev[bank] = ev
            evs.append(ev)
        vdmas.append(p.add(
            "sync",
            lambda e, m=m: e.dma_start(out=vh_r[:, m, :], in_=ab(bSTG, 2048)),
            waits=evs, dma="out"))

    if "stop_a0" in dbg:
        p.add("sync", lambda e: e.nop(), waits=qdmas + kdmas + vdmas)
        return

    bQ = bBIG
    bK = [bBIG + 8192, bBIG + 16384]
    bV = [bBIG + 24576, bBIG + 32768]
    bE1 = [bBIG + 40960, bBIG + 45056, bBIG + 49152]
    bE2 = [bBIG + 53248, bBIG + 57344]
    bPT = bBIG + 61440
    bOST = bPT + 4096
    bOTM = bOST + 1024
    assert bOTM + 512 <= 73728

    fD1 = [fBIG + 1024, fBIG + 1040]
    fD2 = [fBIG + 1028, fBIG + 1044]
    fDEN = [fBIG + 1056, fBIG + 1056 + 16]
    fON = [fBIG + 1100, fBIG + 1100 + 256]
    fSQ = fBIG + 1100 + 512

    qt_r = E["QTh"].rearrange("(c p) t -> p c t", p=128)
    kt_r = E["KTh"].rearrange("(c p) t -> p c t", p=128)

    def load_head(h, buf, waits):
        lq = p.add("sync", lambda e, h=h: e.dma_start(
            out=AB[:, bQ:bQ + 8192].rearrange("p (c t) -> p c t", c=2),
            in_=qt_r[:, 2 * h:2 * h + 2, :]), dma="win",
            waits=waits + (qdmas if h == 0 else []))
        lk = p.add("sync", lambda e, h=h, buf=buf: e.dma_start(
            out=AB[:, bK[buf]:bK[buf] + 8192].rearrange("p (c t) -> p c t", c=2),
            in_=kt_r[:, 2 * h:2 * h + 2, :]), dma="win",
            waits=(kdmas if h == 0 else []))
        lv = p.add("sync", lambda e, h=h, buf=buf: e.dma_start(
            out=AB[:, bV[buf]:bV[buf] + 8192].rearrange("p (t o) -> p t o", t=32),
            in_=vh_r[:, :, 256 * h:256 * h + 256]), dma="win",
            waits=(vdmas if h == 0 else []))
        return [lq, lk, lv]

    NU = H * 32
    st = {
        "loads": {0: load_head(0, 0, [])},
        "exp_last": [None, None],
        "comb": {}, "exps": {}, "dens": {}, "tr_last": {}, "pv": {},
        "epi": {}, "sq": {}, "otm": {}, "otr": {}, "ose": {}, "otdma": {},
        "head_last_exp": {},
        "trev_last": [None, None],
    }
    ot_dmas = []

    def emit_head_prefetch(u):
        h, qt = u // 32, u % 32
        if qt == 0 and h >= 1:
            lq = p.add("sync", lambda e, h=h: e.dma_start(
                out=AB[:, bQ:bQ + 8192].rearrange("p (c t) -> p c t", c=2),
                in_=qt_r[:, 2 * h:2 * h + 2, :]), dma="win",
                waits=[st["head_last_exp"][h - 1]])
            st["loads"][h].append(lq)
        if qt == 2 and h + 1 < H:
            w = []
            if h >= 1:
                w.append(st["pv"][(h - 1) * 32 + 31])
            lk = p.add("sync", lambda e, h1=h + 1, buf=1 - (h % 2): e.dma_start(
                out=AB[:, bK[buf]:bK[buf] + 8192].rearrange("p (c t) -> p c t", c=2),
                in_=kt_r[:, 2 * h1:2 * h1 + 2, :]), dma="win",
                waits=w)
            lv = p.add("sync", lambda e, h1=h + 1, buf=1 - (h % 2): e.dma_start(
                out=AB[:, bV[buf]:bV[buf] + 8192].rearrange("p (t o) -> p t o", t=32),
                in_=vh_r[:, :, 256 * h1:256 * h1 + 256]), dma="win")
            st["loads"][h + 1] = [lk, lv]


    def emit_S_span(u, j):
        h, qt = u // 32, u % 32
        buf = h % 2
        sl = u % 2
        loads = st["loads"][h]
        oQ, oK = bQ, bK[buf]
        exps = st["exps"].setdefault(u, {1: [], 2: []})
        for br, Sp, qoff, koff, bEo, fDx in (
                (1, S1p, oQ, oK, bE1[u % 3], fD1[sl]),
                (2, S2p, oQ + 4096, oK + 4096, bE2[sl], fD2[sl])):
            mms = []
            for half in range(2):
                kb = 2 * j + half
                w = []
                if kb == 0:
                    w = list(loads)
                    if st["exp_last"][br - 1] is not None:
                        w.append(st["exp_last"][br - 1])
                elif half == 0:
                    w.append(exps[br][j - 1])
                mms.append(p.add(
                    "tensor",
                    lambda e, qt=qt, kb=kb, Sp=Sp, qoff=qoff, koff=koff, half=half: e.matmul(
                        Sp[:, 512 * half:512 * half + 512],
                        ab(qoff + 128 * qt, 128), ab(koff + 512 * kb, 512),
                        start=True, stop=True),
                    waits=w, tag=f"s{br}u{u}"))
            ew = list(mms)
            if j == 0:
                if br == 1:
                    if (u - 3) in st["tr_last"]:
                        ew.append(st["tr_last"][u - 3])
                    if (u - 2) in st["dens"]:
                        ew.append(st["dens"][u - 2][0])
                if br == 2 and (u - 2) in st["comb"]:
                    ew.append(st["comb"][u - 2])
            exps[br].append(p.add(
                "scalar",
                lambda e, j=j, Sp=Sp, bEo=bEo, fDx=fDx: e.activation(
                    ab(bEo + 1024 * j, 1024), Sp[:, :], AF.Exp, scale=SCALE,
                    accum_out=af(fDx + j, 1)),
                waits=ew, tag=f"e{br}u{u}"))
            if j == 3:
                st["exp_last"][br - 1] = exps[br][-1]
                if qt == 31 and br == 2:
                    st["head_last_exp"][h] = exps[2][-1]

    def emit_TR(u, ktg_lo, ktg_hi):
        sl = u % 3
        c2 = st["comb"][u]
        for ktg in range(ktg_lo, ktg_hi + 1):
            dst = (trp0, trp1)[ktg % 2]
            trs = []
            for i in range(8):
                kt = 8 * ktg + i
                w = []
                if i == 0:
                    w = [c2, ld_idb]
                    if st["trev_last"][ktg % 2] is not None:
                        w.append(st["trev_last"][ktg % 2])
                trs.append(p.add(
                    "tensor",
                    lambda e, kt=kt, dst=dst, sl=sl: e.transpose(
                        dst[:, 128 * (kt % 8):128 * (kt % 8) + 128],
                        ab(bE1[sl] + 128 * kt, 128), ab(bIDENT, 128)),
                    waits=w, tag=f"tru{u}"))
            w2 = [trs[-1]]
            if (u - 1) in st["pv"]:
                w2.append(st["pv"][u - 1])
            tev = p.add(
                "vector",
                lambda e, ktg=ktg, dst=dst: e.tensor_copy(ab(bPT + 1024 * ktg, 1024), dst[:, :]),
                waits=w2, tag=f"trevu{u}")
            st["trev_last"][ktg % 2] = tev
            st.setdefault("trevs", {}).setdefault(u, []).append(tev)
            if ktg == 3:
                st["tr_last"][u] = trs[-1]

    def emit_PV(u, half):
        h = u // 32
        buf = h % 2
        sl = u % 2
        oV = bV[buf]
        trevs = st["trevs"][u]
        pvs = st.setdefault("pvs", {}).setdefault(u, [])
        for kt in range(16 * half, 16 * half + 16):
            w = []
            if kt % 8 == 0:
                w.append(trevs[kt // 8])
            if kt == 0:
                if (u - 1) in st["epi"]:
                    w.append(st["epi"][u - 1])
                if u == 0:
                    w += [nlam, ev_rs, ev_ei]
            pvs.append(p.add(
                "tensor",
                lambda e, kt=kt, sl=sl, oV=oV: e.matmul(
                    ovp[:, 256 * sl:256 * sl + 256],
                    ab(bPT + 128 * kt, 128), ab(oV + 256 * kt, 256),
                    start=(kt == 0), stop=(kt == 31)),
                waits=w, tag=f"pvu{u}"))
        if half == 1:
            st["pv"][u] = pvs[-1]

    def emit_epilogue(u):
        sl = u % 2
        fDx = fDEN[sl]
        fo = fON[sl]
        ms, r1 = st["dens"][u]
        ew = [st["pv"][u], r1]
        if (u - 2) in st["sq"]:
            ew.append(st["sq"][u - 2])
        epi = p.add("vector", lambda e, sl=sl, fo=fo, fDx=fDx: e.tensor_scalar(
            af(fo, 256), ovp[:, 256 * sl:256 * sl + 256], af(fDx + 4, 1), None, ALU.mult),
            waits=ew, tag=f"epiu{u}")
        st["epi"][u] = epi
        sqm = p.add("vector", lambda e, fo=fo: e.tensor_tensor(
            af(fSQ, 256), af(fo, 256), af(fo, 256), ALU.mult), tag=f"squ{u}")
        st["sq"][u] = sqm
        t2 = p.add("vector", lambda e, fDx=fDx: e.tensor_scalar(
            af(fSQ, 256), af(fSQ, 256), 1.0 / DV, EPS, ALU.mult, ALU.add,
            accum_out=af(fDx + 7, 1)), tag=f"t2u{u}")
        ln_ = p.add("scalar", lambda e, fDx=fDx: e.activation(af(fDx + 8, 1), af(fDx + 7, 1), AF.Ln), waits=[t2])
        rinv = p.add("scalar", lambda e, fDx=fDx: e.activation(af(fDx + 9, 1), af(fDx + 8, 1), AF.Exp, scale=-0.5), tag=f"rinvu{u}")
        p.add("vector", lambda e, fo=fo, fDx=fDx: e.tensor_scalar(af(fo, 256), af(fo, 256), af(fDx + 9, 1), None, ALU.mult), waits=[rinv])
        otm_w = [ev_rs]
        if (u - 2) in st["otr"]:
            otm_w.append(st["otr"][u - 2][-1])
        otm = p.add("vector", lambda e, sl=sl, fo=fo: e.tensor_tensor(
            ab(bOTM + 256 * sl, 256), af(fo, 256), af(fRMSB, 256), ALU.mult), waits=otm_w, tag=f"otmu{u}")
        st["otm"][u] = otm

    def emit_S_dve(u):
        h, qt = u // 32, u % 32
        sl = u % 2
        fDx = fDEN[sl]
        exps = st["exps"][u]
        p.add("vector", lambda e, sl=sl, fDx=fDx: e.tensor_reduce(
            af(fDx, 2), AFr[:, fD1[sl]:fD1[sl] + 8].rearrange("p (b x) -> p b x", b=2),
            AX.X, ALU.add), waits=exps[1] + exps[2])
        rd = p.add("vector", lambda e, fDx=fDx: e.reciprocal(af(fDx + 4, 2), af(fDx, 2)))
        p.add("vector", lambda e, fDx=fDx: e.tensor_tensor(af(fDx + 3, 1), af(fDx, 1), af(fDx + 5, 1), ALU.mult))
        ms = p.add("vector", lambda e, h=h, fDx=fDx: e.tensor_tensor(af(fDx + 6, 1), af(fDx + 3, 1), af(fLAM + h, 1), ALU.mult), waits=[nlam])
        st["dens"][u] = (ms, rd)
        c1 = p.add("vector", lambda e, sl=sl, fDx=fDx: e.tensor_scalar(ab(bE2[sl], 4096), ab(bE2[sl], 4096), af(fDx + 6, 1), None, ALU.mult), tag=f"c1u{u}")
        c2 = p.add("gpsimd", lambda e, sl=sl, s3=u % 3: e.tensor_tensor(ab(bE1[s3], 4096), ab(bE1[s3], 4096), ab(bE2[sl], 4096), ALU.add),
                   waits=[c1, exps[1][-1]], tag=f"c2u{u}")
        st["comb"][u] = c2

    def stage_OTR(u):
        h, qt = u // 32, u % 32
        sl = u % 2
        otrs = []
        for chv in range(2):
            w = [st["otm"][u]]
            if qt % 4 == 0 and chv == 0 and (u - 4) in st["ose"]:
                w += st["ose"][u - 4]
            otrs.append(p.add(
                "tensor",
                lambda e, qt=qt, chv=chv, sl=sl: e.transpose(
                    pob[:, 256 * (qt % 4) + 128 * chv:256 * (qt % 4) + 128 * chv + 128],
                    ab(bOTM + 256 * sl + 128 * chv, 128), ab(bIDENT, 128)),
                waits=w, tag=f"otru{u}"))
        st["otr"][u] = otrs
        if qt % 4 == 3:
            qb = u - 3
            grp = (qt - 3) // 4
            pob_v = pob[:, :].rearrange("p (q x) -> p q x", q=4)
            oses = []
            all_otrs = [o for i in range(4) for o in st["otr"][u - 3 + i]]
            for ch in range(2):
                w = list(all_otrs)
                prevd = st["otdma"].get((h, grp - 1), st["otdma"].get((h - 1, 7)))
                if prevd is not None:
                    w.append(prevd[ch])
                oses.append(p.add(
                    "vector",
                    lambda e, ch=ch: e.tensor_copy(
                        AB[:, bOST + 512 * ch:bOST + 512 * ch + 512].rearrange("p (q x) -> p q x", q=4),
                        pob_v[:, :, 128 * ch:128 * ch + 128]),
                    waits=w, tag=f"oseu{u}"))
            st["ose"][qb] = oses
            pair = []
            for ch in range(2):
                dma = p.add(
                    "sync",
                    lambda e, h=h, grp=grp, ch=ch: e.dma_start(
                        out=E["OT"][h * 256 + ch * 128:h * 256 + ch * 128 + 128, grp * 512:grp * 512 + 512],
                        in_=ab(bOST + 512 * ch, 512)),
                    waits=[oses[ch]], dma="out", tag=f"otdma{h}.{grp}")
                ot_dmas.append(dma)
                pair.append(dma)
            st["otdma"][(h, grp)] = pair

    nheads = H
    for d in dbg:
        if isinstance(d, str) and d.startswith("nh"):
            nheads = int(d[2:])
    NU = nheads * 32

    for u in range(NU + 4):
        cur = u < NU
        prv = 2 <= u <= NU + 1
        if cur:
            emit_head_prefetch(u)
            emit_S_span(u, 0)
        if 4 <= u <= NU + 3:
            stage_OTR(u - 4)
        if prv:
            emit_TR(u - 2, 0, 0)
        if cur:
            emit_S_span(u, 1)
        if prv:
            emit_TR(u - 2, 1, 2)
        if cur:
            emit_S_span(u, 2)
        if prv:
            emit_TR(u - 2, 3, 3)
            emit_PV(u - 2, 0)
        if cur:
            emit_S_span(u, 3)
        if prv:
            emit_PV(u - 2, 1)
            emit_epilogue(u - 2)
        if cur:
            emit_S_dve(u)

    if "stop_a1" in dbg:
        p.add("sync", lambda e: e.nop(), waits=ot_dmas)
        return

    bWO = bBIG
    bOB = [bWO + 16384, bWO + 24576]
    bRB = bWO + 32768
    bRSQ = bRB + 4096
    bX1B = bRSQ + 4096
    bTM = bX1B + 4096
    assert bTM + 2048 <= 73728
    fXT = fBIG + 1024
    fR = fXT + 4096
    fX1 = fR + 4096
    fT = fX1 + 4096
    assert fT + 1536 <= 15488

    ldwo = p.add("gpsimd", lambda e: e.dma_start(out=ab(bWO, 16384), in_=E["WoT"]),
        dma="wing", waits=[st["head_last_exp"][H - 1], st["pv"][NU - 1]])
    ot_r = E["OT"].rearrange("(c p) t -> p c t", p=128)
    xt_r = E["XT"].rearrange("(c p) t -> p c t", p=128)
    x1f_r = E["X1F"].rearrange("(c p) t -> p c t", p=128)
    x1b_r = E["X1B"].rearrange("(c p) t -> p c t", p=128)

    prev = {}
    x1_dmas = []
    for n in range(8):
        bufn = n % 2
        lob_w = list(ot_dmas) if n == 0 else ([prev[("womm", n - 2)]] if ("womm", n - 2) in prev else [])
        lob = p.add("sync", lambda e, n=n, bufn=bufn: e.dma_start(
            out=AB[:, bOB[bufn]:bOB[bufn] + 8192].rearrange("p (c t) -> p c t", c=16),
            in_=ot_r[:, :, 512 * n:512 * n + 512]), dma="win", waits=lob_w)
        lxt = p.add("sync", lambda e, n=n: e.dma_start(
            out=AFr[:, fXT:fXT + 4096].rearrange("p (c t) -> p c t", c=8),
            in_=xt_r[:, :, 512 * n:512 * n + 512]), dma="win",
            waits=[prev[("rev_last", n - 1)]] if ("rev_last", n - 1) in prev else [])
        revs = []
        for m in range(8):
            psl = (S1p[:, 0:512], S1p[:, 512:1024])[m % 2]
            mms = []
            for ec in range(16):
                w = []
                if ec == 0:
                    w = [lob, ldwo]
                    if ("rev", n, m - 2) in prev:
                        w.append(prev[("rev", n, m - 2)])
                    if m < 2 and ("rev", n - 1, 6 + m) in prev:
                        w.append(prev[("rev", n - 1, 6 + m)])
                mms.append(p.add(
                    "tensor",
                    lambda e, m=m, ec=ec, psl=psl, bufn=bufn: e.matmul(
                        psl, ab(bWO + 1024 * ec + 128 * m, 128), ab(bOB[bufn] + 512 * ec, 512),
                        start=(ec == 0), stop=(ec == 15)),
                    waits=w, tag=f"wo{n}.{m}"))
            prev[("womm", n)] = mms[-1]
            rev = p.add("vector", lambda e, m=m, psl=psl: e.tensor_tensor(
                af(fR + 512 * m, 512), psl, af(fXT + 512 * m, 512), ALU.add),
                waits=[mms[-1], lxt], tag=f"rev{n}.{m}")
            prev[("rev", n, m)] = rev
            revs.append(rev)
        rbfs, rsqs = [], []
        for m in range(8):
            rbfs.append(p.add("scalar", lambda e, m=m: e.copy(ab(bRB + 512 * m, 512), af(fR + 512 * m, 512)), waits=[revs[m]]))
            rsqs.append(p.add("scalar", lambda e, m=m: e.activation(ab(bRSQ + 512 * m, 512), af(fR + 512 * m, 512), AF.Square), waits=[revs[m]]))
        summ, sqmm = [], []
        for c in range(8):
            w1 = [rbfs[c], ms_ones]
            w2 = [rsqs[c]]
            if c == 0 and ("stat", n - 1) in prev:
                w1.append(prev[("stat", n - 1)])
                w2.append(prev[("stat", n - 1)])
            summ.append(p.add("tensor", lambda e, c=c: e.matmul(
                S2p[:, 0:512], ab(bONES, 128), ab(bRB + 512 * c, 512), start=(c == 0), stop=(c == 7)), waits=w1))
            sqmm.append(p.add("tensor", lambda e, c=c: e.matmul(
                S2p[:, 512:1024], ab(bONES, 128), ab(bRSQ + 512 * c, 512), start=(c == 0), stop=(c == 7)), waits=w2))
        mb = p.add("vector", lambda e: e.tensor_scalar(af(fT, 512), S2p[:, 0:512], 1.0 / D, None, ALU.mult), waits=[summ[-1]])
        t = p.add("vector", lambda e: e.tensor_scalar(af(fT + 512, 512), S2p[:, 512:1024], 1.0 / D, EPS, ALU.mult, ALU.add), waits=[sqmm[-1]])
        prev[("stat", n)] = t
        p.add("vector", lambda e: e.tensor_tensor(af(fT + 1024, 512), af(fT, 512), af(fT, 512), ALU.mult))
        var = p.add("vector", lambda e: e.tensor_tensor(af(fT + 1024, 512), af(fT + 512, 512), af(fT + 1024, 512), ALU.subtract))
        lnv = p.add("scalar", lambda e: e.activation(af(fT + 512, 512), af(fT + 1024, 512), AF.Ln), waits=[var])
        rstd = p.add("scalar", lambda e: e.activation(af(fT + 1024, 512), af(fT + 512, 512), AF.Exp, scale=-0.5), waits=[lnv])
        x1f_ops, x1b_ops = [], []
        for m in range(8):
            wv = [rstd]
            if ("x1dma", n - 1) in prev and m == 0:
                wv.append(prev[("x1dma", n - 1)])
            p.add("vector", lambda e, m=m: e.tensor_tensor(af(fX1 + 512 * m, 512), af(fR + 512 * m, 512), af(fT, 512), ALU.subtract), waits=wv)
            x1c = p.add("vector", lambda e, m=m: e.tensor_tensor(af(fX1 + 512 * m, 512), af(fX1 + 512 * m, 512), af(fT + 1024, 512), ALU.mult))
            wb = [x1c]
            if m == 0 and ("x1dmab", n - 1) in prev:
                wb.append(prev[("x1dmab", n - 1)])
            x1b = p.add("scalar", lambda e, m=m: e.copy(ab(bX1B + 512 * m, 512), af(fX1 + 512 * m, 512)), waits=wb)
            x1f_ops.append(x1c)
            x1b_ops.append(x1b)
        rl = p.add("vector", lambda e: e.tensor_copy(af(fLAM, 1), af(fLAM, 1)))
        prev[("rev_last", n)] = rl
        d1 = p.add("sync", lambda e, n=n: e.dma_start(
            out=x1f_r[:, :, 512 * n:512 * n + 512],
            in_=AFr[:, fX1:fX1 + 4096].rearrange("p (c t) -> p c t", c=8)),
            waits=x1f_ops, dma="out")
        d2 = p.add("sync", lambda e, n=n: e.dma_start(
            out=x1b_r[:, :, 512 * n:512 * n + 512],
            in_=AB[:, bX1B:bX1B + 4096].rearrange("p (c t) -> p c t", c=8)),
            waits=x1b_ops, dma="out")
        prev[("x1dma", n)] = d1
        prev[("x1dmab", n)] = d2
        x1_dmas += [d1, d2]
        for tp in range(4):
            dstp = (trp0, trp1)[tp % 2]
            trsb = []
            for c in range(8):
                w = [x1b_ops[c]]
                if ("x1tmev", n, tp - 2) in prev:
                    w.append(prev[("x1tmev", n, tp - 2)])
                elif tp < 2 and ("x1tmev", n - 1, tp + 2) in prev:
                    w.append(prev[("x1tmev", n - 1, tp + 2)])
                trsb.append(p.add("tensor", lambda e, c=c, tp=tp, dstp=dstp: e.transpose(
                    dstp[:, 128 * c:128 * c + 128], ab(bX1B + 512 * c + 128 * tp, 128), ab(bIDENT, 128)),
                    waits=w))
            w2 = [trsb[-1]]
            if ("x1tmdma", n, tp - 2) in prev:
                w2.append(prev[("x1tmdma", n, tp - 2)])
            elif tp < 2 and ("x1tmdma", n - 1, tp + 2) in prev:
                w2.append(prev[("x1tmdma", n - 1, tp + 2)])
            tev = p.add("vector", lambda e, tp=tp, dstp=dstp: e.tensor_copy(
                ab(bTM + 1024 * (tp % 2), 1024), dstp[:, :]), waits=w2)
            prev[("x1tmev", n, tp)] = tev
            dtm = p.add("sync", lambda e, n=n, tp=tp: e.dma_start(
                out=E["X1TM"][512 * n + 128 * tp:512 * n + 128 * tp + 128, :],
                in_=ab(bTM + 1024 * (tp % 2), 1024)), waits=[tev], dma="out")
            prev[("x1tmdma", n, tp)] = dtm
            x1_dmas.append(dtm)

    if "stop_b" in dbg:
        p.add("sync", lambda e: e.nop(), waits=x1_dmas)
        return

    C = 1024
    CC = 2048
    bXB = bBIG
    bSH = bXB + 8192
    bINT = bSH + 8192
    bDWN = bINT + 8192
    bWG = [bDWN + 8192, bDWN + 20480, bDWN + 32768]
    bG = bDWN + 45056
    bRW = bG + 512
    bSHF = bRW + 128
    bZ = bXB
    assert bSHF + 1024 <= 73728
    fLG = fBIG + 1024
    fLGT = fLG + 1024
    fPRB = fLGT + 128
    fPM = fPRB + 112
    fMB = fPM + 16
    fCW = fMB + 16
    fMET = fCW + 64
    fROFF = fMET + 4
    fMETA = fROFF + 16
    fSC = fMETA + 32
    fX1Q = fSC + 2048
    fR2 = fX1Q + 1024
    fCMB = fR2 + 1024
    fLNS = fCMB + 1024
    fRAC = fLNS + 1024
    fT2 = fRAC + 1024
    assert fT2 + 64 <= 14464

    ldrw = p.add("gpsimd", lambda e: e.dma_start(out=ab(bRW, 128), in_=E["rw"]), dma="wing",
        waits=[prev[("womm", 7)]])
    ld_lt = p.add("sync", lambda e: e.dma_start(out=af(fLTRI, 128), in_=E["ltri"]), dma="win",
        waits=[prev[("x1dma", 7)], prev[("x1dmab", 7)]])

    msz = p.add("gpsimd", lambda e: e.memset(ab(bZ, 4096), 0.0), waits=[prev[("womm", 7)]])
    zdmas = []
    for blk in range(64):
        zdmas.append(p.add("sync", lambda e, blk=blk: e.dma_start(
            out=E["XCOMP"][512 * blk:512 * (blk + 1), :].rearrange("(b p) c -> p b c", p=128),
            in_=AB[:, bZ:bZ + 4096].rearrange("p (b c) -> p b c", b=4)), waits=[msz], dma="out"))
    for blk in range(8):
        zdmas.append(p.add("gpsimd", lambda e, blk=blk: e.dma_start(
            out=E["RACC"][512 * blk:512 * (blk + 1), :].rearrange("(b p) c -> p b c", p=128),
            in_=AB[:, bZ:bZ + 4096].rearrange("p (b c) -> p b c", b=4)), waits=[msz], dma="wing"))
    zdmas.append(p.add("gpsimd", lambda e: e.dma_start(
        out=E["MCOMP"].rearrange("(p x) m -> p (x m)", p=128),
        in_=AB[:, bZ:bZ + 512]), waits=[msz], dma="wing"))
    zro = p.add("vector", lambda e: e.memset(af(fROFF, NE, 0, 1), 0.0))

    y_dmas = []
    cp = {}
    sc_dmas = []
    sh_dmas = []
    x1b_r2 = x1b_r
    for Q in range(4):
        tq = 1024 * Q
        lxb = p.add("sync", lambda e, tq=tq: e.dma_start(
            out=AB[:, bXB:bXB + 8192].rearrange("p (c t) -> p c t", c=8),
            in_=x1b_r2[:, :, tq:tq + 1024]), dma="win",
            waits=([prev[("x1dmab", n)] for n in range(8)] + zdmas if Q == 0 else [cp[("shdma", Q - 1)]]))
        lgevs = []
        for tt in range(2):
            mms = []
            for c in range(8):
                w = []
                if c == 0:
                    w = [lxb, ldrw]
                    if tt == 1:
                        w.append(lgevs[0])
                    elif ("ovp_free",) in cp:
                        w.append(cp[("ovp_free",)])
                    elif Q == 0:
                        w.append(st["epi"].get(NU - 1))
                mms.append(p.add("tensor", lambda e, tt=tt, c=c: e.matmul(
                    ovp[0:16, 0:512], ab(bRW + 16 * c, 16, 0, 128), ab(bXB + 1024 * c + 512 * tt, 512),
                    start=(c == 0), stop=(c == 7)), waits=w))
            lgevs.append(p.add("vector", lambda e, tt=tt: e.tensor_copy(af(fLG + 512 * tt, 512, 0, 16), ovp[0:16, 0:512]), waits=[mms[-1]]))
        trls = []
        for j in range(8):
            w = [lgevs[-1]] if j == 0 else []
            trls.append(p.add("tensor", lambda e, j=j: e.transpose(
                ovp[:, 16 * j:16 * j + 16], af(fLG + 128 * j, 128, 0, 16), af(fIDENT, 16, 0, 16)), waits=w))
        lgt = p.add("vector", lambda e: e.tensor_copy(af(fLGT, 128), ovp[:, 0:128]), waits=[trls[-1]])
        for j in range(8):
            g = Q * 8 + j
            p.add("vector", lambda e, j=j: e.reduce_max(af(fPRB, 1), af(fLGT + 16 * j, 16), AX.X))
            nm1 = p.add("vector", lambda e: e.tensor_scalar(af(fPRB + 5, 1), af(fPRB, 1), -1.0, None, ALU.mult))
            ex = p.add("scalar", lambda e, j=j: e.activation(af(fPRB + 16, 16), af(fLGT + 16 * j, 16), AF.Exp, bias=af(fPRB + 5, 1), accum_out=af(fPRB + 1, 1)), waits=[nm1])
            p.add("vector", lambda e: e.reciprocal(af(fPRB + 2, 1), af(fPRB + 1, 1)), waits=[ex])
            p.add("vector", lambda e: e.tensor_scalar(af(fPRB + 32, 16), af(fPRB + 16, 16), af(fPRB + 2, 1), None, ALU.mult))
            p.add("vector", lambda e: e.reduce_max(af(fPRB + 3, 1), af(fPRB + 32, 16), AX.X))
            eqo = p.add("vector", lambda e: e.tensor_scalar(af(fPRB + 48, 16), af(fPRB + 32, 16), af(fPRB + 3, 1), None, ALU.is_equal))
            p.add("vector", lambda e: e.tensor_tensor(af(fPRB + 64, 16), af(fPRB + 32, 16), af(fPRB + 48, 16), ALU.mult))
            p.add("vector", lambda e: e.tensor_tensor(af(fPRB + 64, 16), af(fPRB + 32, 16), af(fPRB + 64, 16), ALU.subtract))
            p.add("vector", lambda e: e.reduce_max(af(fPRB + 4, 1), af(fPRB + 64, 16), AX.X))
            geo = p.add("vector", lambda e: e.tensor_scalar(af(fMB, 16), af(fPRB + 32, 16), af(fPRB + 4, 1), None, ALU.is_ge))
            w0 = [geo, ld_lt, ld_on, ms_onc]
            if ("cmp_mm",) in cp:
                w0.append(cp[("cmp_mm",)])
            mmro = p.add("tensor", lambda e: e.matmul(ovp[:, 0:16], af(fONES, NE * 0 + 128, 0, 1), af(fROFF, NE, 0, 1), start=True, stop=False), waits=w0 + [zro])
            mmpf = p.add("tensor", lambda e: e.matmul(ovp[:, 0:16], af(fLTRI, 128), af(fMB, 16), start=False, stop=True))
            mmtot = p.add("tensor", lambda e: e.matmul(ovp[0:1, 16:32], af(fONC, 1), af(fMB, 16), start=True, stop=True))
            evpm = p.add("vector", lambda e: e.tensor_copy(af(fPM, 16), ovp[:, 0:16]), waits=[mmpf])
            roff = p.add("vector", lambda e: e.tensor_tensor(af(fROFF, NE, 0, 1), af(fROFF, NE, 0, 1), ovp[0:1, 16:32], ALU.add), waits=[mmtot])
            cp[("cmp_mm",)] = roff
            scs = []
            for rk, (mname, moff, cvoff) in enumerate((("A", fPRB + 48, fPRB + 3), ("B", fPRB + 64, fPRB + 4))):
                if rk == 1:
                    p.add("vector", lambda e: e.tensor_tensor(af(fPRB + 64, 16), af(fMB, 16), af(fPRB + 48, 16), ALU.subtract))
                p.add("vector", lambda e, moff=moff: e.tensor_tensor(af(fCW, 16), af(moff, 16), af(fEIDB, 16), ALU.mult), waits=[ev_ei])
                p.add("vector", lambda e: e.tensor_reduce(af(fCW + 32, 1), af(fCW, 16), AX.X, ALU.add))
                p.add("vector", lambda e, moff=moff: e.tensor_tensor(af(fCW + 16, 16), af(moff, 16), af(fPM, 16), ALU.mult), waits=[evpm])
                p.add("vector", lambda e: e.tensor_reduce(af(fCW + 33, 1), af(fCW + 16, 16), AX.X, ALU.add))
                p.add("vector", lambda e: e.tensor_scalar(af(fCW + 33, 1), af(fCW + 33, 1), float(CC - 1), None, ALU.min))
                slf = p.add("vector", lambda e: e.tensor_scalar(af(fCW + 34, 1), af(fCW + 32, 1), float(CC), af(fCW + 33, 1), ALU.mult, ALU.add))
                wsli = [slf]
                if ("scx", g - 1, rk) in cp:
                    wsli.append(cp[("scx", g - 1, rk)])
                sli = p.add("vector", lambda e, rk=rk: e.tensor_copy(AI[:, rk:rk + 1], af(fCW + 34, 1)), waits=wsli)
                p.add("vector", lambda e, rk=rk, cvoff=cvoff: e.tensor_copy(af(fMET + 2 * rk, 1), af(cvoff, 1)))
                mt = p.add("vector", lambda e, rk=rk, g=g: e.tensor_scalar(af(fMET + 2 * rk + 1, 1), af(fIOTA, 1), 1.0, float(g * 128), ALU.mult, ALU.add), waits=[ld_io])
                wsc = [sli, mt] + (zdmas if g == 0 else [])
                if ("scx", g - 2, rk) in cp:
                    wsc.append(cp[("scx", g - 2, rk)])
                if rk == 0:
                    wst = [prev[("x1tmdma", g // 4, g % 4)]]
                    if ("scx", g - 2, 1) in cp:
                        wst.append(cp[("scx", g - 2, 1)])
                    cp[("xst", g)] = p.add("sync", lambda e, g=g: e.dma_start(
                        out=ab(bDWN + 1024 * (g % 2), 1024),
                        in_=E["X1TM"][g * 128:(g + 1) * 128, :]), waits=wst, dma="win")
                if SER_SC and ("ser_sc",) in cp:
                    wsc.append(cp[("ser_sc",)])
                scx = p.add("gpsimd", lambda e, rk=rk, g=g: e.indirect_dma_start(
                    E["XCOMP"], bass.IndirectOffsetOnAxis(AI[:, rk:rk + 1], 0),
                    ab(bDWN + 1024 * (g % 2), 1024), None), waits=wsc + [cp[("xst", g)]], dma="wsc")
                scm = p.add("gpsimd", lambda e, rk=rk: e.indirect_dma_start(
                    E["MCOMP"], bass.IndirectOffsetOnAxis(AI[:, rk:rk + 1], 0),
                    AFr[:, fMET + 2 * rk:fMET + 2 * rk + 2], None),
                    waits=[scx] if SER_SC else [], dma="wsc")
                if SER_SC:
                    cp[("ser_sc",)] = scm
                cp[("scx", g, rk)] = scm
                scs += [scx, scm]
            sc_dmas += scs
            if "stop_cmp0" in dbg and g == 0:
                dws = scs + [roff]
                dd = []
                dd.append(p.add("sync", lambda e: e.dma_start(out=E["DBG_CMP"][:, 0:112], in_=af(fPRB, 112)), waits=dws, dma="out"))
                dd.append(p.add("sync", lambda e: e.dma_start(out=E["DBG_CMP"][:, 112:128], in_=af(fPM, 16)), waits=dws, dma="out"))
                dd.append(p.add("sync", lambda e: e.dma_start(out=E["DBG_CMP"][:, 128:144], in_=af(fMB, 16)), waits=dws, dma="out"))
                dd.append(p.add("sync", lambda e: e.dma_start(out=E["DBG_CMP"][:, 144:208], in_=af(fCW, 64)), waits=dws, dma="out"))
                dd.append(p.add("sync", lambda e: e.dma_start(out=E["DBG_CMP"][0:1, 208:224], in_=af(fROFF, NE, 0, 1)), waits=dws, dma="out"))
                dd.append(p.add("sync", lambda e: e.dma_start(out=E["DBG_AI"][:, 0:2], in_=AI[:, 0:2]), waits=dws, dma="out"))
                dd.append(p.add("sync", lambda e: e.dma_start(out=E["DBG_CMP"][:, 224:240], in_=af(fLTRI, 16)), waits=dws, dma="out"))
                dd.append(p.add("sync", lambda e: e.dma_start(out=E["DBG_CMP"][:, 240:241], in_=AFr[:, fLTRI + 127:fLTRI + 128], allow_slow_non_contiguous=True), waits=dws, dma="out"))
                dd.append(p.add("sync", lambda e: e.dma_start(out=E["DBG_CMP"][0:1, 241:242], in_=af(fONC, 1, 0, 1)), waits=dws, dma="out"))
                dd.append(p.add("sync", lambda e: e.dma_start(out=E["DBG_CMP"][0:1, 242:250], in_=af(fONES, 8, 0, 1)), waits=dws, dma="out"))
                p.add("sync", lambda e: e.nop(), waits=dd)
                return
        lsg = p.add("gpsimd", lambda e: e.dma_start(out=ab(bWG[0], 8192), in_=E["sgw"]), dma="wing",
            waits=([ldrw, prev[("x1dmab", 7)], prev[("x1dma", 7)]] if Q == 0 else [cp[("shdn_last",)]]))
        lsu = p.add("gpsimd", lambda e: e.dma_start(out=ab(bWG[1], 8192), in_=E["suw"]), dma="wing")
        last_sil = cp.get(("lnb_pe_done",))
        last_mi = cp.get(("sh_mi_last",))
        gu_done = []
        for fm in range(8):
            for tt in range(2):
                gm = []
                for c in range(8):
                    w = []
                    if c == 0:
                        w = [lsg, lxb]
                        if last_sil is not None:
                            w.append(last_sil)
                    gm.append(p.add("tensor", lambda e, fm=fm, tt=tt, c=c: e.matmul(
                        S1p[:, 0:512], ab(bWG[0] + 1024 * c + 128 * fm, 128), ab(bXB + 1024 * c + 512 * tt, 512),
                        start=(c == 0), stop=(c == 7)), waits=w))
                sil = p.add("scalar", lambda e: e.activation(ab(bG, 512), S1p[:, 0:512], AF.Silu), waits=[gm[-1], (last_mi if last_mi is not None else None)], tag=f"sil{Q}.{fm}.{tt}")
                last_sil = sil
                um = []
                for c in range(8):
                    w = []
                    if c == 0:
                        w = [lsu]
                        if last_mi is not None:
                            w.append(last_mi)
                    um.append(p.add("tensor", lambda e, fm=fm, tt=tt, c=c: e.matmul(
                        S1p[:, 512:1024], ab(bWG[1] + 1024 * c + 128 * fm, 128), ab(bXB + 1024 * c + 512 * tt, 512),
                        start=(c == 0), stop=(c == 7)), waits=w))
                mi = p.add("vector", lambda e, fm=fm, tt=tt: e.tensor_tensor(
                    ab(bINT + 1024 * fm + 512 * tt, 512), ab(bG, 512), S1p[:, 512:1024], ALU.mult),
                    waits=[um[-1], sil], tag=f"smi{Q}.{fm}.{tt}")
                last_mi = mi
                gu_done.append(mi)
        cp[("sh_mi_last",)] = last_mi
        lsd = p.add("gpsimd", lambda e: e.dma_start(out=ab(bWG[0], 8192), in_=E["sdw"]), dma="wing", waits=[gu_done[-1], last_sil])
        last_rc = [cp.get(("lnb_pe_done",)), cp.get(("lnb_pe_done",))]
        dn_ops = []
        for dm in range(8):
            for tt in range(2):
                bank = tt
                psl = (S2p[:, 0:512], S2p[:, 512:1024])[bank]
                mm = []
                for c in range(8):
                    w = []
                    if c == 0:
                        w = [lsd] + gu_done[tt::2]
                        if last_rc[bank] is not None:
                            w.append(last_rc[bank])
                    mm.append(p.add("tensor", lambda e, dm=dm, tt=tt, c=c, psl=psl: e.matmul(
                        psl, ab(bWG[0] + 1024 * c + 128 * dm, 128), ab(bINT + 1024 * c + 512 * tt, 512),
                        start=(c == 0), stop=(c == 7)), waits=w))
                w2 = [mm[-1]]
                if ("shdma", Q - 1) in cp and dm == 0 and tt == 0:
                    w2.append(cp[("shdma", Q - 1)])
                rc = p.add("vector", lambda e, dm=dm, tt=tt, psl=psl: e.tensor_copy(
                    ab(bSH + 1024 * dm + 512 * tt, 512), psl), waits=w2, tag=f"sracc{Q}.{dm}.{tt}")
                last_rc[bank] = rc
                dn_ops.append(rc)
        cp[("shdn_last",)] = dn_ops[-1]
        cp[("lnb_pe_done",)] = dn_ops[-1]
        shd = p.add("sync", lambda e, tq=tq: e.dma_start(
            out=E["SHFM"].rearrange("(c p) t -> p c t", p=128)[:, :, tq:tq + 1024],
            in_=AB[:, bSH:bSH + 8192].rearrange("p (c t) -> p c t", c=8)), waits=dn_ops, dma="out")
        sh_dmas.append(shd)
        cp[("shdma", Q)] = shd
        cp[("ovp_free",)] = cp[("cmp_mm",)]

    for ei in range(NE * 2):
        wi = ei // 2
        wbuf = (wi + 2) % 3
        if ei % 2 == 0:
            if wi == 0:
                wld_w = [sh_dmas[-1]]
            elif wi == 1:
                wld_w = [cp[("shdn_last",)]]
            elif wi == 2:
                wld_w = [cp[("sh_mi_last",)]]
            else:
                wld_w = [cp[("emi_last", ei - 3)], cp[("edn_mm_last", ei - 3)]]
            lg_ = p.add("gpsimd", lambda e, wi=wi, wbuf=wbuf: e.dma_start(
                out=ab(bWG[wbuf], 4096), in_=E["gw"][wi]), dma="wing", waits=wld_w)
            lu_ = p.add("gpsimd", lambda e, wi=wi, wbuf=wbuf: e.dma_start(
                out=ab(bWG[wbuf] + 4096, 4096), in_=E["uw"][wi]), dma="wing")
            ld_ = p.add("gpsimd", lambda e, wi=wi, wbuf=wbuf: e.dma_start(
                out=ab(bWG[wbuf] + 8192, 4096), in_=E["dw"][wi]), dma="wing")
        wload = list(sc_dmas) if ei == 0 else [cp[("etr_last", ei - 1)]]
        lxc = []
        for t in range(8):
            lxc.append(p.add("sync", lambda e, ei=ei, t=t: e.dma_start(
                out=ab(bXB + 1024 * t, 1024),
                in_=E["XCOMP"][ei * C + 128 * t:ei * C + 128 * (t + 1), :]),
                waits=wload if t == 0 else [], dma="win"))
        mto = fMETA + 16 * (ei % 2)
        tbs = 8 + 8 * (ei % 2)
        lmt = p.add("sync", lambda e, ei=ei, mto=mto: e.dma_start(
            out=AFr[:, mto:mto + 16].rearrange("p (t m) -> p t m", t=8),
            in_=E["MCOMP"][ei * C:(ei + 1) * C, :].rearrange("(t p) m -> p t m", p=128)),
            waits=[lxc[0]], dma="win")
        tidc = []
        for t in range(8):
            tidc.append(p.add("vector", lambda e, t=t, mto=mto, tbs=tbs: e.tensor_copy(AI[:, tbs + t:tbs + t + 1], af(mto + 2 * t + 1, 1)), waits=[lmt]))
        xtrev = []
        for c in range(8):
            dstp = (trp0, trp1)[c % 2]
            trs = []
            for t in range(8):
                w = [lxc[t]]
                if t == 0:
                    if c >= 2:
                        w.append(cp[("xtrev", ei, c - 2)])
                    elif ("odtrev", ei - 1, c + 6) in cp:
                        w.append(cp[("odtrev", ei - 1, c + 6)])
                    elif ei == 0:
                        w.append(st["trev_last"][c % 2])
                trs.append(p.add("tensor", lambda e, c=c, t=t, dstp=dstp: e.transpose(
                    dstp[:, 128 * t:128 * t + 128], ab(bXB + 1024 * t + 128 * c, 128), ab(bIDENT, 128)), waits=w))
            w2 = [trs[-1]]
            if ("xgate", ei - 1) in cp and c < 2:
                w2.append(cp[("xgate", ei - 1)])
            if ei == 0:
                w2.append(sh_dmas[-1])
            tev = p.add("vector", lambda e, c=c, dstp=dstp: e.tensor_copy(ab(bSH + 1024 * c, 1024), dstp[:, :]), waits=w2)
            cp[("xtrev", ei, c)] = tev
            xtrev.append(tev)
        cp[("etr_last", ei)] = xtrev[-1]
        last_esil = cp.get(("esil_last",))
        last_emi = cp.get(("emi_bank_last",))
        emi = []
        for fm in range(4):
            for tt in range(2):
                gm = []
                for c in range(8):
                    w = []
                    if c == 0:
                        w = [lg_, xtrev[c]] if fm == 0 and tt == 0 else [lg_]
                        w += [x for x in (last_esil,) if x is not None]
                        if fm == 0 and tt == 0:
                            w += xtrev
                    gm.append(p.add("tensor", lambda e, fm=fm, tt=tt, c=c, wbuf=wbuf: e.matmul(
                        S1p[:, 0:512], ab(bWG[wbuf] + 512 * c + 128 * fm, 128), ab(bSH + 1024 * c + 512 * tt, 512),
                        start=(c == 0), stop=(c == 7)), waits=w))
                cp[("xgate", ei)] = gm[-1]
                sil = p.add("scalar", lambda e: e.activation(ab(bG, 512), S1p[:, 0:512], AF.Silu),
                            waits=[gm[-1]] + ([last_emi] if last_emi is not None else []))
                last_esil = sil
                um = []
                for c in range(8):
                    w = []
                    if c == 0:
                        w = [lu_]
                        if last_emi is not None:
                            w.append(last_emi)
                    um.append(p.add("tensor", lambda e, fm=fm, tt=tt, c=c, wbuf=wbuf: e.matmul(
                        S1p[:, 512:1024], ab(bWG[wbuf] + 4096 + 512 * c + 128 * fm, 128), ab(bSH + 1024 * c + 512 * tt, 512),
                        start=(c == 0), stop=(c == 7)), waits=w))
                mi1 = p.add("vector", lambda e, fm=fm, tt=tt: e.tensor_tensor(
                    ab(bINT + 1024 * fm + 512 * tt, 512), ab(bG, 512), S1p[:, 512:1024], ALU.mult),
                    waits=[um[-1], sil])
                last_emi = mi1
                emi.append(mi1)
        cp[("esil_last",)] = last_esil
        cp[("emi_bank_last",)] = last_emi
        cp[("emi_last", ei)] = emi[-1]
        last_rc2 = cp.get(("edn_rc_last",), [None, None])
        edn = []
        for dm in range(8):
            for tt in range(2):
                bank = tt
                psl = (S2p[:, 0:512], S2p[:, 512:1024])[bank]
                mm = []
                for c in range(4):
                    w = []
                    if c == 0:
                        w = [ld_] + emi[tt::2]
                        if last_rc2[bank] is not None:
                            w.append(last_rc2[bank])
                    mm.append(p.add("tensor", lambda e, dm=dm, tt=tt, c=c, psl=psl, wbuf=wbuf: e.matmul(
                        psl, ab(bWG[wbuf] + 8192 + 1024 * c + 128 * dm, 128), ab(bINT + 1024 * c + 512 * tt, 512),
                        start=(c == 0), stop=(c == 3)), waits=w))
                w2 = [mm[-1]]
                if ("odtr_last", ei - 1) in cp and dm == 0 and tt == 0:
                    w2.append(cp[("odtr_last", ei - 1)])
                rc = p.add("vector", lambda e, dm=dm, tt=tt, psl=psl: e.tensor_copy(
                    ab(bDWN + 1024 * dm + 512 * tt, 512), psl), waits=w2)
                last_rc2[bank] = rc
                edn.append(rc)
        cp[("edn_rc_last",)] = last_rc2
        cp[("edn_mm_last", ei)] = mm[-1]
        oglast = None
        for t in range(8):
            dstp = (trp0, trp1)[t % 2]
            trs = []
            for dm in range(8):
                w = [edn[2 * dm + (0 if (128 * t) < 512 else 1)]]
                w = [edn[2 * dm + (t // 4)]]
                if dm == 0 and ("odtrev", ei, t - 2) in cp:
                    w.append(cp[("odtrev", ei, t - 2)])
                elif dm == 0 and t < 2:
                    w.append(cp[("xtrev", ei, t + 6)])
                trs.append(p.add("tensor", lambda e, t=t, dm=dm, dstp=dstp: e.transpose(
                    dstp[:, 128 * dm:128 * dm + 128], ab(bDWN + 1024 * dm + 128 * t, 128), ab(bIDENT, 128)), waits=w))
            slot = t % 2
            wev = [trs[-1], tidc[t]]
            if ("oscat", ei, t - 2) in cp:
                wev.append(cp[("oscat", ei, t - 2)])
            elif t < 2 and ("oscat", ei - 1, t + 6) in cp:
                wev.append(cp[("oscat", ei - 1, t + 6)])
            oev = p.add("vector", lambda e, t=t, slot=slot, dstp=dstp, mto=mto: e.tensor_scalar(
                af(fSC + 1024 * slot, 1024), dstp[:, :], af(mto + 2 * t, 1), None, ALU.mult), waits=wev)
            cp[("odtrev", ei, t)] = oev
            wosc = [oev] + (zdmas if ei == 0 and t == 0 else [])
            if SER_SC and ("ser_sc",) in cp:
                wosc.append(cp[("ser_sc",)])
            osc = p.add("gpsimd", lambda e, t=t, slot=slot, tbs=tbs: e.indirect_dma_start(
                E["RACC"], bass.IndirectOffsetOnAxis(AI[:, tbs + t:tbs + t + 1], 0),
                af(fSC + 1024 * slot, 1024), None, compute_op=ALU.add),
                waits=wosc, dma="wsc")
            if SER_SC:
                cp[("ser_sc",)] = osc
            cp[("oscat", ei, t)] = osc
            sc_dmas.append(osc)
            oglast = osc
        cp[("odtr_last", ei)] = cp[("odtrev", ei, 7)]
        cp[("escat_last", ei)] = oglast

    all_scats = [cp[("escat_last", e_)] for e_ in range(NE * 2)] + [cp[("oscat", NE * 2 - 1, t)] for t in range(8)]
    x1f_r2 = x1f_r
    shf_r = E["SHFM"].rearrange("(c p) t -> p c t", p=128)
    for g in range(32):
        lx1 = p.add("sync", lambda e, g=g: e.dma_start(
            out=AFr[:, fX1Q:fX1Q + 1024].rearrange("p (c t) -> p c t", c=8),
            in_=x1f_r2[:, :, 128 * g:128 * g + 128]), dma="win",
            waits=[cp[("ydma", g - 2)]] if ("ydma", g - 2) in cp else [])
        lsh = p.add("sync", lambda e, g=g: e.dma_start(
            out=AB[:, bSHF:bSHF + 1024].rearrange("p (c t) -> p c t", c=8),
            in_=shf_r[:, :, 128 * g:128 * g + 128]), dma="win",
            waits=sh_dmas if g == 0 else [])
        lrc = p.add("sync", lambda e, g=g: e.dma_start(
            out=af(fRAC, 1024),
            in_=E["RACC"][128 * g:128 * (g + 1), :]), dma="win",
            waits=all_scats if g == 0 else [])
        cmb = p.add("vector", lambda e: e.tensor_tensor(af(fCMB, 1024), af(fX1Q, 1024), ab(bSHF, 1024), ALU.add),
                    waits=[lx1, lsh, cp.get(("yts_last",))], tag=f"cmb{g}")
        radd = []
        for half in range(2):
            trs = []
            for c4 in range(4):
                c = 4 * half + c4
                w = [cmb]
                if half == 0 and c4 == 0 and ("yev", g - 1) in cp:
                    w.append(cp[("yev", g - 1)])
                if c4 == 0 and half == 1:
                    w.append(radd[0])
                trs.append(p.add("tensor", lambda e, c=c, c4=c4: e.transpose(
                    ovp[:, 128 * c4:128 * c4 + 128], af(fCMB + 128 * c, 128), af(fIDENT, 128)), waits=w))
            ra = p.add("vector", lambda e, half=half: e.tensor_tensor(
                af(fR2 + 512 * half, 512), ovp[:, 0:512], af(fRAC + 512 * half, 512), ALU.add),
                waits=[trs[-1], lrc], tag=f"ra{g}.{half}")
            radd.append(ra)
        cp[("yev", g)] = radd[-1]
        s1 = p.add("scalar", lambda e: e.activation(af(fLNS, 1024), af(fR2, 1024), AF.Identity, accum_out=af(fT2, 1)),
                   waits=[radd[-1]] + ([cp[("ydma", g - 1)]] if ("ydma", g - 1) in cp else []))
        s2 = p.add("scalar", lambda e: e.activation(af(fLNS, 1024), af(fR2, 1024), AF.Square, accum_out=af(fT2 + 1, 1)))
        mb2 = p.add("vector", lambda e: e.tensor_scalar(af(fT2 + 2, 1), af(fT2, 1), 1.0 / D, None, ALU.mult), waits=[s1, s2])
        tb2 = p.add("vector", lambda e: e.tensor_scalar(af(fT2 + 3, 1), af(fT2 + 1, 1), 1.0 / D, EPS, ALU.mult, ALU.add))
        m2b = p.add("vector", lambda e: e.tensor_tensor(af(fT2 + 4, 1), af(fT2 + 2, 1), af(fT2 + 2, 1), ALU.mult))
        vrb = p.add("vector", lambda e: e.tensor_tensor(af(fT2 + 4, 1), af(fT2 + 3, 1), af(fT2 + 4, 1), ALU.subtract))
        lnb2 = p.add("scalar", lambda e: e.activation(af(fT2 + 5, 1), af(fT2 + 4, 1), AF.Ln), waits=[vrb])
        rsb2 = p.add("scalar", lambda e: e.activation(af(fT2 + 6, 1), af(fT2 + 5, 1), AF.Exp, scale=-0.5))
        yts = p.add("vector", lambda e: e.tensor_scalar(af(fLNS, 1024), af(fR2, 1024), af(fT2 + 2, 1), af(fT2 + 6, 1), ALU.subtract, ALU.mult), waits=[rsb2], tag=f"yts{g}")
        cp[("yts_last",)] = yts
        yd = p.add("sync", lambda e, g=g: e.dma_start(
            out=E["YT"][128 * g:128 * (g + 1), :], in_=af(fLNS, 1024)), waits=[yts], dma="out")
        cp[("ydma", g)] = yd
        y_dmas.append(yd)

    p.add("sync", lambda e: e.nop(), waits=y_dmas)


def swz(m):
    m = np.asarray(m, dtype=np.float32)
    c = m.shape[0] // 128
    return np.ascontiguousarray(m.reshape(c, 128, m.shape[1]).transpose(1, 0, 2).reshape(128, -1))


def make_inputs_for_core(inputs, b):
    f32 = lambda x: np.ascontiguousarray(np.asarray(x, dtype=np.float32))
    d = {}
    d["XT"] = f32(np.asarray(inputs["L0_attn_X"][b]).T)
    d["WqT"] = swz(np.asarray(inputs["L0_attn_Wq"]).T)
    d["WkT"] = swz(np.asarray(inputs["L0_attn_Wk"]).T)
    d["WvT"] = swz(np.asarray(inputs["L0_attn_Wv"]).T)
    d["WoT"] = swz(np.asarray(inputs["L0_attn_Wo"]).T)
    d["lq1"] = f32(inputs["L0_attn_lambda_q1"])
    d["lk1"] = f32(inputs["L0_attn_lambda_k1"])
    d["lq2"] = f32(inputs["L0_attn_lambda_q2"])
    d["lk2"] = f32(inputs["L0_attn_lambda_k2"])
    d["rms_scale"] = f32(np.asarray(inputs["L0_attn_rms_scale"])[None, :])
    d["rw"] = swz(np.asarray(inputs["L0_ffn_router_w"]))
    d["gw"] = np.stack([swz(np.asarray(inputs["L0_ffn_gate_w"])[e]) for e in range(NE)])
    d["uw"] = np.stack([swz(np.asarray(inputs["L0_ffn_up_w"])[e]) for e in range(NE)])
    d["dw"] = np.stack([swz(np.asarray(inputs["L0_ffn_down_w"])[e]) for e in range(NE)])
    d["sgw"] = swz(np.asarray(inputs["L0_ffn_shared_gate_w"]))
    d["suw"] = swz(np.asarray(inputs["L0_ffn_shared_up_w"]))
    d["sdw"] = swz(np.asarray(inputs["L0_ffn_shared_down_w"]))
    d["identf"] = np.eye(128, dtype=np.float32)
    d["onesf"] = np.ones((1, 128), dtype=np.float32)
    d["iotap"] = np.arange(128, dtype=np.float32).reshape(128, 1)
    d["eids"] = np.arange(NE, dtype=np.float32).reshape(1, NE)
    d["ltri"] = np.triu(np.ones((128, 128), dtype=np.float32), 1)
    return d


def kernel(**inputs):
    nc = build_program()
    in_maps = [make_inputs_for_core(inputs, b) for b in range(8)]
    res = run_bass_kernel_spmd(nc, in_maps, core_ids=list(range(8)))
    out = np.stack([np.asarray(r["YT"]) for r in res.results], axis=0)
    return np.ascontiguousarray(out).astype(np.float32)


# version 14
# speedup vs baseline: 1.2331x; 1.2331x over previous
import math
import numpy as np
import ml_dtypes
import concourse.bass as bass
import concourse.mybir as mybir
from concourse.bass_utils import run_bass_kernel_spmd

F32 = mybir.dt.float32
BF16 = mybir.dt.bfloat16
AF = mybir.ActivationFunctionType
ALU = mybir.AluOpType
AX = mybir.AxisListType

T = 4096
D = 1024
H = 8
DV = 256
NE = 16
FE = 512
FS = 1024
SCALE = 1.0 / math.sqrt(128.0)
LAMBDA_INIT = 0.8 - 0.6 * math.exp(-0.3)
EPS = 1e-5

ENGINES = ("tensor", "scalar", "vector", "gpsimd", "sync")
SER_SC = False


DMA_POOL = 8


class Op:
    __slots__ = ("eng", "fn", "waits", "dma", "marked", "sem", "val", "pre", "tag")

    def __init__(self, eng, fn, waits, dma, tag):
        self.eng, self.fn, self.dma, self.tag = eng, fn, dma, tag
        self.waits = [w for w in waits if w is not None]
        self.marked = False
        self.sem = None
        self.val = None
        self.pre = None


class Prog:
    def __init__(self):
        self.ops = []

    def add(self, eng, fn, waits=(), dma=None, tag=""):
        op = Op(eng, fn, waits, dma, tag)
        self.ops.append(op)
        return op

    def resolve(self, nc, ctx):
        groups = sorted({op.dma for op in self.ops if op.dma is not None})
        sems = {}
        for e in ENGINES:
            sems["e_" + e] = ctx.enter_context(nc.semaphore("e_" + e))
        pool = {}
        for g in groups:
            pool[g] = [ctx.enter_context(nc.semaphore(f"d_{g}{i}")) for i in range(DMA_POOL)]
        CHAIN = ("vector", "scalar", "gpsimd")
        for op in self.ops:
            kept = []
            for w in op.waits:
                if w.dma is None and w.eng == op.eng and op.dma is None:
                    continue
                w.marked = True
                kept.append(w)
            op.waits = kept
        for op in self.ops:
            if op.eng in CHAIN and op.dma is None:
                op.marked = True
        counts = {("e_" + e): 0 for e in ENGINES}
        rr = {g: 0 for g in groups}
        dma_counts = {}
        for op in self.ops:
            if op.dma is not None:
                op.marked = True
                i = rr[op.dma]
                rr[op.dma] = (i + 1) % DMA_POOL
                sem = pool[op.dma][i]
                prevv = dma_counts.get(id(sem), 0)
                if prevv:
                    op.pre = (sem, prevv)
                op.sem = sem
                op.val = prevv + 16
                dma_counts[id(sem)] = op.val
            elif op.marked:
                cnt = counts["e_" + op.eng]
                if op.eng in CHAIN and cnt > 0:
                    op.pre = (sems["e_" + op.eng], cnt)
                counts["e_" + op.eng] = cnt + 1
                op.sem = sems["e_" + op.eng]
                op.val = cnt + 1
            elif op.eng in CHAIN:
                pass
        self.sems = sems
        self.drain_waits = []
        seen = set()
        for op in self.ops:
            if op.dma is not None and id(op.sem) not in seen:
                seen.add(id(op.sem))
        finals = {}
        for op in self.ops:
            if op.dma is not None:
                finals[id(op.sem)] = (op.sem, op.val)
        self.drain_waits = list(finals.values())

    def emit(self, eng_name, eng):
        watermark = {}
        for op in self.ops:
            if op.eng != eng_name:
                continue
            pend = list(op.waits)
            for w in pend:
                key = id(w.sem)
                if watermark.get(key, -1) >= w.val:
                    continue
                eng.wait_ge(w.sem, w.val)
                watermark[key] = w.val
            if op.pre is not None:
                sem, v = op.pre
                if watermark.get(id(sem), -1) < v:
                    eng.wait_ge(sem, v)
                    watermark[id(sem)] = v
            inst = op.fn(eng)
            if op.marked:
                inst.then_inc(op.sem, 16 if op.dma is not None else 1)
        if eng_name == "sync":
            for sem, v in self.drain_waits:
                if watermark.get(id(sem), -1) < v:
                    eng.wait_ge(sem, v)


def build_program(dbg=()):
    nc = bass.Bass("TRN2", target_bir_lowering=False)
    dt = nc.dram_tensor
    E = {}

    for n, s in (("XT", [D, T]), ("WqT", [128, 16384]), ("WkT", [128, 16384]),
                 ("WvT", [128, 16384]), ("WoT", [128, 16384]),
                 ("lq1", [H, 128]), ("lk1", [H, 128]), ("lq2", [H, 128]), ("lk2", [H, 128]),
                 ("rms_scale", [1, DV]), ("rw", [128, 128]),
                 ("gw", [NE, 128, 4096]), ("uw", [NE, 128, 4096]), ("dw", [NE, 128, 4096]),
                 ("sgw", [128, 8192]), ("suw", [128, 8192]), ("sdw", [128, 8192]),
                 ("identf", [128, 128]), ("onesf", [1, 128])):
        E[n] = dt(n, s, F32, kind="ExternalInput").ap()
    E["iotap"] = dt("iotap", [128, 1], F32, kind="ExternalInput").ap()
    E["eids"] = dt("eids", [1, NE], F32, kind="ExternalInput").ap()
    E["ltri"] = dt("ltri", [128, 128], F32, kind="ExternalInput").ap()
    E["YT"] = dt("YT", [T, D], F32, kind="ExternalOutput").ap()

    def scratch(name, shape, dtype):
        kind = "ExternalOutput" if name in dbg else "Internal"
        return dt(name, shape, dtype, kind=kind).ap()

    if "stop_cmp0" in dbg:
        E["DBG_CMP"] = dt("DBG_CMP", [128, 256], F32, kind="ExternalOutput").ap()
        E["DBG_AI"] = dt("DBG_AI", [128, 32], mybir.dt.int32, kind="ExternalOutput").ap()
    if "probe_u0" in dbg:
        E["DBG_P"] = dt("DBG_P", [128, 4096], BF16, kind="ExternalOutput").ap()
        E["DBG_E2"] = dt("DBG_E2", [128, 4096], BF16, kind="ExternalOutput").ap()
        E["DBG_DEN"] = dt("DBG_DEN", [128, 32], F32, kind="ExternalOutput").ap()
        E["DBG_PT"] = dt("DBG_PT", [128, 4096], BF16, kind="ExternalOutput").ap()
        E["DBG_ON"] = dt("DBG_ON", [128, 256], F32, kind="ExternalOutput").ap()
        E["DBG_OTM"] = dt("DBG_OTM", [128, 256], BF16, kind="ExternalOutput").ap()
        E["DBG_Q"] = dt("DBG_Q", [128, 8192], BF16, kind="ExternalOutput").ap()
        E["DBG_K"] = dt("DBG_K", [128, 8192], BF16, kind="ExternalOutput").ap()
        E["DBG_V"] = dt("DBG_V", [128, 8192], BF16, kind="ExternalOutput").ap()
    E["QTh"] = scratch("QTh", [2048, T], BF16)
    E["KTh"] = scratch("KTh", [2048, T], BF16)
    E["VH"] = scratch("VH", [T, 2048], BF16)
    E["OT"] = scratch("OT", [2048, T], BF16)
    E["X1F"] = scratch("X1F", [D, T], F32)
    E["X1B"] = scratch("X1B", [D, T], BF16)
    E["X1TM"] = scratch("X1TM", [T, D], BF16)
    E["XCOMP"] = scratch("XCOMP", [NE * 2048, D], BF16)
    E["XOUT"] = scratch("XOUT", [NE * 2048, D], BF16)
    E["SLCV"] = scratch("SLCV", [T, 4], F32)
    E["SHFM"] = scratch("SHFM", [D, T], BF16)

    from contextlib import ExitStack
    with ExitStack() as ctx:
        E["AB"] = ctx.enter_context(nc.sbuf_tensor("AB", [128, 73728], BF16))
        E["AFr"] = ctx.enter_context(nc.sbuf_tensor("AFr", [128, 15488], F32))
        E["AI"] = ctx.enter_context(nc.sbuf_tensor("AI", [128, 32], mybir.dt.int32))
        E["S1p"] = ctx.enter_context(nc.psum_tensor("S1p", [128, 1024], F32))
        E["S2p"] = ctx.enter_context(nc.psum_tensor("S2p", [128, 1024], F32))
        E["trp0"] = ctx.enter_context(nc.psum_tensor("trp0", [128, 1024], BF16))
        E["trp1"] = ctx.enter_context(nc.psum_tensor("trp1", [128, 1024], BF16))
        E["ovp"] = ctx.enter_context(nc.psum_tensor("ovp", [128, 512], F32))
        E["pob"] = ctx.enter_context(nc.psum_tensor("pob", [128, 1024], BF16))

        p = Prog()
        emit_all(p, E, dbg)
        p.resolve(nc, ctx)

        with nc.Block() as block:
            @block.tensor
            def _(e):
                p.emit("tensor", e)

            @block.scalar
            def _(e):
                p.emit("scalar", e)

            @block.vector
            def _(e):
                p.emit("vector", e)

            @block.gpsimd
            def _(e):
                p.emit("gpsimd", e)

            @block.sync
            def _(e):
                p.emit("sync", e)
    return nc


def emit_all(p, E, dbg):
    AB, AFr, AI = E["AB"], E["AFr"], E["AI"]
    S1p, S2p, trp0, trp1, ovp, pob = E["S1p"], E["S2p"], E["trp0"], E["trp1"], E["ovp"], E["pob"]

    def ab(off, n, p0=0, np_=128):
        return AB[p0:p0 + np_, off:off + n]

    def af(off, n, p0=0, np_=128):
        return AFr[p0:p0 + np_, off:off + n]

    fIDENT, fONES, fRMSB, fLAM = 0, 128, 256, 512
    fBIG = 640
    bIDENT, bONES, bSEL = 0, 128, 256
    bBIG = bSEL + 2048

    fIOTA, fONC, fEIDS, fEIDB = 520, 521, 524, 544
    fLTRI = 15232
    ld_id = p.add("sync", lambda e: e.dma_start(out=af(fIDENT, 128), in_=E["identf"]), dma="win")
    ld_on = p.add("sync", lambda e: e.dma_start(out=af(fONES, 128, 0, 1), in_=E["onesf"]), dma="win")
    ld_io = p.add("sync", lambda e: e.dma_start(out=af(fIOTA, 1), in_=E["iotap"]), dma="win")
    ld_ei = p.add("sync", lambda e: e.dma_start(out=af(fEIDS, NE, 0, 1), in_=E["eids"]), dma="win")
    ld_idb = p.add("gpsimd", lambda e: e.dma_start(out=ab(bIDENT, 128), in_=E["identf"]), dma="wing")
    ms_ones = p.add("gpsimd", lambda e: e.memset(ab(bONES, 128), 1.0))
    ms_onc = p.add("vector", lambda e: e.memset(af(fONC, 1), 1.0))
    mm_ei = p.add("tensor", lambda e: e.matmul(ovp[:, 272:288], af(fONES, 128, 0, 1), af(fEIDS, NE, 0, 1), start=True, stop=True), waits=[ld_on, ld_ei])
    ev_ei = p.add("vector", lambda e: e.tensor_copy(af(fEIDB, NE), ovp[:, 272:288]), waits=[mm_ei])

    fSM = fBIG
    ld_rs = p.add("sync", lambda e: e.dma_start(out=af(fSM, DV, 0, 1), in_=E["rms_scale"]), dma="win")
    mm_rs = p.add("tensor", lambda e: e.matmul(ovp[:, 0:256], af(fONES, 128, 0, 1), af(fSM, DV, 0, 1), start=True, stop=True), waits=[ld_on, ld_rs])
    ev_rs = p.add("vector", lambda e: e.tensor_scalar(af(fRMSB, DV), ovp[:, 0:256], 1.0 - LAMBDA_INIT, None, ALU.mult), waits=[mm_rs])

    fL = fBIG + 256
    fLs = fL + 512
    ldl = []
    for i, t in enumerate(("lq1", "lk1", "lq2", "lk2")):
        ldl.append(p.add("sync", lambda e, t=t, i=i: e.dma_start(out=af(fL + 128 * i, 128, 0, 8), in_=E[t]), dma="win"))
    p.add("vector", lambda e: e.tensor_tensor(af(fL, 128, 0, 8), af(fL, 128, 0, 8), af(fL + 128, 128, 0, 8), ALU.mult), waits=ldl[:2])
    p.add("vector", lambda e: e.tensor_tensor(af(fL + 256, 128, 0, 8), af(fL + 256, 128, 0, 8), af(fL + 384, 128, 0, 8), ALU.mult), waits=ldl[2:])
    rs1 = p.add("vector", lambda e: e.tensor_reduce(af(fLs, 1, 0, 8), af(fL, 128, 0, 8), AX.X, ALU.add))
    rs2 = p.add("vector", lambda e: e.tensor_reduce(af(fLs + 1, 1, 0, 8), af(fL + 256, 128, 0, 8), AX.X, ALU.add))
    ex12 = p.add("scalar", lambda e: e.activation(af(fLs + 2, 2, 0, 8), af(fLs, 2, 0, 8), AF.Exp), waits=[rs1, rs2])
    p.add("vector", lambda e: e.tensor_tensor(af(fLs + 4, 1, 0, 8), af(fLs + 3, 1, 0, 8), af(fLs + 2, 1, 0, 8), ALU.subtract), waits=[ex12])
    nl2 = p.add("vector", lambda e: e.tensor_scalar(af(fLs + 5, 1, 0, 8), af(fLs + 4, 1, 0, 8), -LAMBDA_INIT, None, ALU.add))
    trl = p.add("tensor", lambda e: e.transpose(ovp[0:1, 256:264], af(fLs + 5, 1, 0, 8), af(fIDENT, 8, 0, 8)), waits=[nl2, ev_rs, ld_id])
    evl = p.add("vector", lambda e: e.tensor_copy(af(fLs + 6, 8, 0, 1), ovp[0:1, 256:264]), waits=[trl])
    mml = p.add("tensor", lambda e: e.matmul(ovp[:, 264:272], af(fONES, 128, 0, 1), af(fLs + 6, 8, 0, 1), start=True, stop=True), waits=[evl])
    nlam = p.add("vector", lambda e: e.tensor_copy(af(fLAM, 8), ovp[:, 264:272]), waits=[mml])

    bXTB = bBIG
    bW = bXTB + 32768
    bW2 = bW + 16384
    bSTG = bW2 + 16384
    assert bSTG + 4096 <= 73728

    xchunks = [p.add("gpsimd",
                     lambda e, c=c: e.dma_start(out=ab(bXTB + 4096 * c, 4096), in_=E["XT"][c * 128:(c + 1) * 128, :]),
                     dma="wing") for c in range(8)]
    ldwq = p.add("gpsimd", lambda e: e.dma_start(out=ab(bW, 16384), in_=E["WqT"]), dma="wing")
    ldwk = p.add("gpsimd", lambda e: e.dma_start(out=ab(bW2, 16384), in_=E["WkT"]), dma="wing")

    vh_r = E["VH"].rearrange("(t p) o -> p t o", p=128)

    def proj_fm(dst, woff, wait_w, tag, stg_prev=None, ev_seed=None):
        dmas = []
        prev_stg = stg_prev
        prev_ev = dict(ev_seed) if ev_seed else {}
        for m in range(16):
            evs = []
            for n in range(8):
                bank = (m * 8 + n) % 2
                psl = (S1p[:, 0:512], S1p[:, 512:1024])[bank]
                mms = []
                for c in range(8):
                    w = []
                    if c == 0:
                        w = [wait_w] + xchunks
                        if prev_ev.get(bank) is not None:
                            w.append(prev_ev[bank])
                    mms.append(p.add(
                        "tensor",
                        lambda e, m=m, n=n, c=c, psl=psl, woff=woff: e.matmul(
                            psl, ab(woff + 2048 * c + 128 * m, 128),
                            ab(bXTB + 4096 * c + 512 * n, 512),
                            start=(c == 0), stop=(c == 7)),
                        waits=w, tag=f"{tag}mm"))
                w2 = [mms[-1]]
                if n == 0:
                    if dmas:
                        w2.append(dmas[-1])
                    elif prev_stg is not None:
                        w2.append(prev_stg)
                ev = p.add("scalar", lambda e, n=n, psl=psl: e.copy(ab(bSTG + 512 * n, 512), psl), waits=w2)
                prev_ev[bank] = ev
                evs.append(ev)
            dmas.append(p.add(
                "sync",
                lambda e, m=m, dst=dst: e.dma_start(out=dst[m * 128:(m + 1) * 128, :], in_=ab(bSTG, 4096)),
                waits=evs, dma="out", tag=f"{tag}dma"))
        return dmas, prev_ev

    qdmas, qev = proj_fm(E["QTh"], bW, ldwq, "q")
    ldwv = p.add("gpsimd", lambda e: e.dma_start(out=ab(bW, 16384), in_=E["WvT"]), waits=[qdmas[-1]], dma="wing")
    kdmas, _ = proj_fm(E["KTh"], bW2, ldwk, "k", stg_prev=qdmas[-1], ev_seed=qev)

    vdmas = []
    prev_ev = {}
    for m in range(32):
        evs = []
        for gi in range(4):
            bank = (m * 4 + gi) % 2
            psl = (S2p[:, 0:512], S2p[:, 512:1024])[bank]
            mms = []
            for c in range(8):
                w = []
                if c == 0:
                    w = [ldwv]
                    if prev_ev.get(bank) is not None:
                        w.append(prev_ev[bank])
                mms.append(p.add(
                    "tensor",
                    lambda e, m=m, gi=gi, c=c, psl=psl: e.matmul(
                        psl, ab(bXTB + 4096 * c + 128 * m, 128),
                        ab(bW + 2048 * c + 512 * gi, 512),
                        start=(c == 0), stop=(c == 7)),
                    waits=w, tag="vmm"))
            w2 = [mms[-1]]
            if gi == 0:
                w2.append(vdmas[-1] if vdmas else kdmas[-1])
            ev = p.add("scalar", lambda e, gi=gi, psl=psl: e.copy(ab(bSTG + 512 * gi, 512), psl), waits=w2)
            prev_ev[bank] = ev
            evs.append(ev)
        vdmas.append(p.add(
            "sync",
            lambda e, m=m: e.dma_start(out=vh_r[:, m, :], in_=ab(bSTG, 2048)),
            waits=evs, dma="out"))

    if "stop_a0" in dbg:
        p.add("sync", lambda e: e.nop(), waits=qdmas + kdmas + vdmas)
        return

    bQ = bBIG
    bK = [bBIG + 8192, bBIG + 16384]
    bV = [bBIG + 24576, bBIG + 32768]
    bE1 = [bBIG + 40960, bBIG + 45056, bBIG + 49152]
    bE2 = [bBIG + 53248, bBIG + 57344]
    bPT = bBIG + 61440
    bOST = bPT + 4096
    bOTM = bOST + 1024
    assert bOTM + 512 <= 73728

    fD1 = [fBIG + 1024, fBIG + 1040]
    fD2 = [fBIG + 1028, fBIG + 1044]
    fDEN = [fBIG + 1056, fBIG + 1056 + 16]
    fON = [fBIG + 1100, fBIG + 1100 + 256]
    fSQ = fBIG + 1100 + 512

    qt_r = E["QTh"].rearrange("(c p) t -> p c t", p=128)
    kt_r = E["KTh"].rearrange("(c p) t -> p c t", p=128)

    def load_head(h, buf, waits):
        lq = p.add("sync", lambda e, h=h: e.dma_start(
            out=AB[:, bQ:bQ + 8192].rearrange("p (c t) -> p c t", c=2),
            in_=qt_r[:, 2 * h:2 * h + 2, :]), dma="win",
            waits=waits + (qdmas if h == 0 else []))
        lk = p.add("sync", lambda e, h=h, buf=buf: e.dma_start(
            out=AB[:, bK[buf]:bK[buf] + 8192].rearrange("p (c t) -> p c t", c=2),
            in_=kt_r[:, 2 * h:2 * h + 2, :]), dma="win",
            waits=(kdmas if h == 0 else []))
        lv = p.add("sync", lambda e, h=h, buf=buf: e.dma_start(
            out=AB[:, bV[buf]:bV[buf] + 8192].rearrange("p (t o) -> p t o", t=32),
            in_=vh_r[:, :, 256 * h:256 * h + 256]), dma="win",
            waits=(vdmas if h == 0 else []))
        return [lq, lk, lv]

    NU = H * 32
    st = {
        "loads": {0: load_head(0, 0, [])},
        "exp_last": [None, None],
        "comb": {}, "exps": {}, "dens": {}, "tr_last": {}, "pv": {},
        "epi": {}, "sq": {}, "otm": {}, "otr": {}, "ose": {}, "otdma": {},
        "head_last_exp": {},
        "trev_last": [None, None],
    }
    ot_dmas = []

    def emit_head_prefetch(u):
        h, qt = u // 32, u % 32
        if qt == 0 and h >= 1:
            lq = p.add("sync", lambda e, h=h: e.dma_start(
                out=AB[:, bQ:bQ + 8192].rearrange("p (c t) -> p c t", c=2),
                in_=qt_r[:, 2 * h:2 * h + 2, :]), dma="win",
                waits=[st["head_last_exp"][h - 1]])
            st["loads"][h].append(lq)
        if qt == 2 and h + 1 < H:
            w = []
            if h >= 1:
                w.append(st["pv"][(h - 1) * 32 + 31])
            lk = p.add("sync", lambda e, h1=h + 1, buf=1 - (h % 2): e.dma_start(
                out=AB[:, bK[buf]:bK[buf] + 8192].rearrange("p (c t) -> p c t", c=2),
                in_=kt_r[:, 2 * h1:2 * h1 + 2, :]), dma="win",
                waits=w)
            lv = p.add("sync", lambda e, h1=h + 1, buf=1 - (h % 2): e.dma_start(
                out=AB[:, bV[buf]:bV[buf] + 8192].rearrange("p (t o) -> p t o", t=32),
                in_=vh_r[:, :, 256 * h1:256 * h1 + 256]), dma="win")
            st["loads"][h + 1] = [lk, lv]


    def emit_S_span(u, j):
        h, qt = u // 32, u % 32
        buf = h % 2
        sl = u % 2
        loads = st["loads"][h]
        oQ, oK = bQ, bK[buf]
        exps = st["exps"].setdefault(u, {1: [], 2: []})
        for br, Sp, qoff, koff, bEo, fDx in (
                (1, S1p, oQ, oK, bE1[u % 3], fD1[sl]),
                (2, S2p, oQ + 4096, oK + 4096, bE2[sl], fD2[sl])):
            mms = []
            for half in range(2):
                kb = 2 * j + half
                w = []
                if kb == 0:
                    w = list(loads)
                    if st["exp_last"][br - 1] is not None:
                        w.append(st["exp_last"][br - 1])
                elif half == 0:
                    w.append(exps[br][j - 1])
                mms.append(p.add(
                    "tensor",
                    lambda e, qt=qt, kb=kb, Sp=Sp, qoff=qoff, koff=koff, half=half: e.matmul(
                        Sp[:, 512 * half:512 * half + 512],
                        ab(qoff + 128 * qt, 128), ab(koff + 512 * kb, 512),
                        start=True, stop=True),
                    waits=w, tag=f"s{br}u{u}"))
            ew = list(mms)
            if j == 0:
                if br == 1:
                    if (u - 3) in st["tr_last"]:
                        ew.append(st["tr_last"][u - 3])
                    if (u - 2) in st["dens"]:
                        ew.append(st["dens"][u - 2][0])
                if br == 2 and (u - 2) in st["comb"]:
                    ew.append(st["comb"][u - 2])
            exps[br].append(p.add(
                "scalar",
                lambda e, j=j, Sp=Sp, bEo=bEo, fDx=fDx: e.activation(
                    ab(bEo + 1024 * j, 1024), Sp[:, :], AF.Exp, scale=SCALE,
                    accum_out=af(fDx + j, 1)),
                waits=ew, tag=f"e{br}u{u}"))
            if j == 3:
                st["exp_last"][br - 1] = exps[br][-1]
                if qt == 31 and br == 2:
                    st["head_last_exp"][h] = exps[2][-1]

    def emit_TR(u, ktg_lo, ktg_hi):
        sl = u % 3
        c2 = st["comb"][u]
        for ktg in range(ktg_lo, ktg_hi + 1):
            dst = (trp0, trp1)[ktg % 2]
            trs = []
            for i in range(8):
                kt = 8 * ktg + i
                w = []
                if i == 0:
                    w = [c2, ld_idb]
                    if st["trev_last"][ktg % 2] is not None:
                        w.append(st["trev_last"][ktg % 2])
                trs.append(p.add(
                    "tensor",
                    lambda e, kt=kt, dst=dst, sl=sl: e.transpose(
                        dst[:, 128 * (kt % 8):128 * (kt % 8) + 128],
                        ab(bE1[sl] + 128 * kt, 128), ab(bIDENT, 128)),
                    waits=w, tag=f"tru{u}"))
            w2 = [trs[-1]]
            if (u - 1) in st["pv"]:
                w2.append(st["pv"][u - 1])
            tev = p.add(
                "vector",
                lambda e, ktg=ktg, dst=dst: e.tensor_copy(ab(bPT + 1024 * ktg, 1024), dst[:, :]),
                waits=w2, tag=f"trevu{u}")
            st["trev_last"][ktg % 2] = tev
            st.setdefault("trevs", {}).setdefault(u, []).append(tev)
            if ktg == 3:
                st["tr_last"][u] = trs[-1]

    def emit_PV(u, half):
        h = u // 32
        buf = h % 2
        sl = u % 2
        oV = bV[buf]
        trevs = st["trevs"][u]
        pvs = st.setdefault("pvs", {}).setdefault(u, [])
        for kt in range(16 * half, 16 * half + 16):
            w = []
            if kt % 8 == 0:
                w.append(trevs[kt // 8])
            if kt == 0:
                if (u - 1) in st["epi"]:
                    w.append(st["epi"][u - 1])
                if u == 0:
                    w += [nlam, ev_rs, ev_ei]
            pvs.append(p.add(
                "tensor",
                lambda e, kt=kt, sl=sl, oV=oV: e.matmul(
                    ovp[:, 256 * sl:256 * sl + 256],
                    ab(bPT + 128 * kt, 128), ab(oV + 256 * kt, 256),
                    start=(kt == 0), stop=(kt == 31)),
                waits=w, tag=f"pvu{u}"))
        if half == 1:
            st["pv"][u] = pvs[-1]

    def emit_epilogue(u):
        sl = u % 2
        fDx = fDEN[sl]
        fo = fON[sl]
        ms, r1 = st["dens"][u]
        ew = [st["pv"][u], r1]
        if (u - 2) in st["sq"]:
            ew.append(st["sq"][u - 2])
        epi = p.add("vector", lambda e, sl=sl, fo=fo, fDx=fDx: e.tensor_scalar(
            af(fo, 256), ovp[:, 256 * sl:256 * sl + 256], af(fDx + 4, 1), None, ALU.mult),
            waits=ew, tag=f"epiu{u}")
        st["epi"][u] = epi
        sqm = p.add("vector", lambda e, fo=fo: e.tensor_tensor(
            af(fSQ, 256), af(fo, 256), af(fo, 256), ALU.mult), tag=f"squ{u}")
        st["sq"][u] = sqm
        t2 = p.add("vector", lambda e, fDx=fDx: e.tensor_scalar(
            af(fSQ, 256), af(fSQ, 256), 1.0 / DV, EPS, ALU.mult, ALU.add,
            accum_out=af(fDx + 7, 1)), tag=f"t2u{u}")
        ln_ = p.add("scalar", lambda e, fDx=fDx: e.activation(af(fDx + 8, 1), af(fDx + 7, 1), AF.Ln), waits=[t2])
        rinv = p.add("scalar", lambda e, fDx=fDx: e.activation(af(fDx + 9, 1), af(fDx + 8, 1), AF.Exp, scale=-0.5), tag=f"rinvu{u}")
        p.add("vector", lambda e, fo=fo, fDx=fDx: e.tensor_scalar(af(fo, 256), af(fo, 256), af(fDx + 9, 1), None, ALU.mult), waits=[rinv])
        otm_w = [ev_rs]
        if (u - 2) in st["otr"]:
            otm_w.append(st["otr"][u - 2][-1])
        otm = p.add("vector", lambda e, sl=sl, fo=fo: e.tensor_tensor(
            ab(bOTM + 256 * sl, 256), af(fo, 256), af(fRMSB, 256), ALU.mult), waits=otm_w, tag=f"otmu{u}")
        st["otm"][u] = otm

    def emit_S_dve(u):
        h, qt = u // 32, u % 32
        sl = u % 2
        fDx = fDEN[sl]
        exps = st["exps"][u]
        p.add("vector", lambda e, sl=sl, fDx=fDx: e.tensor_reduce(
            af(fDx, 2), AFr[:, fD1[sl]:fD1[sl] + 8].rearrange("p (b x) -> p b x", b=2),
            AX.X, ALU.add), waits=exps[1] + exps[2])
        rd = p.add("vector", lambda e, fDx=fDx: e.reciprocal(af(fDx + 4, 2), af(fDx, 2)))
        p.add("vector", lambda e, fDx=fDx: e.tensor_tensor(af(fDx + 3, 1), af(fDx, 1), af(fDx + 5, 1), ALU.mult))
        ms = p.add("vector", lambda e, h=h, fDx=fDx: e.tensor_tensor(af(fDx + 6, 1), af(fDx + 3, 1), af(fLAM + h, 1), ALU.mult), waits=[nlam])
        st["dens"][u] = (ms, rd)
        c1 = p.add("vector", lambda e, sl=sl, fDx=fDx: e.tensor_scalar(ab(bE2[sl], 4096), ab(bE2[sl], 4096), af(fDx + 6, 1), None, ALU.mult), tag=f"c1u{u}")
        c2 = p.add("gpsimd", lambda e, sl=sl, s3=u % 3: e.tensor_tensor(ab(bE1[s3], 4096), ab(bE1[s3], 4096), ab(bE2[sl], 4096), ALU.add),
                   waits=[c1, exps[1][-1]], tag=f"c2u{u}")
        st["comb"][u] = c2

    def stage_OTR(u):
        h, qt = u // 32, u % 32
        sl = u % 2
        otrs = []
        for chv in range(2):
            w = [st["otm"][u]]
            if qt % 4 == 0 and chv == 0 and (u - 4) in st["ose"]:
                w += st["ose"][u - 4]
            otrs.append(p.add(
                "tensor",
                lambda e, qt=qt, chv=chv, sl=sl: e.transpose(
                    pob[:, 256 * (qt % 4) + 128 * chv:256 * (qt % 4) + 128 * chv + 128],
                    ab(bOTM + 256 * sl + 128 * chv, 128), ab(bIDENT, 128)),
                waits=w, tag=f"otru{u}"))
        st["otr"][u] = otrs
        if qt % 4 == 3:
            qb = u - 3
            grp = (qt - 3) // 4
            pob_v = pob[:, :].rearrange("p (q x) -> p q x", q=4)
            oses = []
            all_otrs = [o for i in range(4) for o in st["otr"][u - 3 + i]]
            for ch in range(2):
                w = list(all_otrs)
                prevd = st["otdma"].get((h, grp - 1), st["otdma"].get((h - 1, 7)))
                if prevd is not None:
                    w.append(prevd[ch])
                oses.append(p.add(
                    "vector",
                    lambda e, ch=ch: e.tensor_copy(
                        AB[:, bOST + 512 * ch:bOST + 512 * ch + 512].rearrange("p (q x) -> p q x", q=4),
                        pob_v[:, :, 128 * ch:128 * ch + 128]),
                    waits=w, tag=f"oseu{u}"))
            st["ose"][qb] = oses
            pair = []
            for ch in range(2):
                dma = p.add(
                    "sync",
                    lambda e, h=h, grp=grp, ch=ch: e.dma_start(
                        out=E["OT"][h * 256 + ch * 128:h * 256 + ch * 128 + 128, grp * 512:grp * 512 + 512],
                        in_=ab(bOST + 512 * ch, 512)),
                    waits=[oses[ch]], dma="out", tag=f"otdma{h}.{grp}")
                ot_dmas.append(dma)
                pair.append(dma)
            st["otdma"][(h, grp)] = pair

    nheads = H
    for d in dbg:
        if isinstance(d, str) and d.startswith("nh"):
            nheads = int(d[2:])
    NU = nheads * 32

    for u in range(NU + 4):
        cur = u < NU
        prv = 2 <= u <= NU + 1
        if cur:
            emit_head_prefetch(u)
            emit_S_span(u, 0)
        if 4 <= u <= NU + 3:
            stage_OTR(u - 4)
        if prv:
            emit_TR(u - 2, 0, 0)
        if cur:
            emit_S_span(u, 1)
        if prv:
            emit_TR(u - 2, 1, 2)
        if cur:
            emit_S_span(u, 2)
        if prv:
            emit_TR(u - 2, 3, 3)
            emit_PV(u - 2, 0)
        if cur:
            emit_S_span(u, 3)
        if prv:
            emit_PV(u - 2, 1)
            emit_epilogue(u - 2)
        if cur:
            emit_S_dve(u)

    if "stop_a1" in dbg:
        p.add("sync", lambda e: e.nop(), waits=ot_dmas)
        return

    bWO = bBIG
    bOB = [bWO + 16384, bWO + 24576]
    bRB = bWO + 32768
    bRSQ = bRB + 4096
    bX1B = bRSQ + 4096
    bTM = bX1B + 4096
    assert bTM + 2048 <= 73728
    fXT = fBIG + 1024
    fR = fXT + 4096
    fX1 = fR + 4096
    fT = fX1 + 4096
    assert fT + 1536 <= 15488

    ldwo = p.add("gpsimd", lambda e: e.dma_start(out=ab(bWO, 16384), in_=E["WoT"]),
        dma="wing", waits=[st["head_last_exp"][H - 1], st["pv"][NU - 1]])
    ot_r = E["OT"].rearrange("(c p) t -> p c t", p=128)
    xt_r = E["XT"].rearrange("(c p) t -> p c t", p=128)
    x1f_r = E["X1F"].rearrange("(c p) t -> p c t", p=128)
    x1b_r = E["X1B"].rearrange("(c p) t -> p c t", p=128)

    prev = {}
    x1_dmas = []
    for n in range(8):
        bufn = n % 2
        lob_w = list(ot_dmas) if n == 0 else ([prev[("womm", n - 2)]] if ("womm", n - 2) in prev else [])
        lob = p.add("sync", lambda e, n=n, bufn=bufn: e.dma_start(
            out=AB[:, bOB[bufn]:bOB[bufn] + 8192].rearrange("p (c t) -> p c t", c=16),
            in_=ot_r[:, :, 512 * n:512 * n + 512]), dma="win", waits=lob_w)
        lxt = p.add("sync", lambda e, n=n: e.dma_start(
            out=AFr[:, fXT:fXT + 4096].rearrange("p (c t) -> p c t", c=8),
            in_=xt_r[:, :, 512 * n:512 * n + 512]), dma="win",
            waits=[prev[("rev_last", n - 1)]] if ("rev_last", n - 1) in prev else [])
        revs = []
        for m in range(8):
            psl = (S1p[:, 0:512], S1p[:, 512:1024])[m % 2]
            mms = []
            for ec in range(16):
                w = []
                if ec == 0:
                    w = [lob, ldwo]
                    if ("rev", n, m - 2) in prev:
                        w.append(prev[("rev", n, m - 2)])
                    if m < 2 and ("rev", n - 1, 6 + m) in prev:
                        w.append(prev[("rev", n - 1, 6 + m)])
                mms.append(p.add(
                    "tensor",
                    lambda e, m=m, ec=ec, psl=psl, bufn=bufn: e.matmul(
                        psl, ab(bWO + 1024 * ec + 128 * m, 128), ab(bOB[bufn] + 512 * ec, 512),
                        start=(ec == 0), stop=(ec == 15)),
                    waits=w, tag=f"wo{n}.{m}"))
            prev[("womm", n)] = mms[-1]
            rev = p.add("vector", lambda e, m=m, psl=psl: e.tensor_tensor(
                af(fR + 512 * m, 512), psl, af(fXT + 512 * m, 512), ALU.add),
                waits=[mms[-1], lxt], tag=f"rev{n}.{m}")
            prev[("rev", n, m)] = rev
            revs.append(rev)
        rbfs, rsqs = [], []
        for m in range(8):
            rbfs.append(p.add("scalar", lambda e, m=m: e.copy(ab(bRB + 512 * m, 512), af(fR + 512 * m, 512)), waits=[revs[m]]))
            rsqs.append(p.add("scalar", lambda e, m=m: e.activation(ab(bRSQ + 512 * m, 512), af(fR + 512 * m, 512), AF.Square), waits=[revs[m]]))
        summ, sqmm = [], []
        for c in range(8):
            w1 = [rbfs[c], ms_ones]
            w2 = [rsqs[c]]
            if c == 0 and ("stat", n - 1) in prev:
                w1.append(prev[("stat", n - 1)])
                w2.append(prev[("stat", n - 1)])
            summ.append(p.add("tensor", lambda e, c=c: e.matmul(
                S2p[:, 0:512], ab(bONES, 128), ab(bRB + 512 * c, 512), start=(c == 0), stop=(c == 7)), waits=w1))
            sqmm.append(p.add("tensor", lambda e, c=c: e.matmul(
                S2p[:, 512:1024], ab(bONES, 128), ab(bRSQ + 512 * c, 512), start=(c == 0), stop=(c == 7)), waits=w2))
        mb = p.add("vector", lambda e: e.tensor_scalar(af(fT, 512), S2p[:, 0:512], 1.0 / D, None, ALU.mult), waits=[summ[-1]])
        t = p.add("vector", lambda e: e.tensor_scalar(af(fT + 512, 512), S2p[:, 512:1024], 1.0 / D, EPS, ALU.mult, ALU.add), waits=[sqmm[-1]])
        prev[("stat", n)] = t
        p.add("vector", lambda e: e.tensor_tensor(af(fT + 1024, 512), af(fT, 512), af(fT, 512), ALU.mult))
        var = p.add("vector", lambda e: e.tensor_tensor(af(fT + 1024, 512), af(fT + 512, 512), af(fT + 1024, 512), ALU.subtract))
        lnv = p.add("scalar", lambda e: e.activation(af(fT + 512, 512), af(fT + 1024, 512), AF.Ln), waits=[var])
        rstd = p.add("scalar", lambda e: e.activation(af(fT + 1024, 512), af(fT + 512, 512), AF.Exp, scale=-0.5), waits=[lnv])
        x1f_ops, x1b_ops = [], []
        for m in range(8):
            wv = [rstd]
            if ("x1dma", n - 1) in prev and m == 0:
                wv.append(prev[("x1dma", n - 1)])
            p.add("vector", lambda e, m=m: e.tensor_tensor(af(fX1 + 512 * m, 512), af(fR + 512 * m, 512), af(fT, 512), ALU.subtract), waits=wv)
            x1c = p.add("vector", lambda e, m=m: e.tensor_tensor(af(fX1 + 512 * m, 512), af(fX1 + 512 * m, 512), af(fT + 1024, 512), ALU.mult))
            wb = [x1c]
            if m == 0 and ("x1dmab", n - 1) in prev:
                wb.append(prev[("x1dmab", n - 1)])
            x1b = p.add("scalar", lambda e, m=m: e.copy(ab(bX1B + 512 * m, 512), af(fX1 + 512 * m, 512)), waits=wb)
            x1f_ops.append(x1c)
            x1b_ops.append(x1b)
        rl = p.add("vector", lambda e: e.tensor_copy(af(fLAM, 1), af(fLAM, 1)))
        prev[("rev_last", n)] = rl
        d1 = p.add("sync", lambda e, n=n: e.dma_start(
            out=x1f_r[:, :, 512 * n:512 * n + 512],
            in_=AFr[:, fX1:fX1 + 4096].rearrange("p (c t) -> p c t", c=8)),
            waits=x1f_ops, dma="out")
        d2 = p.add("sync", lambda e, n=n: e.dma_start(
            out=x1b_r[:, :, 512 * n:512 * n + 512],
            in_=AB[:, bX1B:bX1B + 4096].rearrange("p (c t) -> p c t", c=8)),
            waits=x1b_ops, dma="out")
        prev[("x1dma", n)] = d1
        prev[("x1dmab", n)] = d2
        x1_dmas += [d1, d2]
        for tp in range(4):
            dstp = (trp0, trp1)[tp % 2]
            trsb = []
            for c in range(8):
                w = [x1b_ops[c]]
                if ("x1tmev", n, tp - 2) in prev:
                    w.append(prev[("x1tmev", n, tp - 2)])
                elif tp < 2 and ("x1tmev", n - 1, tp + 2) in prev:
                    w.append(prev[("x1tmev", n - 1, tp + 2)])
                trsb.append(p.add("tensor", lambda e, c=c, tp=tp, dstp=dstp: e.transpose(
                    dstp[:, 128 * c:128 * c + 128], ab(bX1B + 512 * c + 128 * tp, 128), ab(bIDENT, 128)),
                    waits=w))
            w2 = [trsb[-1]]
            if ("x1tmdma", n, tp - 2) in prev:
                w2.append(prev[("x1tmdma", n, tp - 2)])
            elif tp < 2 and ("x1tmdma", n - 1, tp + 2) in prev:
                w2.append(prev[("x1tmdma", n - 1, tp + 2)])
            tev = p.add("vector", lambda e, tp=tp, dstp=dstp: e.tensor_copy(
                ab(bTM + 1024 * (tp % 2), 1024), dstp[:, :]), waits=w2)
            prev[("x1tmev", n, tp)] = tev
            dtm = p.add("sync", lambda e, n=n, tp=tp: e.dma_start(
                out=E["X1TM"][512 * n + 128 * tp:512 * n + 128 * tp + 128, :],
                in_=ab(bTM + 1024 * (tp % 2), 1024)), waits=[tev], dma="out")
            prev[("x1tmdma", n, tp)] = dtm
            x1_dmas.append(dtm)

    if "stop_b" in dbg:
        p.add("sync", lambda e: e.nop(), waits=x1_dmas)
        return

    C = 1024
    CC = 2048
    bXB = bBIG
    bSH = bXB + 8192
    bINT = bSH + 8192
    bDWN = bINT + 8192
    bWG = [bDWN + 8192, bDWN + 20480, bDWN + 32768]
    bG = bDWN + 45056
    bRW = bG + 512
    bSHF = bRW + 128
    bZ = bXB
    bXO = bINT + 4096
    assert bSHF + 1024 <= 73728
    fLG = fBIG + 1024
    fLGT = fLG + 1024
    fPRB = fLGT + 128
    fPM = fPRB + 112
    fMB = fPM + 16
    fCW = fMB + 16
    fMET = fCW + 64
    fROFF = fMET + 4
    fMETA = fROFF + 16
    fSC = fMETA + 32
    fX1Q = fSC + 2048
    fR2 = fX1Q + 1024
    fCMB = fR2 + 1024
    fLNS = fCMB + 1024
    fRAC = fLNS + 1024
    fT2 = fRAC + 1024
    fGB = fT2 + 64
    fSLC = fGB + 1024
    assert fSLC + 16 <= 15232

    ldrw = p.add("gpsimd", lambda e: e.dma_start(out=ab(bRW, 128), in_=E["rw"]), dma="wing",
        waits=[prev[("womm", 7)]])
    ld_lt = p.add("sync", lambda e: e.dma_start(out=af(fLTRI, 128), in_=E["ltri"]), dma="win",
        waits=[prev[("x1dma", 7)], prev[("x1dmab", 7)]])

    msz = p.add("gpsimd", lambda e: e.memset(ab(bZ, 4096), 0.0), waits=[prev[("womm", 7)]])
    zdmas = []
    for blk in range(64):
        zdmas.append(p.add("sync", lambda e, blk=blk: e.dma_start(
            out=E["XCOMP"][512 * blk:512 * (blk + 1), :].rearrange("(b p) c -> p b c", p=128),
            in_=AB[:, bZ:bZ + 4096].rearrange("p (b c) -> p b c", b=4)), waits=[msz], dma="out"))
    zro = p.add("vector", lambda e: e.memset(af(fROFF, NE, 0, 1), 0.0))

    y_dmas = []
    cp = {}
    sc_dmas = []
    sh_dmas = []
    x1b_r2 = x1b_r
    for Q in range(4):
        tq = 1024 * Q
        lxb = p.add("sync", lambda e, tq=tq: e.dma_start(
            out=AB[:, bXB:bXB + 8192].rearrange("p (c t) -> p c t", c=8),
            in_=x1b_r2[:, :, tq:tq + 1024]), dma="win",
            waits=([prev[("x1dmab", n)] for n in range(8)] + zdmas if Q == 0 else [cp[("shdma", Q - 1)]]))
        lgevs = []
        for tt in range(2):
            mms = []
            for c in range(8):
                w = []
                if c == 0:
                    w = [lxb, ldrw]
                    if tt == 1:
                        w.append(lgevs[0])
                    elif ("ovp_free",) in cp:
                        w.append(cp[("ovp_free",)])
                    elif Q == 0:
                        w.append(st["epi"].get(NU - 1))
                mms.append(p.add("tensor", lambda e, tt=tt, c=c: e.matmul(
                    ovp[0:16, 0:512], ab(bRW + 16 * c, 16, 0, 128), ab(bXB + 1024 * c + 512 * tt, 512),
                    start=(c == 0), stop=(c == 7)), waits=w))
            lgevs.append(p.add("vector", lambda e, tt=tt: e.tensor_copy(af(fLG + 512 * tt, 512, 0, 16), ovp[0:16, 0:512]), waits=[mms[-1]]))
        trls = []
        for j in range(8):
            w = [lgevs[-1]] if j == 0 else []
            trls.append(p.add("tensor", lambda e, j=j: e.transpose(
                ovp[:, 16 * j:16 * j + 16], af(fLG + 128 * j, 128, 0, 16), af(fIDENT, 16, 0, 16)), waits=w))
        lgt = p.add("vector", lambda e: e.tensor_copy(af(fLGT, 128), ovp[:, 0:128]), waits=[trls[-1]])
        for j in range(8):
            g = Q * 8 + j
            p.add("vector", lambda e, j=j: e.reduce_max(af(fPRB, 1), af(fLGT + 16 * j, 16), AX.X))
            nm1 = p.add("vector", lambda e: e.tensor_scalar(af(fPRB + 5, 1), af(fPRB, 1), -1.0, None, ALU.mult))
            ex = p.add("scalar", lambda e, j=j: e.activation(af(fPRB + 16, 16), af(fLGT + 16 * j, 16), AF.Exp, bias=af(fPRB + 5, 1), accum_out=af(fPRB + 1, 1)), waits=[nm1])
            p.add("vector", lambda e: e.reciprocal(af(fPRB + 2, 1), af(fPRB + 1, 1)), waits=[ex])
            p.add("vector", lambda e: e.tensor_scalar(af(fPRB + 32, 16), af(fPRB + 16, 16), af(fPRB + 2, 1), None, ALU.mult))
            p.add("vector", lambda e: e.reduce_max(af(fPRB + 3, 1), af(fPRB + 32, 16), AX.X))
            eqo = p.add("vector", lambda e: e.tensor_scalar(af(fPRB + 48, 16), af(fPRB + 32, 16), af(fPRB + 3, 1), None, ALU.is_equal))
            p.add("vector", lambda e: e.tensor_tensor(af(fPRB + 64, 16), af(fPRB + 32, 16), af(fPRB + 48, 16), ALU.mult))
            p.add("vector", lambda e: e.tensor_tensor(af(fPRB + 64, 16), af(fPRB + 32, 16), af(fPRB + 64, 16), ALU.subtract))
            p.add("vector", lambda e: e.reduce_max(af(fPRB + 4, 1), af(fPRB + 64, 16), AX.X))
            geo = p.add("vector", lambda e: e.tensor_scalar(af(fMB, 16), af(fPRB + 32, 16), af(fPRB + 4, 1), None, ALU.is_ge))
            w0 = [geo, ld_lt, ld_on, ms_onc]
            if ("cmp_mm",) in cp:
                w0.append(cp[("cmp_mm",)])
            mmro = p.add("tensor", lambda e: e.matmul(ovp[:, 0:16], af(fONES, NE * 0 + 128, 0, 1), af(fROFF, NE, 0, 1), start=True, stop=False), waits=w0 + [zro])
            mmpf = p.add("tensor", lambda e: e.matmul(ovp[:, 0:16], af(fLTRI, 128), af(fMB, 16), start=False, stop=True))
            mmtot = p.add("tensor", lambda e: e.matmul(ovp[0:1, 16:32], af(fONC, 1), af(fMB, 16), start=True, stop=True))
            evpm = p.add("vector", lambda e: e.tensor_copy(af(fPM, 16), ovp[:, 0:16]), waits=[mmpf])
            roff = p.add("vector", lambda e: e.tensor_tensor(af(fROFF, NE, 0, 1), af(fROFF, NE, 0, 1), ovp[0:1, 16:32], ALU.add), waits=[mmtot])
            cp[("cmp_mm",)] = roff
            scs = []
            for rk, (mname, moff, cvoff) in enumerate((("A", fPRB + 48, fPRB + 3), ("B", fPRB + 64, fPRB + 4))):
                if rk == 1:
                    p.add("vector", lambda e: e.tensor_tensor(af(fPRB + 64, 16), af(fMB, 16), af(fPRB + 48, 16), ALU.subtract))
                p.add("vector", lambda e, moff=moff: e.tensor_tensor(af(fCW, 16), af(moff, 16), af(fEIDB, 16), ALU.mult), waits=[ev_ei])
                p.add("vector", lambda e: e.tensor_reduce(af(fCW + 32, 1), af(fCW, 16), AX.X, ALU.add))
                p.add("vector", lambda e, moff=moff: e.tensor_tensor(af(fCW + 16, 16), af(moff, 16), af(fPM, 16), ALU.mult), waits=[evpm])
                p.add("vector", lambda e: e.tensor_reduce(af(fCW + 33, 1), af(fCW + 16, 16), AX.X, ALU.add))
                p.add("vector", lambda e: e.tensor_scalar(af(fCW + 33, 1), af(fCW + 33, 1), float(CC - 1), None, ALU.min))
                slf = p.add("vector", lambda e: e.tensor_scalar(af(fCW + 34, 1), af(fCW + 32, 1), float(CC), af(fCW + 33, 1), ALU.mult, ALU.add))
                wsli = [slf]
                if ("scx", g - 1, rk) in cp:
                    wsli.append(cp[("scx", g - 1, rk)])
                sli = p.add("vector", lambda e, rk=rk: e.tensor_copy(AI[:, rk:rk + 1], af(fCW + 34, 1)), waits=wsli)
                wmv = [cp[("slcv", g - 1)]] if rk == 0 and ("slcv", g - 1) in cp else []
                p.add("vector", lambda e, rk=rk: e.tensor_copy(af(fMET + 2 * rk, 1), af(fCW + 34, 1)), waits=wmv)
                mcv = p.add("vector", lambda e, rk=rk, cvoff=cvoff: e.tensor_copy(af(fMET + 2 * rk + 1, 1), af(cvoff, 1)))
                wsc = [sli] + (zdmas if g == 0 else [])
                if ("scx", g - 2, rk) in cp:
                    wsc.append(cp[("scx", g - 2, rk)])
                if rk == 0:
                    wst = [prev[("x1tmdma", g // 4, g % 4)]]
                    if ("scx", g - 2, 1) in cp:
                        wst.append(cp[("scx", g - 2, 1)])
                    cp[("xst", g)] = p.add("sync", lambda e, g=g: e.dma_start(
                        out=ab(bDWN + 1024 * (g % 2), 1024),
                        in_=E["X1TM"][g * 128:(g + 1) * 128, :]), waits=wst, dma="win")
                if SER_SC and ("ser_sc",) in cp:
                    wsc.append(cp[("ser_sc",)])
                scx = p.add("gpsimd", lambda e, rk=rk, g=g: e.indirect_dma_start(
                    E["XCOMP"], bass.IndirectOffsetOnAxis(AI[:, rk:rk + 1], 0),
                    ab(bDWN + 1024 * (g % 2), 1024), None), waits=wsc + [cp[("xst", g)]], dma="wsc")
                if SER_SC:
                    cp[("ser_sc",)] = scx
                cp[("scx", g, rk)] = scx
                scs.append(scx)
            slcv = p.add("sync", lambda e, g=g: e.dma_start(
                out=E["SLCV"][g * 128:(g + 1) * 128, :], in_=AFr[:, fMET:fMET + 4]),
                waits=[mcv], dma="out")
            cp[("slcv", g)] = slcv
            sc_dmas += scs
            if "stop_cmp0" in dbg and g == 0:
                dws = scs + [roff]
                dd = []
                dd.append(p.add("sync", lambda e: e.dma_start(out=E["DBG_CMP"][:, 0:112], in_=af(fPRB, 112)), waits=dws, dma="out"))
                dd.append(p.add("sync", lambda e: e.dma_start(out=E["DBG_CMP"][:, 112:128], in_=af(fPM, 16)), waits=dws, dma="out"))
                dd.append(p.add("sync", lambda e: e.dma_start(out=E["DBG_CMP"][:, 128:144], in_=af(fMB, 16)), waits=dws, dma="out"))
                dd.append(p.add("sync", lambda e: e.dma_start(out=E["DBG_CMP"][:, 144:208], in_=af(fCW, 64)), waits=dws, dma="out"))
                dd.append(p.add("sync", lambda e: e.dma_start(out=E["DBG_CMP"][0:1, 208:224], in_=af(fROFF, NE, 0, 1)), waits=dws, dma="out"))
                dd.append(p.add("sync", lambda e: e.dma_start(out=E["DBG_AI"][:, 0:2], in_=AI[:, 0:2]), waits=dws, dma="out"))
                dd.append(p.add("sync", lambda e: e.dma_start(out=E["DBG_CMP"][:, 224:240], in_=af(fLTRI, 16)), waits=dws, dma="out"))
                dd.append(p.add("sync", lambda e: e.dma_start(out=E["DBG_CMP"][:, 240:241], in_=AFr[:, fLTRI + 127:fLTRI + 128], allow_slow_non_contiguous=True), waits=dws, dma="out"))
                dd.append(p.add("sync", lambda e: e.dma_start(out=E["DBG_CMP"][0:1, 241:242], in_=af(fONC, 1, 0, 1)), waits=dws, dma="out"))
                dd.append(p.add("sync", lambda e: e.dma_start(out=E["DBG_CMP"][0:1, 242:250], in_=af(fONES, 8, 0, 1)), waits=dws, dma="out"))
                p.add("sync", lambda e: e.nop(), waits=dd)
                return
        lsg = p.add("gpsimd", lambda e: e.dma_start(out=ab(bWG[0], 8192), in_=E["sgw"]), dma="wing",
            waits=([ldrw, prev[("x1dmab", 7)], prev[("x1dma", 7)]] if Q == 0 else [cp[("shdn_last",)]]))
        lsu = p.add("gpsimd", lambda e: e.dma_start(out=ab(bWG[1], 8192), in_=E["suw"]), dma="wing")
        last_sil = cp.get(("lnb_pe_done",))
        last_mi = cp.get(("sh_mi_last",))
        gu_done = []
        for fm in range(8):
            for tt in range(2):
                gm = []
                for c in range(8):
                    w = []
                    if c == 0:
                        w = [lsg, lxb]
                        if last_sil is not None:
                            w.append(last_sil)
                    gm.append(p.add("tensor", lambda e, fm=fm, tt=tt, c=c: e.matmul(
                        S1p[:, 0:512], ab(bWG[0] + 1024 * c + 128 * fm, 128), ab(bXB + 1024 * c + 512 * tt, 512),
                        start=(c == 0), stop=(c == 7)), waits=w))
                sil = p.add("scalar", lambda e: e.activation(ab(bG, 512), S1p[:, 0:512], AF.Silu), waits=[gm[-1], (last_mi if last_mi is not None else None)], tag=f"sil{Q}.{fm}.{tt}")
                last_sil = sil
                um = []
                for c in range(8):
                    w = []
                    if c == 0:
                        w = [lsu]
                        if last_mi is not None:
                            w.append(last_mi)
                    um.append(p.add("tensor", lambda e, fm=fm, tt=tt, c=c: e.matmul(
                        S1p[:, 512:1024], ab(bWG[1] + 1024 * c + 128 * fm, 128), ab(bXB + 1024 * c + 512 * tt, 512),
                        start=(c == 0), stop=(c == 7)), waits=w))
                mi = p.add("vector", lambda e, fm=fm, tt=tt: e.tensor_tensor(
                    ab(bINT + 1024 * fm + 512 * tt, 512), ab(bG, 512), S1p[:, 512:1024], ALU.mult),
                    waits=[um[-1], sil], tag=f"smi{Q}.{fm}.{tt}")
                last_mi = mi
                gu_done.append(mi)
        cp[("sh_mi_last",)] = last_mi
        lsd = p.add("gpsimd", lambda e: e.dma_start(out=ab(bWG[0], 8192), in_=E["sdw"]), dma="wing", waits=[gu_done[-1], last_sil])
        last_rc = [cp.get(("lnb_pe_done",)), cp.get(("lnb_pe_done",))]
        dn_ops = []
        for dm in range(8):
            for tt in range(2):
                bank = tt
                psl = (S2p[:, 0:512], S2p[:, 512:1024])[bank]
                mm = []
                for c in range(8):
                    w = []
                    if c == 0:
                        w = [lsd] + gu_done[tt::2]
                        if last_rc[bank] is not None:
                            w.append(last_rc[bank])
                    mm.append(p.add("tensor", lambda e, dm=dm, tt=tt, c=c, psl=psl: e.matmul(
                        psl, ab(bWG[0] + 1024 * c + 128 * dm, 128), ab(bINT + 1024 * c + 512 * tt, 512),
                        start=(c == 0), stop=(c == 7)), waits=w))
                w2 = [mm[-1]]
                if ("shdma", Q - 1) in cp and dm == 0 and tt == 0:
                    w2.append(cp[("shdma", Q - 1)])
                rc = p.add("vector", lambda e, dm=dm, tt=tt, psl=psl: e.tensor_copy(
                    ab(bSH + 1024 * dm + 512 * tt, 512), psl), waits=w2, tag=f"sracc{Q}.{dm}.{tt}")
                last_rc[bank] = rc
                dn_ops.append(rc)
        cp[("shdn_last",)] = dn_ops[-1]
        cp[("lnb_pe_done",)] = dn_ops[-1]
        shd = p.add("sync", lambda e, tq=tq: e.dma_start(
            out=E["SHFM"].rearrange("(c p) t -> p c t", p=128)[:, :, tq:tq + 1024],
            in_=AB[:, bSH:bSH + 8192].rearrange("p (c t) -> p c t", c=8)), waits=dn_ops, dma="out")
        sh_dmas.append(shd)
        cp[("shdma", Q)] = shd
        cp[("ovp_free",)] = cp[("cmp_mm",)]

    for ei in range(NE * 2):
        wi = ei // 2
        wbuf = (wi + 2) % 3
        if ei % 2 == 0:
            if wi == 0:
                wld_w = [sh_dmas[-1]]
            elif wi == 1:
                wld_w = [cp[("shdn_last",)]]
            elif wi == 2:
                wld_w = [cp[("sh_mi_last",)]]
            else:
                wld_w = [cp[("emi_last", ei - 3)], cp[("edn_mm_last", ei - 3)]]
            lg_ = p.add("gpsimd", lambda e, wi=wi, wbuf=wbuf: e.dma_start(
                out=ab(bWG[wbuf], 4096), in_=E["gw"][wi]), dma="wing", waits=wld_w)
            lu_ = p.add("gpsimd", lambda e, wi=wi, wbuf=wbuf: e.dma_start(
                out=ab(bWG[wbuf] + 4096, 4096), in_=E["uw"][wi]), dma="wing")
            ld_ = p.add("gpsimd", lambda e, wi=wi, wbuf=wbuf: e.dma_start(
                out=ab(bWG[wbuf] + 8192, 4096), in_=E["dw"][wi]), dma="wing")
        wload = list(sc_dmas) if ei == 0 else [cp[("etr_last", ei - 1)]]
        lxc = []
        for t in range(8):
            lxc.append(p.add("sync", lambda e, ei=ei, t=t: e.dma_start(
                out=ab(bXB + 1024 * t, 1024),
                in_=E["XCOMP"][ei * C + 128 * t:ei * C + 128 * (t + 1), :]),
                waits=wload if t == 0 else [], dma="win"))
        xtrev = []
        for c in range(8):
            dstp = (trp0, trp1)[c % 2]
            trs = []
            for t in range(8):
                w = [lxc[t]]
                if t == 0:
                    if c >= 2:
                        w.append(cp[("xtrev", ei, c - 2)])
                    elif ("odtrev", ei - 1, c + 6) in cp:
                        w.append(cp[("odtrev", ei - 1, c + 6)])
                    elif ei == 0:
                        w.append(st["trev_last"][c % 2])
                trs.append(p.add("tensor", lambda e, c=c, t=t, dstp=dstp: e.transpose(
                    dstp[:, 128 * t:128 * t + 128], ab(bXB + 1024 * t + 128 * c, 128), ab(bIDENT, 128)), waits=w))
            w2 = [trs[-1]]
            if ("xgate", ei - 1) in cp and c < 2:
                w2.append(cp[("xgate", ei - 1)])
            if ei == 0:
                w2.append(sh_dmas[-1])
            tev = p.add("vector", lambda e, c=c, dstp=dstp: e.tensor_copy(ab(bSH + 1024 * c, 1024), dstp[:, :]), waits=w2)
            cp[("xtrev", ei, c)] = tev
            xtrev.append(tev)
        cp[("etr_last", ei)] = xtrev[-1]
        last_esil = cp.get(("esil_last",))
        last_emi = cp.get(("emi_bank_last",))
        emi = []
        for fm in range(4):
            for tt in range(2):
                gm = []
                for c in range(8):
                    w = []
                    if c == 0:
                        w = [lg_, xtrev[c]] if fm == 0 and tt == 0 else [lg_]
                        w += [x for x in (last_esil,) if x is not None]
                        if fm == 0 and tt == 0:
                            w += xtrev
                    gm.append(p.add("tensor", lambda e, fm=fm, tt=tt, c=c, wbuf=wbuf: e.matmul(
                        S1p[:, 0:512], ab(bWG[wbuf] + 512 * c + 128 * fm, 128), ab(bSH + 1024 * c + 512 * tt, 512),
                        start=(c == 0), stop=(c == 7)), waits=w))
                cp[("xgate", ei)] = gm[-1]
                sil = p.add("scalar", lambda e: e.activation(ab(bG, 512), S1p[:, 0:512], AF.Silu),
                            waits=[gm[-1]] + ([last_emi] if last_emi is not None else []))
                last_esil = sil
                um = []
                for c in range(8):
                    w = []
                    if c == 0:
                        w = [lu_]
                        if last_emi is not None:
                            w.append(last_emi)
                    um.append(p.add("tensor", lambda e, fm=fm, tt=tt, c=c, wbuf=wbuf: e.matmul(
                        S1p[:, 512:1024], ab(bWG[wbuf] + 4096 + 512 * c + 128 * fm, 128), ab(bSH + 1024 * c + 512 * tt, 512),
                        start=(c == 0), stop=(c == 7)), waits=w))
                mi1 = p.add("vector", lambda e, fm=fm, tt=tt: e.tensor_tensor(
                    ab(bINT + 1024 * fm + 512 * tt, 512), ab(bG, 512), S1p[:, 512:1024], ALU.mult),
                    waits=[um[-1], sil])
                last_emi = mi1
                emi.append(mi1)
        cp[("esil_last",)] = last_esil
        cp[("emi_bank_last",)] = last_emi
        cp[("emi_last", ei)] = emi[-1]
        last_rc2 = cp.get(("edn_rc_last",), [None, None])
        edn = []
        for dm in range(8):
            for tt in range(2):
                bank = tt
                psl = (S2p[:, 0:512], S2p[:, 512:1024])[bank]
                mm = []
                for c in range(4):
                    w = []
                    if c == 0:
                        w = [ld_] + emi[tt::2]
                        if last_rc2[bank] is not None:
                            w.append(last_rc2[bank])
                    mm.append(p.add("tensor", lambda e, dm=dm, tt=tt, c=c, psl=psl, wbuf=wbuf: e.matmul(
                        psl, ab(bWG[wbuf] + 8192 + 1024 * c + 128 * dm, 128), ab(bINT + 1024 * c + 512 * tt, 512),
                        start=(c == 0), stop=(c == 3)), waits=w))
                w2 = [mm[-1]]
                if ("odtr_last", ei - 1) in cp and dm == 0 and tt == 0:
                    w2.append(cp[("odtr_last", ei - 1)])
                rc = p.add("vector", lambda e, dm=dm, tt=tt, psl=psl: e.tensor_copy(
                    ab(bDWN + 1024 * dm + 512 * tt, 512), psl), waits=w2)
                last_rc2[bank] = rc
                edn.append(rc)
        cp[("edn_rc_last",)] = last_rc2
        cp[("edn_mm_last", ei)] = mm[-1]
        oglast = None
        for t in range(8):
            dstp = (trp0, trp1)[t % 2]
            trs = []
            for dm in range(8):
                w = [edn[2 * dm + (t // 4)]]
                if dm == 0 and ("odtrev", ei, t - 2) in cp:
                    w.append(cp[("odtrev", ei, t - 2)])
                elif dm == 0 and t < 2:
                    w.append(cp[("xtrev", ei, t + 6)])
                trs.append(p.add("tensor", lambda e, t=t, dm=dm, dstp=dstp: e.transpose(
                    dstp[:, 128 * dm:128 * dm + 128], ab(bDWN + 1024 * dm + 128 * t, 128), ab(bIDENT, 128)), waits=w))
            slot = t % 2
            wev = [trs[-1]]
            if ("xod", ei, t - 2) in cp:
                wev.append(cp[("xod", ei, t - 2)])
            elif t < 2 and ("xod", ei - 1, t + 6) in cp:
                wev.append(cp[("xod", ei - 1, t + 6)])
            oev = p.add("vector", lambda e, t=t, slot=slot, dstp=dstp: e.tensor_copy(
                ab(bXO + 1024 * slot, 1024), dstp[:, :]), waits=wev)
            cp[("odtrev", ei, t)] = oev
            xod = p.add("sync", lambda e, ei=ei, t=t, slot=slot: e.dma_start(
                out=E["XOUT"][ei * C + 128 * t:ei * C + 128 * (t + 1), :],
                in_=ab(bXO + 1024 * slot, 1024)), waits=[oev], dma="out")
            cp[("xod", ei, t)] = xod
            oglast = xod
        cp[("odtr_last", ei)] = cp[("odtrev", ei, 7)]
        cp[("escat_last", ei)] = oglast

    all_scats = [cp[("escat_last", e_)] for e_ in range(NE * 2)] + [cp[("xod", NE * 2 - 1, t)] for t in range(8)]
    slcv_all = [cp[("slcv", gg)] for gg in range(32)]
    bGT4 = [bXO, bXO + 1024, bDWN, bDWN + 1024]
    x1f_r2 = x1f_r
    shf_r = E["SHFM"].rearrange("(c p) t -> p c t", p=128)
    for g in range(32):
        lx1 = p.add("sync", lambda e, g=g: e.dma_start(
            out=AFr[:, fX1Q:fX1Q + 1024].rearrange("p (c t) -> p c t", c=8),
            in_=x1f_r2[:, :, 128 * g:128 * g + 128]), dma="win",
            waits=[cp[("ydma", g - 2)]] if ("ydma", g - 2) in cp else [])
        lsh = p.add("sync", lambda e, g=g: e.dma_start(
            out=AB[:, bSHF:bSHF + 1024].rearrange("p (c t) -> p c t", c=8),
            in_=shf_r[:, :, 128 * g:128 * g + 128]), dma="win",
            waits=sh_dmas if g == 0 else [])
        fSL = fSLC + 8 * (g % 2)
        aco = 2 * (g % 2)
        wl = list(slcv_all) if g == 0 else []
        if ("gmulB", g - 2) in cp:
            wl.append(cp[("gmulB", g - 2)])
        lsl = p.add("sync", lambda e, g=g, fSL=fSL: e.dma_start(
            out=AFr[:, fSL:fSL + 4], in_=E["SLCV"][g * 128:(g + 1) * 128, :]), waits=wl, dma="win")
        aics = []
        for rk in range(2):
            wa = [lsl]
            if ("gat", g - 2, rk) in cp:
                wa.append(cp[("gat", g - 2, rk)])
            elif g == 0:
                wa.append(cp[("scx", 31, rk)])
            aics.append(p.add("vector", lambda e, rk=rk, fSL=fSL, aco=aco: e.tensor_copy(
                AI[:, aco + rk:aco + rk + 1], af(fSL + 2 * rk, 1)), waits=wa))
        gats = []
        for rk in range(2):
            gslot = bGT4[2 * (g % 2) + rk]
            wg_ = [aics[rk]] + (all_scats if g == 0 else [])
            if ("gmul", g - 2, rk) in cp:
                wg_.append(cp[("gmul", g - 2, rk)])
            gat = p.add("gpsimd", lambda e, rk=rk, gslot=gslot, aco=aco: e.indirect_dma_start(
                AB[:, gslot:gslot + 1024], None,
                E["XOUT"], bass.IndirectOffsetOnAxis(AI[:, aco + rk:aco + rk + 1], 0)),
                waits=wg_, dma="wsc")
            cp[("gat", g, rk)] = gat
            gats.append(gat)
        cA = p.add("vector", lambda e, g=g, fSL=fSL: e.tensor_scalar(
            af(fRAC, 1024), AB[:, bGT4[2 * (g % 2)]:bGT4[2 * (g % 2)] + 1024], af(fSL + 1, 1), None, ALU.mult),
            waits=[gats[0]])
        cB = p.add("vector", lambda e, g=g, fSL=fSL: e.tensor_scalar(
            af(fGB, 1024), AB[:, bGT4[2 * (g % 2) + 1]:bGT4[2 * (g % 2) + 1] + 1024], af(fSL + 3, 1), None, ALU.mult),
            waits=[gats[1]])
        cp[("gmul", g, 0)] = cA
        cp[("gmul", g, 1)] = cB
        cp[("gmulB", g)] = cB
        cmb = p.add("vector", lambda e: e.tensor_tensor(af(fCMB, 1024), af(fX1Q, 1024), ab(bSHF, 1024), ALU.add),
                    waits=[lx1, lsh, cp.get(("yts_last",))], tag=f"cmb{g}")
        radd = []
        for half in range(2):
            trs = []
            for c4 in range(4):
                c = 4 * half + c4
                w = [cmb]
                if half == 0 and c4 == 0 and ("yev", g - 1) in cp:
                    w.append(cp[("yev", g - 1)])
                if c4 == 0 and half == 1:
                    w.append(radd[0])
                trs.append(p.add("tensor", lambda e, c=c, c4=c4: e.transpose(
                    ovp[:, 128 * c4:128 * c4 + 128], af(fCMB + 128 * c, 128), af(fIDENT, 128)), waits=w))
            ra = p.add("vector", lambda e, half=half: e.tensor_tensor(
                af(fR2 + 512 * half, 512), ovp[:, 0:512], af(fRAC + 512 * half, 512), ALU.add),
                waits=[trs[-1], cA], tag=f"ra{g}.{half}")
            radd.append(ra)
        rb = p.add("vector", lambda e: e.tensor_tensor(af(fR2, 1024), af(fR2, 1024), af(fGB, 1024), ALU.add),
                   waits=[radd[-1], cB])
        cp[("yev", g)] = rb
        s1 = p.add("scalar", lambda e: e.activation(af(fLNS, 1024), af(fR2, 1024), AF.Identity, accum_out=af(fT2, 1)),
                   waits=[rb] + ([cp[("ydma", g - 1)]] if ("ydma", g - 1) in cp else []))
        s2 = p.add("scalar", lambda e: e.activation(af(fLNS, 1024), af(fR2, 1024), AF.Square, accum_out=af(fT2 + 1, 1)))
        mb2 = p.add("vector", lambda e: e.tensor_scalar(af(fT2 + 2, 1), af(fT2, 1), 1.0 / D, None, ALU.mult), waits=[s1, s2])
        tb2 = p.add("vector", lambda e: e.tensor_scalar(af(fT2 + 3, 1), af(fT2 + 1, 1), 1.0 / D, EPS, ALU.mult, ALU.add))
        m2b = p.add("vector", lambda e: e.tensor_tensor(af(fT2 + 4, 1), af(fT2 + 2, 1), af(fT2 + 2, 1), ALU.mult))
        vrb = p.add("vector", lambda e: e.tensor_tensor(af(fT2 + 4, 1), af(fT2 + 3, 1), af(fT2 + 4, 1), ALU.subtract))
        lnb2 = p.add("scalar", lambda e: e.activation(af(fT2 + 5, 1), af(fT2 + 4, 1), AF.Ln), waits=[vrb])
        rsb2 = p.add("scalar", lambda e: e.activation(af(fT2 + 6, 1), af(fT2 + 5, 1), AF.Exp, scale=-0.5))
        yts = p.add("vector", lambda e: e.tensor_scalar(af(fLNS, 1024), af(fR2, 1024), af(fT2 + 2, 1), af(fT2 + 6, 1), ALU.subtract, ALU.mult), waits=[rsb2], tag=f"yts{g}")
        cp[("yts_last",)] = yts
        yd = p.add("sync", lambda e, g=g: e.dma_start(
            out=E["YT"][128 * g:128 * (g + 1), :], in_=af(fLNS, 1024)), waits=[yts], dma="out")
        cp[("ydma", g)] = yd
        y_dmas.append(yd)

    p.add("sync", lambda e: e.nop(), waits=y_dmas)


def swz(m):
    m = np.asarray(m, dtype=np.float32)
    c = m.shape[0] // 128
    return np.ascontiguousarray(m.reshape(c, 128, m.shape[1]).transpose(1, 0, 2).reshape(128, -1))


def make_inputs_for_core(inputs, b):
    f32 = lambda x: np.ascontiguousarray(np.asarray(x, dtype=np.float32))
    d = {}
    d["XT"] = f32(np.asarray(inputs["L0_attn_X"][b]).T)
    d["WqT"] = swz(np.asarray(inputs["L0_attn_Wq"]).T)
    d["WkT"] = swz(np.asarray(inputs["L0_attn_Wk"]).T)
    d["WvT"] = swz(np.asarray(inputs["L0_attn_Wv"]).T)
    d["WoT"] = swz(np.asarray(inputs["L0_attn_Wo"]).T)
    d["lq1"] = f32(inputs["L0_attn_lambda_q1"])
    d["lk1"] = f32(inputs["L0_attn_lambda_k1"])
    d["lq2"] = f32(inputs["L0_attn_lambda_q2"])
    d["lk2"] = f32(inputs["L0_attn_lambda_k2"])
    d["rms_scale"] = f32(np.asarray(inputs["L0_attn_rms_scale"])[None, :])
    d["rw"] = swz(np.asarray(inputs["L0_ffn_router_w"]))
    d["gw"] = np.stack([swz(np.asarray(inputs["L0_ffn_gate_w"])[e]) for e in range(NE)])
    d["uw"] = np.stack([swz(np.asarray(inputs["L0_ffn_up_w"])[e]) for e in range(NE)])
    d["dw"] = np.stack([swz(np.asarray(inputs["L0_ffn_down_w"])[e]) for e in range(NE)])
    d["sgw"] = swz(np.asarray(inputs["L0_ffn_shared_gate_w"]))
    d["suw"] = swz(np.asarray(inputs["L0_ffn_shared_up_w"]))
    d["sdw"] = swz(np.asarray(inputs["L0_ffn_shared_down_w"]))
    d["identf"] = np.eye(128, dtype=np.float32)
    d["onesf"] = np.ones((1, 128), dtype=np.float32)
    d["iotap"] = np.arange(128, dtype=np.float32).reshape(128, 1)
    d["eids"] = np.arange(NE, dtype=np.float32).reshape(1, NE)
    d["ltri"] = np.triu(np.ones((128, 128), dtype=np.float32), 1)
    return d


def kernel(**inputs):
    nc = build_program()
    in_maps = [make_inputs_for_core(inputs, b) for b in range(8)]
    res = run_bass_kernel_spmd(nc, in_maps, core_ids=list(range(8)))
    out = np.stack([np.asarray(r["YT"]) for r in res.results], axis=0)
    return np.ascontiguousarray(out).astype(np.float32)


# version 15
# speedup vs baseline: 1.2353x; 1.0018x over previous
import math
import numpy as np
import ml_dtypes
import concourse.bass as bass
import concourse.mybir as mybir
from concourse.bass_utils import run_bass_kernel_spmd

F32 = mybir.dt.float32
BF16 = mybir.dt.bfloat16
AF = mybir.ActivationFunctionType
ALU = mybir.AluOpType
AX = mybir.AxisListType

T = 4096
D = 1024
H = 8
DV = 256
NE = 16
FE = 512
FS = 1024
SCALE = 1.0 / math.sqrt(128.0)
LAMBDA_INIT = 0.8 - 0.6 * math.exp(-0.3)
EPS = 1e-5

ENGINES = ("tensor", "scalar", "vector", "gpsimd", "sync")
SER_SC = False


DMA_POOL = 8


class Op:
    __slots__ = ("eng", "fn", "waits", "dma", "marked", "sem", "val", "pre", "tag")

    def __init__(self, eng, fn, waits, dma, tag):
        self.eng, self.fn, self.dma, self.tag = eng, fn, dma, tag
        self.waits = [w for w in waits if w is not None]
        self.marked = False
        self.sem = None
        self.val = None
        self.pre = None


class Prog:
    def __init__(self):
        self.ops = []

    def add(self, eng, fn, waits=(), dma=None, tag=""):
        op = Op(eng, fn, waits, dma, tag)
        self.ops.append(op)
        return op

    def resolve(self, nc, ctx):
        groups = sorted({op.dma for op in self.ops if op.dma is not None})
        sems = {}
        for e in ENGINES:
            sems["e_" + e] = ctx.enter_context(nc.semaphore("e_" + e))
        pool = {}
        for g in groups:
            pool[g] = [ctx.enter_context(nc.semaphore(f"d_{g}{i}")) for i in range(DMA_POOL)]
        CHAIN = ("vector", "scalar", "gpsimd")
        for op in self.ops:
            kept = []
            for w in op.waits:
                if w.dma is None and w.eng == op.eng and op.dma is None:
                    continue
                w.marked = True
                kept.append(w)
            op.waits = kept
        for op in self.ops:
            if op.eng in CHAIN and op.dma is None:
                op.marked = True
        counts = {("e_" + e): 0 for e in ENGINES}
        rr = {g: 0 for g in groups}
        dma_counts = {}
        for op in self.ops:
            if op.dma is not None:
                op.marked = True
                i = rr[op.dma]
                rr[op.dma] = (i + 1) % DMA_POOL
                sem = pool[op.dma][i]
                prevv = dma_counts.get(id(sem), 0)
                if prevv:
                    op.pre = (sem, prevv)
                op.sem = sem
                op.val = prevv + 16
                dma_counts[id(sem)] = op.val
            elif op.marked:
                cnt = counts["e_" + op.eng]
                if op.eng in CHAIN and cnt > 0:
                    op.pre = (sems["e_" + op.eng], cnt)
                counts["e_" + op.eng] = cnt + 1
                op.sem = sems["e_" + op.eng]
                op.val = cnt + 1
            elif op.eng in CHAIN:
                pass
        self.sems = sems
        self.drain_waits = []
        seen = set()
        for op in self.ops:
            if op.dma is not None and id(op.sem) not in seen:
                seen.add(id(op.sem))
        finals = {}
        for op in self.ops:
            if op.dma is not None:
                finals[id(op.sem)] = (op.sem, op.val)
        self.drain_waits = list(finals.values())

    def emit(self, eng_name, eng):
        watermark = {}
        for op in self.ops:
            if op.eng != eng_name:
                continue
            pend = list(op.waits)
            for w in pend:
                key = id(w.sem)
                if watermark.get(key, -1) >= w.val:
                    continue
                eng.wait_ge(w.sem, w.val)
                watermark[key] = w.val
            if op.pre is not None:
                sem, v = op.pre
                if watermark.get(id(sem), -1) < v:
                    eng.wait_ge(sem, v)
                    watermark[id(sem)] = v
            inst = op.fn(eng)
            if op.marked:
                inst.then_inc(op.sem, 16 if op.dma is not None else 1)
        if eng_name == "sync":
            for sem, v in self.drain_waits:
                if watermark.get(id(sem), -1) < v:
                    eng.wait_ge(sem, v)


def build_program(dbg=()):
    nc = bass.Bass("TRN2", target_bir_lowering=False)
    dt = nc.dram_tensor
    E = {}

    for n, s in (("XT", [D, T]), ("WqT", [128, 16384]), ("WkT", [128, 16384]),
                 ("WvT", [128, 16384]), ("WoT", [128, 16384]),
                 ("lq1", [H, 128]), ("lk1", [H, 128]), ("lq2", [H, 128]), ("lk2", [H, 128]),
                 ("rms_scale", [1, DV]), ("rw", [128, 128]),
                 ("gw", [NE, 128, 4096]), ("uw", [NE, 128, 4096]), ("dw", [NE, 128, 4096]),
                 ("sgw", [128, 8192]), ("suw", [128, 8192]), ("sdw", [128, 8192]),
                 ("identf", [128, 128]), ("onesf", [1, 128])):
        E[n] = dt(n, s, F32, kind="ExternalInput").ap()
    E["iotap"] = dt("iotap", [128, 1], F32, kind="ExternalInput").ap()
    E["eids"] = dt("eids", [1, NE], F32, kind="ExternalInput").ap()
    E["ltri"] = dt("ltri", [128, 128], F32, kind="ExternalInput").ap()
    E["YT"] = dt("YT", [T, D], F32, kind="ExternalOutput").ap()

    def scratch(name, shape, dtype):
        kind = "ExternalOutput" if name in dbg else "Internal"
        return dt(name, shape, dtype, kind=kind).ap()

    if "stop_cmp0" in dbg:
        E["DBG_CMP"] = dt("DBG_CMP", [128, 256], F32, kind="ExternalOutput").ap()
        E["DBG_AI"] = dt("DBG_AI", [128, 32], mybir.dt.int32, kind="ExternalOutput").ap()
    if "probe_u0" in dbg:
        E["DBG_P"] = dt("DBG_P", [128, 4096], BF16, kind="ExternalOutput").ap()
        E["DBG_E2"] = dt("DBG_E2", [128, 4096], BF16, kind="ExternalOutput").ap()
        E["DBG_DEN"] = dt("DBG_DEN", [128, 32], F32, kind="ExternalOutput").ap()
        E["DBG_PT"] = dt("DBG_PT", [128, 4096], BF16, kind="ExternalOutput").ap()
        E["DBG_ON"] = dt("DBG_ON", [128, 256], F32, kind="ExternalOutput").ap()
        E["DBG_OTM"] = dt("DBG_OTM", [128, 256], BF16, kind="ExternalOutput").ap()
        E["DBG_Q"] = dt("DBG_Q", [128, 8192], BF16, kind="ExternalOutput").ap()
        E["DBG_K"] = dt("DBG_K", [128, 8192], BF16, kind="ExternalOutput").ap()
        E["DBG_V"] = dt("DBG_V", [128, 8192], BF16, kind="ExternalOutput").ap()
    E["QTh"] = scratch("QTh", [2048, T], BF16)
    E["KTh"] = scratch("KTh", [2048, T], BF16)
    E["VH"] = scratch("VH", [T, 2048], BF16)
    E["OT"] = scratch("OT", [2048, T], BF16)
    E["X1F"] = scratch("X1F", [D, T], F32)
    E["X1B"] = scratch("X1B", [D, T], BF16)
    E["X1TM"] = scratch("X1TM", [T, D], BF16)
    E["XCOMP"] = scratch("XCOMP", [NE * 2048, D], BF16)
    E["XOUT"] = scratch("XOUT", [NE * 2048, D], BF16)
    E["SLCV"] = scratch("SLCV", [T, 4], F32)
    E["SHFM"] = scratch("SHFM", [D, T], BF16)

    from contextlib import ExitStack
    with ExitStack() as ctx:
        E["AB"] = ctx.enter_context(nc.sbuf_tensor("AB", [128, 73728], BF16))
        E["AFr"] = ctx.enter_context(nc.sbuf_tensor("AFr", [128, 15488], F32))
        E["AI"] = ctx.enter_context(nc.sbuf_tensor("AI", [128, 32], mybir.dt.int32))
        E["S1p"] = ctx.enter_context(nc.psum_tensor("S1p", [128, 1024], F32))
        E["S2p"] = ctx.enter_context(nc.psum_tensor("S2p", [128, 1024], F32))
        E["trp0"] = ctx.enter_context(nc.psum_tensor("trp0", [128, 1024], BF16))
        E["trp1"] = ctx.enter_context(nc.psum_tensor("trp1", [128, 1024], BF16))
        E["ovp"] = ctx.enter_context(nc.psum_tensor("ovp", [128, 512], F32))
        E["pob"] = ctx.enter_context(nc.psum_tensor("pob", [128, 1024], BF16))

        p = Prog()
        emit_all(p, E, dbg)
        p.resolve(nc, ctx)

        with nc.Block() as block:
            @block.tensor
            def _(e):
                p.emit("tensor", e)

            @block.scalar
            def _(e):
                p.emit("scalar", e)

            @block.vector
            def _(e):
                p.emit("vector", e)

            @block.gpsimd
            def _(e):
                p.emit("gpsimd", e)

            @block.sync
            def _(e):
                p.emit("sync", e)
    return nc


def emit_all(p, E, dbg):
    AB, AFr, AI = E["AB"], E["AFr"], E["AI"]
    S1p, S2p, trp0, trp1, ovp, pob = E["S1p"], E["S2p"], E["trp0"], E["trp1"], E["ovp"], E["pob"]

    def ab(off, n, p0=0, np_=128):
        return AB[p0:p0 + np_, off:off + n]

    def af(off, n, p0=0, np_=128):
        return AFr[p0:p0 + np_, off:off + n]

    fIDENT, fONES, fRMSB, fLAM = 0, 128, 256, 512
    fBIG = 640
    bIDENT, bONES, bSEL = 0, 128, 256
    bBIG = bSEL + 2048

    fIOTA, fONC, fEIDS, fEIDB = 520, 521, 524, 544
    fLTRI = 15232
    ld_id = p.add("sync", lambda e: e.dma_start(out=af(fIDENT, 128), in_=E["identf"]), dma="win")
    ld_on = p.add("sync", lambda e: e.dma_start(out=af(fONES, 128, 0, 1), in_=E["onesf"]), dma="win")
    ld_io = p.add("sync", lambda e: e.dma_start(out=af(fIOTA, 1), in_=E["iotap"]), dma="win")
    ld_ei = p.add("sync", lambda e: e.dma_start(out=af(fEIDS, NE, 0, 1), in_=E["eids"]), dma="win")
    ld_idb = p.add("gpsimd", lambda e: e.dma_start(out=ab(bIDENT, 128), in_=E["identf"]), dma="wing")
    ms_ones = p.add("gpsimd", lambda e: e.memset(ab(bONES, 128), 1.0))
    ms_onc = p.add("vector", lambda e: e.memset(af(fONC, 1), 1.0))
    mm_ei = p.add("tensor", lambda e: e.matmul(ovp[:, 272:288], af(fONES, 128, 0, 1), af(fEIDS, NE, 0, 1), start=True, stop=True), waits=[ld_on, ld_ei])
    ev_ei = p.add("vector", lambda e: e.tensor_copy(af(fEIDB, NE), ovp[:, 272:288]), waits=[mm_ei])

    fSM = fBIG
    ld_rs = p.add("sync", lambda e: e.dma_start(out=af(fSM, DV, 0, 1), in_=E["rms_scale"]), dma="win")
    mm_rs = p.add("tensor", lambda e: e.matmul(ovp[:, 0:256], af(fONES, 128, 0, 1), af(fSM, DV, 0, 1), start=True, stop=True), waits=[ld_on, ld_rs])
    ev_rs = p.add("vector", lambda e: e.tensor_scalar(af(fRMSB, DV), ovp[:, 0:256], 1.0 - LAMBDA_INIT, None, ALU.mult), waits=[mm_rs])

    fL = fBIG + 256
    fLs = fL + 512
    ldl = []
    for i, t in enumerate(("lq1", "lk1", "lq2", "lk2")):
        ldl.append(p.add("sync", lambda e, t=t, i=i: e.dma_start(out=af(fL + 128 * i, 128, 0, 8), in_=E[t]), dma="win"))
    p.add("vector", lambda e: e.tensor_tensor(af(fL, 128, 0, 8), af(fL, 128, 0, 8), af(fL + 128, 128, 0, 8), ALU.mult), waits=ldl[:2])
    p.add("vector", lambda e: e.tensor_tensor(af(fL + 256, 128, 0, 8), af(fL + 256, 128, 0, 8), af(fL + 384, 128, 0, 8), ALU.mult), waits=ldl[2:])
    rs1 = p.add("vector", lambda e: e.tensor_reduce(af(fLs, 1, 0, 8), af(fL, 128, 0, 8), AX.X, ALU.add))
    rs2 = p.add("vector", lambda e: e.tensor_reduce(af(fLs + 1, 1, 0, 8), af(fL + 256, 128, 0, 8), AX.X, ALU.add))
    ex12 = p.add("scalar", lambda e: e.activation(af(fLs + 2, 2, 0, 8), af(fLs, 2, 0, 8), AF.Exp), waits=[rs1, rs2])
    p.add("vector", lambda e: e.tensor_tensor(af(fLs + 4, 1, 0, 8), af(fLs + 3, 1, 0, 8), af(fLs + 2, 1, 0, 8), ALU.subtract), waits=[ex12])
    nl2 = p.add("vector", lambda e: e.tensor_scalar(af(fLs + 5, 1, 0, 8), af(fLs + 4, 1, 0, 8), -LAMBDA_INIT, None, ALU.add))
    trl = p.add("tensor", lambda e: e.transpose(ovp[0:1, 256:264], af(fLs + 5, 1, 0, 8), af(fIDENT, 8, 0, 8)), waits=[nl2, ev_rs, ld_id])
    evl = p.add("vector", lambda e: e.tensor_copy(af(fLs + 6, 8, 0, 1), ovp[0:1, 256:264]), waits=[trl])
    mml = p.add("tensor", lambda e: e.matmul(ovp[:, 264:272], af(fONES, 128, 0, 1), af(fLs + 6, 8, 0, 1), start=True, stop=True), waits=[evl])
    nlam = p.add("vector", lambda e: e.tensor_copy(af(fLAM, 8), ovp[:, 264:272]), waits=[mml])

    bXTB = bBIG
    bW = bXTB + 32768
    bW2 = bW + 16384
    bSTG = bW2 + 16384
    assert bSTG + 4096 <= 73728

    xchunks = [p.add("gpsimd",
                     lambda e, c=c: e.dma_start(out=ab(bXTB + 4096 * c, 4096), in_=E["XT"][c * 128:(c + 1) * 128, :]),
                     dma="wing") for c in range(8)]
    ldwq = p.add("gpsimd", lambda e: e.dma_start(out=ab(bW, 16384), in_=E["WqT"]), dma="wing")
    ldwk = p.add("gpsimd", lambda e: e.dma_start(out=ab(bW2, 16384), in_=E["WkT"]), dma="wing")

    vh_r = E["VH"].rearrange("(t p) o -> p t o", p=128)

    def proj_fm(dst, woff, wait_w, tag, stg_prev=None, ev_seed=None):
        dmas = []
        prev_stg = stg_prev
        prev_ev = dict(ev_seed) if ev_seed else {}
        for m in range(16):
            evs = []
            for n in range(8):
                bank = (m * 8 + n) % 2
                psl = (S1p[:, 0:512], S1p[:, 512:1024])[bank]
                mms = []
                for c in range(8):
                    w = []
                    if c == 0:
                        w = [wait_w] + xchunks
                        if prev_ev.get(bank) is not None:
                            w.append(prev_ev[bank])
                    mms.append(p.add(
                        "tensor",
                        lambda e, m=m, n=n, c=c, psl=psl, woff=woff: e.matmul(
                            psl, ab(woff + 2048 * c + 128 * m, 128),
                            ab(bXTB + 4096 * c + 512 * n, 512),
                            start=(c == 0), stop=(c == 7)),
                        waits=w, tag=f"{tag}mm"))
                w2 = [mms[-1]]
                if n == 0:
                    if dmas:
                        w2.append(dmas[-1])
                    elif prev_stg is not None:
                        w2.append(prev_stg)
                ev = p.add("scalar", lambda e, n=n, psl=psl: e.copy(ab(bSTG + 512 * n, 512), psl), waits=w2)
                prev_ev[bank] = ev
                evs.append(ev)
            dmas.append(p.add(
                "sync",
                lambda e, m=m, dst=dst: e.dma_start(out=dst[m * 128:(m + 1) * 128, :], in_=ab(bSTG, 4096)),
                waits=evs, dma="out", tag=f"{tag}dma"))
        return dmas, prev_ev

    qdmas, qev = proj_fm(E["QTh"], bW, ldwq, "q")
    ldwv = p.add("gpsimd", lambda e: e.dma_start(out=ab(bW, 16384), in_=E["WvT"]), waits=[qdmas[-1]], dma="wing")
    kdmas, _ = proj_fm(E["KTh"], bW2, ldwk, "k", stg_prev=qdmas[-1], ev_seed=qev)

    vdmas = []
    prev_ev = {}
    for m in range(32):
        evs = []
        for gi in range(4):
            bank = (m * 4 + gi) % 2
            psl = (S2p[:, 0:512], S2p[:, 512:1024])[bank]
            mms = []
            for c in range(8):
                w = []
                if c == 0:
                    w = [ldwv]
                    if prev_ev.get(bank) is not None:
                        w.append(prev_ev[bank])
                mms.append(p.add(
                    "tensor",
                    lambda e, m=m, gi=gi, c=c, psl=psl: e.matmul(
                        psl, ab(bXTB + 4096 * c + 128 * m, 128),
                        ab(bW + 2048 * c + 512 * gi, 512),
                        start=(c == 0), stop=(c == 7)),
                    waits=w, tag="vmm"))
            w2 = [mms[-1]]
            if gi == 0:
                w2.append(vdmas[-1] if vdmas else kdmas[-1])
            ev = p.add("scalar", lambda e, gi=gi, psl=psl: e.copy(ab(bSTG + 512 * gi, 512), psl), waits=w2)
            prev_ev[bank] = ev
            evs.append(ev)
        vdmas.append(p.add(
            "sync",
            lambda e, m=m: e.dma_start(out=vh_r[:, m, :], in_=ab(bSTG, 2048)),
            waits=evs, dma="out"))

    if "stop_a0" in dbg:
        p.add("sync", lambda e: e.nop(), waits=qdmas + kdmas + vdmas)
        return

    bQ = bBIG
    bK = [bBIG + 8192, bBIG + 16384]
    bV = [bBIG + 24576, bBIG + 32768]
    bE1 = [bBIG + 40960, bBIG + 45056, bBIG + 49152]
    bE2 = [bBIG + 53248, bBIG + 57344]
    bPT = bBIG + 61440
    bOST = bPT + 4096
    bOTM = bOST + 1024
    assert bOTM + 512 <= 73728

    fD1 = [fBIG + 1024, fBIG + 1040]
    fD2 = [fBIG + 1028, fBIG + 1044]
    fDEN = [fBIG + 1056, fBIG + 1056 + 16]
    fON = [fBIG + 1100, fBIG + 1100 + 256]
    fSQ = fBIG + 1100 + 512

    qt_r = E["QTh"].rearrange("(c p) t -> p c t", p=128)
    kt_r = E["KTh"].rearrange("(c p) t -> p c t", p=128)

    def load_head(h, buf, waits):
        lq = p.add("sync", lambda e, h=h: e.dma_start(
            out=AB[:, bQ:bQ + 8192].rearrange("p (c t) -> p c t", c=2),
            in_=qt_r[:, 2 * h:2 * h + 2, :]), dma="win",
            waits=waits + (qdmas if h == 0 else []))
        lk = p.add("sync", lambda e, h=h, buf=buf: e.dma_start(
            out=AB[:, bK[buf]:bK[buf] + 8192].rearrange("p (c t) -> p c t", c=2),
            in_=kt_r[:, 2 * h:2 * h + 2, :]), dma="win",
            waits=(kdmas if h == 0 else []))
        lv = p.add("sync", lambda e, h=h, buf=buf: e.dma_start(
            out=AB[:, bV[buf]:bV[buf] + 8192].rearrange("p (t o) -> p t o", t=32),
            in_=vh_r[:, :, 256 * h:256 * h + 256]), dma="win",
            waits=(vdmas if h == 0 else []))
        return [lq, lk, lv]

    NU = H * 32
    st = {
        "loads": {0: load_head(0, 0, [])},
        "exp_last": [None, None],
        "comb": {}, "exps": {}, "dens": {}, "tr_last": {}, "pv": {},
        "epi": {}, "sq": {}, "otm": {}, "otr": {}, "ose": {}, "otdma": {},
        "head_last_exp": {},
        "trev_last": [None, None],
    }
    ot_dmas = []

    def emit_head_prefetch(u):
        h, qt = u // 32, u % 32
        if qt == 0 and h >= 1:
            lq = p.add("sync", lambda e, h=h: e.dma_start(
                out=AB[:, bQ:bQ + 8192].rearrange("p (c t) -> p c t", c=2),
                in_=qt_r[:, 2 * h:2 * h + 2, :]), dma="win",
                waits=[st["head_last_exp"][h - 1]])
            st["loads"][h].append(lq)
        if qt == 2 and h + 1 < H:
            w = []
            if h >= 1:
                w.append(st["pv"][(h - 1) * 32 + 31])
            lk = p.add("sync", lambda e, h1=h + 1, buf=1 - (h % 2): e.dma_start(
                out=AB[:, bK[buf]:bK[buf] + 8192].rearrange("p (c t) -> p c t", c=2),
                in_=kt_r[:, 2 * h1:2 * h1 + 2, :]), dma="win",
                waits=w)
            lv = p.add("sync", lambda e, h1=h + 1, buf=1 - (h % 2): e.dma_start(
                out=AB[:, bV[buf]:bV[buf] + 8192].rearrange("p (t o) -> p t o", t=32),
                in_=vh_r[:, :, 256 * h1:256 * h1 + 256]), dma="win")
            st["loads"][h + 1] = [lk, lv]


    def emit_S_span(u, j):
        h, qt = u // 32, u % 32
        buf = h % 2
        sl = u % 2
        loads = st["loads"][h]
        oQ, oK = bQ, bK[buf]
        exps = st["exps"].setdefault(u, {1: [], 2: []})
        for br, Sp, qoff, koff, bEo, fDx in (
                (1, S1p, oQ, oK, bE1[u % 3], fD1[sl]),
                (2, S2p, oQ + 4096, oK + 4096, bE2[sl], fD2[sl])):
            mms = []
            for half in range(2):
                kb = 2 * j + half
                w = []
                if kb == 0:
                    w = list(loads)
                    if st["exp_last"][br - 1] is not None:
                        w.append(st["exp_last"][br - 1])
                elif half == 0:
                    w.append(exps[br][j - 1])
                mms.append(p.add(
                    "tensor",
                    lambda e, qt=qt, kb=kb, Sp=Sp, qoff=qoff, koff=koff, half=half: e.matmul(
                        Sp[:, 512 * half:512 * half + 512],
                        ab(qoff + 128 * qt, 128), ab(koff + 512 * kb, 512),
                        start=True, stop=True),
                    waits=w, tag=f"s{br}u{u}"))
            ew = list(mms)
            if j == 0:
                if br == 1:
                    if (u - 3) in st["tr_last"]:
                        ew.append(st["tr_last"][u - 3])
                    if (u - 2) in st["dens"]:
                        ew.append(st["dens"][u - 2][0])
                if br == 2 and (u - 2) in st["comb"]:
                    ew.append(st["comb"][u - 2])
            exps[br].append(p.add(
                "scalar",
                lambda e, j=j, Sp=Sp, bEo=bEo, fDx=fDx: e.activation(
                    ab(bEo + 1024 * j, 1024), Sp[:, :], AF.Exp, scale=SCALE,
                    accum_out=af(fDx + j, 1)),
                waits=ew, tag=f"e{br}u{u}"))
            if j == 3:
                st["exp_last"][br - 1] = exps[br][-1]
                if qt == 31 and br == 2:
                    st["head_last_exp"][h] = exps[2][-1]

    def emit_TR(u, ktg_lo, ktg_hi):
        sl = u % 3
        c2 = st["comb"][u]
        for ktg in range(ktg_lo, ktg_hi + 1):
            dst = (trp0, trp1)[ktg % 2]
            trs = []
            for i in range(8):
                kt = 8 * ktg + i
                w = []
                if i == 0:
                    w = [c2, ld_idb]
                    if st["trev_last"][ktg % 2] is not None:
                        w.append(st["trev_last"][ktg % 2])
                trs.append(p.add(
                    "tensor",
                    lambda e, kt=kt, dst=dst, sl=sl: e.transpose(
                        dst[:, 128 * (kt % 8):128 * (kt % 8) + 128],
                        ab(bE1[sl] + 128 * kt, 128), ab(bIDENT, 128)),
                    waits=w, tag=f"tru{u}"))
            w2 = [trs[-1]]
            if (u - 1) in st["pv"]:
                w2.append(st["pv"][u - 1])
            tev = p.add(
                "vector",
                lambda e, ktg=ktg, dst=dst: e.tensor_copy(ab(bPT + 1024 * ktg, 1024), dst[:, :]),
                waits=w2, tag=f"trevu{u}")
            st["trev_last"][ktg % 2] = tev
            st.setdefault("trevs", {}).setdefault(u, []).append(tev)
            if ktg == 3:
                st["tr_last"][u] = trs[-1]

    def emit_PV(u, half):
        h = u // 32
        buf = h % 2
        sl = u % 2
        oV = bV[buf]
        trevs = st["trevs"][u]
        pvs = st.setdefault("pvs", {}).setdefault(u, [])
        for kt in range(16 * half, 16 * half + 16):
            w = []
            if kt % 8 == 0:
                w.append(trevs[kt // 8])
            if kt == 0:
                if (u - 1) in st["epi"]:
                    w.append(st["epi"][u - 1])
                if u == 0:
                    w += [nlam, ev_rs, ev_ei]
            pvs.append(p.add(
                "tensor",
                lambda e, kt=kt, sl=sl, oV=oV: e.matmul(
                    ovp[:, 256 * sl:256 * sl + 256],
                    ab(bPT + 128 * kt, 128), ab(oV + 256 * kt, 256),
                    start=(kt == 0), stop=(kt == 31)),
                waits=w, tag=f"pvu{u}"))
        if half == 1:
            st["pv"][u] = pvs[-1]

    def emit_epilogue(u):
        sl = u % 2
        fDx = fDEN[sl]
        fo = fON[sl]
        ms, r1 = st["dens"][u]
        ew = [st["pv"][u], r1]
        if (u - 2) in st["sq"]:
            ew.append(st["sq"][u - 2])
        epi = p.add("vector", lambda e, sl=sl, fo=fo, fDx=fDx: e.tensor_scalar(
            af(fo, 256), ovp[:, 256 * sl:256 * sl + 256], af(fDx + 4, 1), None, ALU.mult),
            waits=ew, tag=f"epiu{u}")
        st["epi"][u] = epi
        sqm = p.add("vector", lambda e, fo=fo: e.tensor_tensor(
            af(fSQ, 256), af(fo, 256), af(fo, 256), ALU.mult), tag=f"squ{u}")
        st["sq"][u] = sqm
        t2 = p.add("vector", lambda e, fDx=fDx: e.tensor_scalar(
            af(fSQ, 256), af(fSQ, 256), 1.0 / DV, EPS, ALU.mult, ALU.add,
            accum_out=af(fDx + 7, 1)), tag=f"t2u{u}")
        ln_ = p.add("scalar", lambda e, fDx=fDx: e.activation(af(fDx + 8, 1), af(fDx + 7, 1), AF.Ln), waits=[t2])
        rinv = p.add("scalar", lambda e, fDx=fDx: e.activation(af(fDx + 9, 1), af(fDx + 8, 1), AF.Exp, scale=-0.5), tag=f"rinvu{u}")
        p.add("vector", lambda e, fo=fo, fDx=fDx: e.tensor_scalar(af(fo, 256), af(fo, 256), af(fDx + 9, 1), None, ALU.mult), waits=[rinv])
        otm_w = [ev_rs]
        if (u - 2) in st["otr"]:
            otm_w.append(st["otr"][u - 2][-1])
        otm = p.add("vector", lambda e, sl=sl, fo=fo: e.tensor_tensor(
            ab(bOTM + 256 * sl, 256), af(fo, 256), af(fRMSB, 256), ALU.mult), waits=otm_w, tag=f"otmu{u}")
        st["otm"][u] = otm

    def emit_S_dve(u):
        h, qt = u // 32, u % 32
        sl = u % 2
        fDx = fDEN[sl]
        exps = st["exps"][u]
        p.add("vector", lambda e, sl=sl, fDx=fDx: e.tensor_reduce(
            af(fDx, 2), AFr[:, fD1[sl]:fD1[sl] + 8].rearrange("p (b x) -> p b x", b=2),
            AX.X, ALU.add), waits=exps[1] + exps[2])
        rd = p.add("vector", lambda e, fDx=fDx: e.reciprocal(af(fDx + 4, 2), af(fDx, 2)))
        p.add("vector", lambda e, fDx=fDx: e.tensor_tensor(af(fDx + 3, 1), af(fDx, 1), af(fDx + 5, 1), ALU.mult))
        ms = p.add("vector", lambda e, h=h, fDx=fDx: e.tensor_tensor(af(fDx + 6, 1), af(fDx + 3, 1), af(fLAM + h, 1), ALU.mult), waits=[nlam])
        st["dens"][u] = (ms, rd)
        c1 = p.add("vector", lambda e, sl=sl, fDx=fDx: e.tensor_scalar(ab(bE2[sl], 4096), ab(bE2[sl], 4096), af(fDx + 6, 1), None, ALU.mult), tag=f"c1u{u}")
        c2 = p.add("gpsimd", lambda e, sl=sl, s3=u % 3: e.tensor_tensor(ab(bE1[s3], 4096), ab(bE1[s3], 4096), ab(bE2[sl], 4096), ALU.add),
                   waits=[c1, exps[1][-1]], tag=f"c2u{u}")
        st["comb"][u] = c2

    def stage_OTR(u):
        h, qt = u // 32, u % 32
        sl = u % 2
        otrs = []
        for chv in range(2):
            w = [st["otm"][u]]
            if qt % 4 == 0 and chv == 0 and (u - 4) in st["ose"]:
                w += st["ose"][u - 4]
            otrs.append(p.add(
                "tensor",
                lambda e, qt=qt, chv=chv, sl=sl: e.transpose(
                    pob[:, 256 * (qt % 4) + 128 * chv:256 * (qt % 4) + 128 * chv + 128],
                    ab(bOTM + 256 * sl + 128 * chv, 128), ab(bIDENT, 128)),
                waits=w, tag=f"otru{u}"))
        st["otr"][u] = otrs
        if qt % 4 == 3:
            qb = u - 3
            grp = (qt - 3) // 4
            pob_v = pob[:, :].rearrange("p (q x) -> p q x", q=4)
            oses = []
            all_otrs = [o for i in range(4) for o in st["otr"][u - 3 + i]]
            for ch in range(2):
                w = list(all_otrs)
                prevd = st["otdma"].get((h, grp - 1), st["otdma"].get((h - 1, 7)))
                if prevd is not None:
                    w.append(prevd[ch])
                oses.append(p.add(
                    "vector",
                    lambda e, ch=ch: e.tensor_copy(
                        AB[:, bOST + 512 * ch:bOST + 512 * ch + 512].rearrange("p (q x) -> p q x", q=4),
                        pob_v[:, :, 128 * ch:128 * ch + 128]),
                    waits=w, tag=f"oseu{u}"))
            st["ose"][qb] = oses
            pair = []
            for ch in range(2):
                dma = p.add(
                    "sync",
                    lambda e, h=h, grp=grp, ch=ch: e.dma_start(
                        out=E["OT"][h * 256 + ch * 128:h * 256 + ch * 128 + 128, grp * 512:grp * 512 + 512],
                        in_=ab(bOST + 512 * ch, 512)),
                    waits=[oses[ch]], dma="out", tag=f"otdma{h}.{grp}")
                ot_dmas.append(dma)
                pair.append(dma)
            st["otdma"][(h, grp)] = pair

    nheads = H
    for d in dbg:
        if isinstance(d, str) and d.startswith("nh"):
            nheads = int(d[2:])
    NU = nheads * 32

    for u in range(NU + 4):
        cur = u < NU
        prv = 2 <= u <= NU + 1
        if cur:
            emit_head_prefetch(u)
            emit_S_span(u, 0)
        if 4 <= u <= NU + 3:
            stage_OTR(u - 4)
        if prv:
            emit_TR(u - 2, 0, 0)
        if cur:
            emit_S_span(u, 1)
        if prv:
            emit_TR(u - 2, 1, 2)
        if cur:
            emit_S_span(u, 2)
        if prv:
            emit_TR(u - 2, 3, 3)
            emit_PV(u - 2, 0)
        if cur:
            emit_S_span(u, 3)
        if prv:
            emit_PV(u - 2, 1)
            emit_epilogue(u - 2)
        if cur:
            emit_S_dve(u)

    if "stop_a1" in dbg:
        p.add("sync", lambda e: e.nop(), waits=ot_dmas)
        return

    bWO = bBIG
    bOB = [bWO + 16384, bWO + 24576]
    bRB = bWO + 32768
    bRSQ = bRB + 4096
    bX1B = bRSQ + 4096
    bTM = bX1B + 4096
    assert bTM + 2048 <= 73728
    fXT = fBIG + 1024
    fR = fXT + 4096
    fX1 = fR + 4096
    fT = fX1 + 4096
    assert fT + 1536 <= 15488

    ldwo = p.add("gpsimd", lambda e: e.dma_start(out=ab(bWO, 16384), in_=E["WoT"]),
        dma="wing", waits=[st["head_last_exp"][H - 1], st["pv"][NU - 1]])
    ot_r = E["OT"].rearrange("(c p) t -> p c t", p=128)
    xt_r = E["XT"].rearrange("(c p) t -> p c t", p=128)
    x1f_r = E["X1F"].rearrange("(c p) t -> p c t", p=128)
    x1b_r = E["X1B"].rearrange("(c p) t -> p c t", p=128)

    prev = {}
    x1_dmas = []
    for n in range(8):
        bufn = n % 2
        lob_w = list(ot_dmas) if n == 0 else ([prev[("womm", n - 2)]] if ("womm", n - 2) in prev else [])
        lob = p.add("sync", lambda e, n=n, bufn=bufn: e.dma_start(
            out=AB[:, bOB[bufn]:bOB[bufn] + 8192].rearrange("p (c t) -> p c t", c=16),
            in_=ot_r[:, :, 512 * n:512 * n + 512]), dma="win", waits=lob_w)
        lxt = p.add("sync", lambda e, n=n: e.dma_start(
            out=AFr[:, fXT:fXT + 4096].rearrange("p (c t) -> p c t", c=8),
            in_=xt_r[:, :, 512 * n:512 * n + 512]), dma="win",
            waits=[prev[("rev_last", n - 1)]] if ("rev_last", n - 1) in prev else [])
        revs = []
        for m in range(8):
            psl = (S1p[:, 0:512], S1p[:, 512:1024])[m % 2]
            mms = []
            for ec in range(16):
                w = []
                if ec == 0:
                    w = [lob, ldwo]
                    if ("rev", n, m - 2) in prev:
                        w.append(prev[("rev", n, m - 2)])
                    if m < 2 and ("rev", n - 1, 6 + m) in prev:
                        w.append(prev[("rev", n - 1, 6 + m)])
                mms.append(p.add(
                    "tensor",
                    lambda e, m=m, ec=ec, psl=psl, bufn=bufn: e.matmul(
                        psl, ab(bWO + 1024 * ec + 128 * m, 128), ab(bOB[bufn] + 512 * ec, 512),
                        start=(ec == 0), stop=(ec == 15)),
                    waits=w, tag=f"wo{n}.{m}"))
            prev[("womm", n)] = mms[-1]
            rev = p.add("vector", lambda e, m=m, psl=psl: e.tensor_tensor(
                af(fR + 512 * m, 512), psl, af(fXT + 512 * m, 512), ALU.add),
                waits=[mms[-1], lxt], tag=f"rev{n}.{m}")
            prev[("rev", n, m)] = rev
            revs.append(rev)
        rbfs, rsqs = [], []
        for m in range(8):
            rbfs.append(p.add("scalar", lambda e, m=m: e.copy(ab(bRB + 512 * m, 512), af(fR + 512 * m, 512)), waits=[revs[m]]))
            rsqs.append(p.add("scalar", lambda e, m=m: e.activation(ab(bRSQ + 512 * m, 512), af(fR + 512 * m, 512), AF.Square), waits=[revs[m]]))
        summ, sqmm = [], []
        for c in range(8):
            w1 = [rbfs[c], ms_ones]
            w2 = [rsqs[c]]
            if c == 0 and ("stat", n - 1) in prev:
                w1.append(prev[("stat", n - 1)])
                w2.append(prev[("stat", n - 1)])
            summ.append(p.add("tensor", lambda e, c=c: e.matmul(
                S2p[:, 0:512], ab(bONES, 128), ab(bRB + 512 * c, 512), start=(c == 0), stop=(c == 7)), waits=w1))
            sqmm.append(p.add("tensor", lambda e, c=c: e.matmul(
                S2p[:, 512:1024], ab(bONES, 128), ab(bRSQ + 512 * c, 512), start=(c == 0), stop=(c == 7)), waits=w2))
        mb = p.add("vector", lambda e: e.tensor_scalar(af(fT, 512), S2p[:, 0:512], 1.0 / D, None, ALU.mult), waits=[summ[-1]])
        t = p.add("vector", lambda e: e.tensor_scalar(af(fT + 512, 512), S2p[:, 512:1024], 1.0 / D, EPS, ALU.mult, ALU.add), waits=[sqmm[-1]])
        prev[("stat", n)] = t
        p.add("vector", lambda e: e.tensor_tensor(af(fT + 1024, 512), af(fT, 512), af(fT, 512), ALU.mult))
        var = p.add("vector", lambda e: e.tensor_tensor(af(fT + 1024, 512), af(fT + 512, 512), af(fT + 1024, 512), ALU.subtract))
        lnv = p.add("scalar", lambda e: e.activation(af(fT + 512, 512), af(fT + 1024, 512), AF.Ln), waits=[var])
        rstd = p.add("scalar", lambda e: e.activation(af(fT + 1024, 512), af(fT + 512, 512), AF.Exp, scale=-0.5), waits=[lnv])
        x1f_ops, x1b_ops = [], []
        for m in range(8):
            wv = [rstd]
            if ("x1dma", n - 1) in prev and m == 0:
                wv.append(prev[("x1dma", n - 1)])
            p.add("vector", lambda e, m=m: e.tensor_tensor(af(fX1 + 512 * m, 512), af(fR + 512 * m, 512), af(fT, 512), ALU.subtract), waits=wv)
            x1c = p.add("vector", lambda e, m=m: e.tensor_tensor(af(fX1 + 512 * m, 512), af(fX1 + 512 * m, 512), af(fT + 1024, 512), ALU.mult))
            wb = [x1c]
            if m == 0 and ("x1dmab", n - 1) in prev:
                wb.append(prev[("x1dmab", n - 1)])
            x1b = p.add("scalar", lambda e, m=m: e.copy(ab(bX1B + 512 * m, 512), af(fX1 + 512 * m, 512)), waits=wb)
            x1f_ops.append(x1c)
            x1b_ops.append(x1b)
        rl = p.add("vector", lambda e: e.tensor_copy(af(fLAM, 1), af(fLAM, 1)))
        prev[("rev_last", n)] = rl
        d1 = p.add("sync", lambda e, n=n: e.dma_start(
            out=x1f_r[:, :, 512 * n:512 * n + 512],
            in_=AFr[:, fX1:fX1 + 4096].rearrange("p (c t) -> p c t", c=8)),
            waits=x1f_ops, dma="out")
        d2 = p.add("sync", lambda e, n=n: e.dma_start(
            out=x1b_r[:, :, 512 * n:512 * n + 512],
            in_=AB[:, bX1B:bX1B + 4096].rearrange("p (c t) -> p c t", c=8)),
            waits=x1b_ops, dma="out")
        prev[("x1dma", n)] = d1
        prev[("x1dmab", n)] = d2
        x1_dmas += [d1, d2]
        for tp in range(4):
            dstp = (trp0, trp1)[tp % 2]
            trsb = []
            for c in range(8):
                w = [x1b_ops[c]]
                if ("x1tmev", n, tp - 2) in prev:
                    w.append(prev[("x1tmev", n, tp - 2)])
                elif tp < 2 and ("x1tmev", n - 1, tp + 2) in prev:
                    w.append(prev[("x1tmev", n - 1, tp + 2)])
                trsb.append(p.add("tensor", lambda e, c=c, tp=tp, dstp=dstp: e.transpose(
                    dstp[:, 128 * c:128 * c + 128], ab(bX1B + 512 * c + 128 * tp, 128), ab(bIDENT, 128)),
                    waits=w))
            w2 = [trsb[-1]]
            if ("x1tmdma", n, tp - 2) in prev:
                w2.append(prev[("x1tmdma", n, tp - 2)])
            elif tp < 2 and ("x1tmdma", n - 1, tp + 2) in prev:
                w2.append(prev[("x1tmdma", n - 1, tp + 2)])
            tev = p.add("vector", lambda e, tp=tp, dstp=dstp: e.tensor_copy(
                ab(bTM + 1024 * (tp % 2), 1024), dstp[:, :]), waits=w2)
            prev[("x1tmev", n, tp)] = tev
            dtm = p.add("sync", lambda e, n=n, tp=tp: e.dma_start(
                out=E["X1TM"][512 * n + 128 * tp:512 * n + 128 * tp + 128, :],
                in_=ab(bTM + 1024 * (tp % 2), 1024)), waits=[tev], dma="out")
            prev[("x1tmdma", n, tp)] = dtm
            x1_dmas.append(dtm)

    if "stop_b" in dbg:
        p.add("sync", lambda e: e.nop(), waits=x1_dmas)
        return

    C = 1024
    CC = 2048
    bXB = bBIG
    bSH = bXB + 8192
    bINT = bSH + 8192
    bDWN = bINT + 8192
    bWG = [bDWN + 8192, bDWN + 20480, bDWN + 32768]
    bG = bDWN + 45056
    bRW = bG + 512
    bSHF = bRW + 128
    bZ = bXB
    bXO = bINT + 4096
    assert bSHF + 1024 <= 73728
    fLG = fBIG + 1024
    fLGT = fLG + 1024
    fPRB = fLGT + 128
    fPM = fPRB + 112
    fMB = fPM + 16
    fCW = fMB + 16
    fMET = fCW + 64
    fROFF = fMET + 4
    fMETA = fROFF + 16
    fSC = fMETA + 32
    fX1Q = fSC + 2048
    fR2 = fX1Q + 1024
    fCMB = fR2 + 1024
    fLNS = fCMB + 1024
    fRAC = fLNS + 1024
    fT2 = fRAC + 1024
    fGB = fT2 + 64
    fSLC = fGB + 1024
    fLN2 = [fLNS, fSLC + 16]
    assert fSLC + 16 + 1024 <= 15232

    ldrw = p.add("gpsimd", lambda e: e.dma_start(out=ab(bRW, 128), in_=E["rw"]), dma="wing",
        waits=[prev[("womm", 7)]])
    ld_lt = p.add("sync", lambda e: e.dma_start(out=af(fLTRI, 128), in_=E["ltri"]), dma="win",
        waits=[prev[("x1dma", 7)], prev[("x1dmab", 7)]])

    msz = p.add("gpsimd", lambda e: e.memset(ab(bZ, 4096), 0.0), waits=[prev[("womm", 7)]])
    zdmas = []
    for blk in range(64):
        zdmas.append(p.add("sync", lambda e, blk=blk: e.dma_start(
            out=E["XCOMP"][512 * blk:512 * (blk + 1), :].rearrange("(b p) c -> p b c", p=128),
            in_=AB[:, bZ:bZ + 4096].rearrange("p (b c) -> p b c", b=4)), waits=[msz], dma="out"))
    zro = p.add("vector", lambda e: e.memset(af(fROFF, NE, 0, 1), 0.0))

    y_dmas = []
    cp = {}
    sc_dmas = []
    sh_dmas = []
    x1b_r2 = x1b_r
    for Q in range(4):
        tq = 1024 * Q
        lxb = p.add("sync", lambda e, tq=tq: e.dma_start(
            out=AB[:, bXB:bXB + 8192].rearrange("p (c t) -> p c t", c=8),
            in_=x1b_r2[:, :, tq:tq + 1024]), dma="win",
            waits=([prev[("x1dmab", n)] for n in range(8)] + zdmas if Q == 0 else [cp[("shdma", Q - 1)]]))
        lgevs = []
        for tt in range(2):
            mms = []
            for c in range(8):
                w = []
                if c == 0:
                    w = [lxb, ldrw]
                    if tt == 1:
                        w.append(lgevs[0])
                    elif ("ovp_free",) in cp:
                        w.append(cp[("ovp_free",)])
                    elif Q == 0:
                        w.append(st["epi"].get(NU - 1))
                mms.append(p.add("tensor", lambda e, tt=tt, c=c: e.matmul(
                    ovp[0:16, 0:512], ab(bRW + 16 * c, 16, 0, 128), ab(bXB + 1024 * c + 512 * tt, 512),
                    start=(c == 0), stop=(c == 7)), waits=w))
            lgevs.append(p.add("vector", lambda e, tt=tt: e.tensor_copy(af(fLG + 512 * tt, 512, 0, 16), ovp[0:16, 0:512]), waits=[mms[-1]]))
        trls = []
        for j in range(8):
            w = [lgevs[-1]] if j == 0 else []
            trls.append(p.add("tensor", lambda e, j=j: e.transpose(
                ovp[:, 16 * j:16 * j + 16], af(fLG + 128 * j, 128, 0, 16), af(fIDENT, 16, 0, 16)), waits=w))
        lgt = p.add("vector", lambda e: e.tensor_copy(af(fLGT, 128), ovp[:, 0:128]), waits=[trls[-1]])
        for j in range(8):
            g = Q * 8 + j
            p.add("vector", lambda e, j=j: e.reduce_max(af(fPRB, 1), af(fLGT + 16 * j, 16), AX.X))
            nm1 = p.add("vector", lambda e: e.tensor_scalar(af(fPRB + 5, 1), af(fPRB, 1), -1.0, None, ALU.mult))
            ex = p.add("scalar", lambda e, j=j: e.activation(af(fPRB + 16, 16), af(fLGT + 16 * j, 16), AF.Exp, bias=af(fPRB + 5, 1), accum_out=af(fPRB + 1, 1)), waits=[nm1])
            p.add("vector", lambda e: e.reciprocal(af(fPRB + 2, 1), af(fPRB + 1, 1)), waits=[ex])
            p.add("vector", lambda e: e.tensor_scalar(af(fPRB + 32, 16), af(fPRB + 16, 16), af(fPRB + 2, 1), None, ALU.mult))
            p.add("vector", lambda e: e.reduce_max(af(fPRB + 3, 1), af(fPRB + 32, 16), AX.X))
            eqo = p.add("vector", lambda e: e.tensor_scalar(af(fPRB + 48, 16), af(fPRB + 32, 16), af(fPRB + 3, 1), None, ALU.is_equal))
            p.add("vector", lambda e: e.tensor_tensor(af(fPRB + 64, 16), af(fPRB + 32, 16), af(fPRB + 48, 16), ALU.mult))
            p.add("vector", lambda e: e.tensor_tensor(af(fPRB + 64, 16), af(fPRB + 32, 16), af(fPRB + 64, 16), ALU.subtract))
            p.add("vector", lambda e: e.reduce_max(af(fPRB + 4, 1), af(fPRB + 64, 16), AX.X))
            geo = p.add("vector", lambda e: e.tensor_scalar(af(fMB, 16), af(fPRB + 32, 16), af(fPRB + 4, 1), None, ALU.is_ge))
            w0 = [geo, ld_lt, ld_on, ms_onc]
            if ("cmp_mm",) in cp:
                w0.append(cp[("cmp_mm",)])
            mmro = p.add("tensor", lambda e: e.matmul(ovp[:, 0:16], af(fONES, NE * 0 + 128, 0, 1), af(fROFF, NE, 0, 1), start=True, stop=False), waits=w0 + [zro])
            mmpf = p.add("tensor", lambda e: e.matmul(ovp[:, 0:16], af(fLTRI, 128), af(fMB, 16), start=False, stop=True))
            mmtot = p.add("tensor", lambda e: e.matmul(ovp[0:1, 16:32], af(fONC, 1), af(fMB, 16), start=True, stop=True))
            evpm = p.add("vector", lambda e: e.tensor_copy(af(fPM, 16), ovp[:, 0:16]), waits=[mmpf])
            roff = p.add("vector", lambda e: e.tensor_tensor(af(fROFF, NE, 0, 1), af(fROFF, NE, 0, 1), ovp[0:1, 16:32], ALU.add), waits=[mmtot])
            cp[("cmp_mm",)] = roff
            scs = []
            for rk, (mname, moff, cvoff) in enumerate((("A", fPRB + 48, fPRB + 3), ("B", fPRB + 64, fPRB + 4))):
                if rk == 1:
                    p.add("vector", lambda e: e.tensor_tensor(af(fPRB + 64, 16), af(fMB, 16), af(fPRB + 48, 16), ALU.subtract))
                p.add("vector", lambda e, moff=moff: e.tensor_tensor(af(fCW, 16), af(moff, 16), af(fEIDB, 16), ALU.mult), waits=[ev_ei])
                p.add("vector", lambda e: e.tensor_reduce(af(fCW + 32, 1), af(fCW, 16), AX.X, ALU.add))
                p.add("vector", lambda e, moff=moff: e.tensor_tensor(af(fCW + 16, 16), af(moff, 16), af(fPM, 16), ALU.mult), waits=[evpm])
                p.add("vector", lambda e: e.tensor_reduce(af(fCW + 33, 1), af(fCW + 16, 16), AX.X, ALU.add))
                p.add("vector", lambda e: e.tensor_scalar(af(fCW + 33, 1), af(fCW + 33, 1), float(CC - 1), None, ALU.min))
                slf = p.add("vector", lambda e: e.tensor_scalar(af(fCW + 34, 1), af(fCW + 32, 1), float(CC), af(fCW + 33, 1), ALU.mult, ALU.add))
                wsli = [slf]
                if ("scx", g - 1, rk) in cp:
                    wsli.append(cp[("scx", g - 1, rk)])
                sli = p.add("vector", lambda e, rk=rk: e.tensor_copy(AI[:, rk:rk + 1], af(fCW + 34, 1)), waits=wsli)
                wmv = [cp[("slcv", g - 1)]] if rk == 0 and ("slcv", g - 1) in cp else []
                p.add("vector", lambda e, rk=rk: e.tensor_copy(af(fMET + 2 * rk, 1), af(fCW + 34, 1)), waits=wmv)
                mcv = p.add("vector", lambda e, rk=rk, cvoff=cvoff: e.tensor_copy(af(fMET + 2 * rk + 1, 1), af(cvoff, 1)))
                wsc = [sli] + (zdmas if g == 0 else [])
                if ("scx", g - 2, rk) in cp:
                    wsc.append(cp[("scx", g - 2, rk)])
                if rk == 0:
                    wst = [prev[("x1tmdma", g // 4, g % 4)]]
                    if ("scx", g - 2, 1) in cp:
                        wst.append(cp[("scx", g - 2, 1)])
                    cp[("xst", g)] = p.add("sync", lambda e, g=g: e.dma_start(
                        out=ab(bDWN + 1024 * (g % 2), 1024),
                        in_=E["X1TM"][g * 128:(g + 1) * 128, :]), waits=wst, dma="win")
                if SER_SC and ("ser_sc",) in cp:
                    wsc.append(cp[("ser_sc",)])
                scx = p.add("gpsimd", lambda e, rk=rk, g=g: e.indirect_dma_start(
                    E["XCOMP"], bass.IndirectOffsetOnAxis(AI[:, rk:rk + 1], 0),
                    ab(bDWN + 1024 * (g % 2), 1024), None), waits=wsc + [cp[("xst", g)]], dma="wsc")
                if SER_SC:
                    cp[("ser_sc",)] = scx
                cp[("scx", g, rk)] = scx
                scs.append(scx)
            slcv = p.add("sync", lambda e, g=g: e.dma_start(
                out=E["SLCV"][g * 128:(g + 1) * 128, :], in_=AFr[:, fMET:fMET + 4]),
                waits=[mcv], dma="out")
            cp[("slcv", g)] = slcv
            sc_dmas += scs
            if "stop_cmp0" in dbg and g == 0:
                dws = scs + [roff]
                dd = []
                dd.append(p.add("sync", lambda e: e.dma_start(out=E["DBG_CMP"][:, 0:112], in_=af(fPRB, 112)), waits=dws, dma="out"))
                dd.append(p.add("sync", lambda e: e.dma_start(out=E["DBG_CMP"][:, 112:128], in_=af(fPM, 16)), waits=dws, dma="out"))
                dd.append(p.add("sync", lambda e: e.dma_start(out=E["DBG_CMP"][:, 128:144], in_=af(fMB, 16)), waits=dws, dma="out"))
                dd.append(p.add("sync", lambda e: e.dma_start(out=E["DBG_CMP"][:, 144:208], in_=af(fCW, 64)), waits=dws, dma="out"))
                dd.append(p.add("sync", lambda e: e.dma_start(out=E["DBG_CMP"][0:1, 208:224], in_=af(fROFF, NE, 0, 1)), waits=dws, dma="out"))
                dd.append(p.add("sync", lambda e: e.dma_start(out=E["DBG_AI"][:, 0:2], in_=AI[:, 0:2]), waits=dws, dma="out"))
                dd.append(p.add("sync", lambda e: e.dma_start(out=E["DBG_CMP"][:, 224:240], in_=af(fLTRI, 16)), waits=dws, dma="out"))
                dd.append(p.add("sync", lambda e: e.dma_start(out=E["DBG_CMP"][:, 240:241], in_=AFr[:, fLTRI + 127:fLTRI + 128], allow_slow_non_contiguous=True), waits=dws, dma="out"))
                dd.append(p.add("sync", lambda e: e.dma_start(out=E["DBG_CMP"][0:1, 241:242], in_=af(fONC, 1, 0, 1)), waits=dws, dma="out"))
                dd.append(p.add("sync", lambda e: e.dma_start(out=E["DBG_CMP"][0:1, 242:250], in_=af(fONES, 8, 0, 1)), waits=dws, dma="out"))
                p.add("sync", lambda e: e.nop(), waits=dd)
                return
        lsg = p.add("gpsimd", lambda e: e.dma_start(out=ab(bWG[0], 8192), in_=E["sgw"]), dma="wing",
            waits=([ldrw, prev[("x1dmab", 7)], prev[("x1dma", 7)]] if Q == 0 else [cp[("shdn_last",)]]))
        lsu = p.add("gpsimd", lambda e: e.dma_start(out=ab(bWG[1], 8192), in_=E["suw"]), dma="wing")
        last_sil = cp.get(("lnb_pe_done",))
        last_mi = cp.get(("sh_mi_last",))
        gu_done = []
        for fm in range(8):
            for tt in range(2):
                gm = []
                for c in range(8):
                    w = []
                    if c == 0:
                        w = [lsg, lxb]
                        if last_sil is not None:
                            w.append(last_sil)
                    gm.append(p.add("tensor", lambda e, fm=fm, tt=tt, c=c: e.matmul(
                        S1p[:, 0:512], ab(bWG[0] + 1024 * c + 128 * fm, 128), ab(bXB + 1024 * c + 512 * tt, 512),
                        start=(c == 0), stop=(c == 7)), waits=w))
                sil = p.add("scalar", lambda e: e.activation(ab(bG, 512), S1p[:, 0:512], AF.Silu), waits=[gm[-1], (last_mi if last_mi is not None else None)], tag=f"sil{Q}.{fm}.{tt}")
                last_sil = sil
                um = []
                for c in range(8):
                    w = []
                    if c == 0:
                        w = [lsu]
                        if last_mi is not None:
                            w.append(last_mi)
                    um.append(p.add("tensor", lambda e, fm=fm, tt=tt, c=c: e.matmul(
                        S1p[:, 512:1024], ab(bWG[1] + 1024 * c + 128 * fm, 128), ab(bXB + 1024 * c + 512 * tt, 512),
                        start=(c == 0), stop=(c == 7)), waits=w))
                mi = p.add("vector", lambda e, fm=fm, tt=tt: e.tensor_tensor(
                    ab(bINT + 1024 * fm + 512 * tt, 512), ab(bG, 512), S1p[:, 512:1024], ALU.mult),
                    waits=[um[-1], sil], tag=f"smi{Q}.{fm}.{tt}")
                last_mi = mi
                gu_done.append(mi)
        cp[("sh_mi_last",)] = last_mi
        lsd = p.add("gpsimd", lambda e: e.dma_start(out=ab(bWG[0], 8192), in_=E["sdw"]), dma="wing", waits=[gu_done[-1], last_sil])
        last_rc = [cp.get(("lnb_pe_done",)), cp.get(("lnb_pe_done",))]
        dn_ops = []
        for dm in range(8):
            for tt in range(2):
                bank = tt
                psl = (S2p[:, 0:512], S2p[:, 512:1024])[bank]
                mm = []
                for c in range(8):
                    w = []
                    if c == 0:
                        w = [lsd] + gu_done[tt::2]
                        if last_rc[bank] is not None:
                            w.append(last_rc[bank])
                    mm.append(p.add("tensor", lambda e, dm=dm, tt=tt, c=c, psl=psl: e.matmul(
                        psl, ab(bWG[0] + 1024 * c + 128 * dm, 128), ab(bINT + 1024 * c + 512 * tt, 512),
                        start=(c == 0), stop=(c == 7)), waits=w))
                w2 = [mm[-1]]
                if ("shdma", Q - 1) in cp and dm == 0 and tt == 0:
                    w2.append(cp[("shdma", Q - 1)])
                rc = p.add("vector", lambda e, dm=dm, tt=tt, psl=psl: e.tensor_copy(
                    ab(bSH + 1024 * dm + 512 * tt, 512), psl), waits=w2, tag=f"sracc{Q}.{dm}.{tt}")
                last_rc[bank] = rc
                dn_ops.append(rc)
        cp[("shdn_last",)] = dn_ops[-1]
        cp[("lnb_pe_done",)] = dn_ops[-1]
        shd = p.add("sync", lambda e, tq=tq: e.dma_start(
            out=E["SHFM"].rearrange("(c p) t -> p c t", p=128)[:, :, tq:tq + 1024],
            in_=AB[:, bSH:bSH + 8192].rearrange("p (c t) -> p c t", c=8)), waits=dn_ops, dma="out")
        sh_dmas.append(shd)
        cp[("shdma", Q)] = shd
        cp[("ovp_free",)] = cp[("cmp_mm",)]

    for ei in range(NE * 2):
        wi = ei // 2
        wbuf = (wi + 2) % 3
        if ei % 2 == 0:
            if wi == 0:
                wld_w = [sh_dmas[-1]]
            elif wi == 1:
                wld_w = [cp[("shdn_last",)]]
            elif wi == 2:
                wld_w = [cp[("sh_mi_last",)]]
            else:
                wld_w = [cp[("emi_last", ei - 3)], cp[("edn_mm_last", ei - 3)]]
            lg_ = p.add("gpsimd", lambda e, wi=wi, wbuf=wbuf: e.dma_start(
                out=ab(bWG[wbuf], 4096), in_=E["gw"][wi]), dma="wing", waits=wld_w)
            lu_ = p.add("gpsimd", lambda e, wi=wi, wbuf=wbuf: e.dma_start(
                out=ab(bWG[wbuf] + 4096, 4096), in_=E["uw"][wi]), dma="wing")
            ld_ = p.add("gpsimd", lambda e, wi=wi, wbuf=wbuf: e.dma_start(
                out=ab(bWG[wbuf] + 8192, 4096), in_=E["dw"][wi]), dma="wing")
        wload = list(sc_dmas) if ei == 0 else [cp[("etr_last", ei - 1)]]
        lxc = []
        for t in range(8):
            lxc.append(p.add("sync", lambda e, ei=ei, t=t: e.dma_start(
                out=ab(bXB + 1024 * t, 1024),
                in_=E["XCOMP"][ei * C + 128 * t:ei * C + 128 * (t + 1), :]),
                waits=wload if t == 0 else [], dma="win"))
        xtrev = []
        for c in range(8):
            dstp = (trp0, trp1)[c % 2]
            trs = []
            for t in range(8):
                w = [lxc[t]]
                if t == 0:
                    if c >= 2:
                        w.append(cp[("xtrev", ei, c - 2)])
                    elif ("odtrev", ei - 1, c + 6) in cp:
                        w.append(cp[("odtrev", ei - 1, c + 6)])
                    elif ei == 0:
                        w.append(st["trev_last"][c % 2])
                trs.append(p.add("tensor", lambda e, c=c, t=t, dstp=dstp: e.transpose(
                    dstp[:, 128 * t:128 * t + 128], ab(bXB + 1024 * t + 128 * c, 128), ab(bIDENT, 128)), waits=w))
            w2 = [trs[-1]]
            if ("xgate", ei - 1) in cp and c < 2:
                w2.append(cp[("xgate", ei - 1)])
            if ei == 0:
                w2.append(sh_dmas[-1])
            tev = p.add("vector", lambda e, c=c, dstp=dstp: e.tensor_copy(ab(bSH + 1024 * c, 1024), dstp[:, :]), waits=w2)
            cp[("xtrev", ei, c)] = tev
            xtrev.append(tev)
        cp[("etr_last", ei)] = xtrev[-1]
        last_esil = cp.get(("esil_last",))
        last_emi = cp.get(("emi_bank_last",))
        emi = []
        for fm in range(4):
            for tt in range(2):
                gm = []
                for c in range(8):
                    w = []
                    if c == 0:
                        w = [lg_, xtrev[c]] if fm == 0 and tt == 0 else [lg_]
                        w += [x for x in (last_esil,) if x is not None]
                        if fm == 0 and tt == 0:
                            w += xtrev
                    gm.append(p.add("tensor", lambda e, fm=fm, tt=tt, c=c, wbuf=wbuf: e.matmul(
                        S1p[:, 0:512], ab(bWG[wbuf] + 512 * c + 128 * fm, 128), ab(bSH + 1024 * c + 512 * tt, 512),
                        start=(c == 0), stop=(c == 7)), waits=w))
                cp[("xgate", ei)] = gm[-1]
                sil = p.add("scalar", lambda e: e.activation(ab(bG, 512), S1p[:, 0:512], AF.Silu),
                            waits=[gm[-1]] + ([last_emi] if last_emi is not None else []))
                last_esil = sil
                um = []
                for c in range(8):
                    w = []
                    if c == 0:
                        w = [lu_]
                        if last_emi is not None:
                            w.append(last_emi)
                    um.append(p.add("tensor", lambda e, fm=fm, tt=tt, c=c, wbuf=wbuf: e.matmul(
                        S1p[:, 512:1024], ab(bWG[wbuf] + 4096 + 512 * c + 128 * fm, 128), ab(bSH + 1024 * c + 512 * tt, 512),
                        start=(c == 0), stop=(c == 7)), waits=w))
                mi1 = p.add("vector", lambda e, fm=fm, tt=tt: e.tensor_tensor(
                    ab(bINT + 1024 * fm + 512 * tt, 512), ab(bG, 512), S1p[:, 512:1024], ALU.mult),
                    waits=[um[-1], sil])
                last_emi = mi1
                emi.append(mi1)
        cp[("esil_last",)] = last_esil
        cp[("emi_bank_last",)] = last_emi
        cp[("emi_last", ei)] = emi[-1]
        last_rc2 = cp.get(("edn_rc_last",), [None, None])
        edn = []
        for dm in range(8):
            for tt in range(2):
                bank = tt
                psl = (S2p[:, 0:512], S2p[:, 512:1024])[bank]
                mm = []
                for c in range(4):
                    w = []
                    if c == 0:
                        w = [ld_] + emi[tt::2]
                        if last_rc2[bank] is not None:
                            w.append(last_rc2[bank])
                    mm.append(p.add("tensor", lambda e, dm=dm, tt=tt, c=c, psl=psl, wbuf=wbuf: e.matmul(
                        psl, ab(bWG[wbuf] + 8192 + 1024 * c + 128 * dm, 128), ab(bINT + 1024 * c + 512 * tt, 512),
                        start=(c == 0), stop=(c == 3)), waits=w))
                w2 = [mm[-1]]
                if ("odtr_last", ei - 1) in cp and dm == 0 and tt == 0:
                    w2.append(cp[("odtr_last", ei - 1)])
                rc = p.add("vector", lambda e, dm=dm, tt=tt, psl=psl: e.tensor_copy(
                    ab(bDWN + 1024 * dm + 512 * tt, 512), psl), waits=w2)
                last_rc2[bank] = rc
                edn.append(rc)
        cp[("edn_rc_last",)] = last_rc2
        cp[("edn_mm_last", ei)] = mm[-1]
        oglast = None
        for t in range(8):
            dstp = (trp0, trp1)[t % 2]
            trs = []
            for dm in range(8):
                w = [edn[2 * dm + (t // 4)]]
                if dm == 0 and ("odtrev", ei, t - 2) in cp:
                    w.append(cp[("odtrev", ei, t - 2)])
                elif dm == 0 and t < 2:
                    w.append(cp[("xtrev", ei, t + 6)])
                trs.append(p.add("tensor", lambda e, t=t, dm=dm, dstp=dstp: e.transpose(
                    dstp[:, 128 * dm:128 * dm + 128], ab(bDWN + 1024 * dm + 128 * t, 128), ab(bIDENT, 128)), waits=w))
            slot = t % 2
            wev = [trs[-1]]
            if ("xod", ei, t - 2) in cp:
                wev.append(cp[("xod", ei, t - 2)])
            elif t < 2 and ("xod", ei - 1, t + 6) in cp:
                wev.append(cp[("xod", ei - 1, t + 6)])
            oev = p.add("vector", lambda e, t=t, slot=slot, dstp=dstp: e.tensor_copy(
                ab(bXO + 1024 * slot, 1024), dstp[:, :]), waits=wev)
            cp[("odtrev", ei, t)] = oev
            xod = p.add("sync", lambda e, ei=ei, t=t, slot=slot: e.dma_start(
                out=E["XOUT"][ei * C + 128 * t:ei * C + 128 * (t + 1), :],
                in_=ab(bXO + 1024 * slot, 1024)), waits=[oev], dma="out")
            cp[("xod", ei, t)] = xod
            oglast = xod
        cp[("odtr_last", ei)] = cp[("odtrev", ei, 7)]
        cp[("escat_last", ei)] = oglast

    all_scats = [cp[("escat_last", e_)] for e_ in range(NE * 2)] + [cp[("xod", NE * 2 - 1, t)] for t in range(8)]
    slcv_all = [cp[("slcv", gg)] for gg in range(32)]
    bGT4 = [bXO, bXO + 1024, bDWN, bDWN + 1024]
    x1f_r2 = x1f_r
    shf_r = E["SHFM"].rearrange("(c p) t -> p c t", p=128)

    def gat_prep(g):
        fSL = fSLC + 8 * (g % 2)
        aco = 2 * (g % 2)
        wl = list(slcv_all) if g == 0 else []
        if ("gmulB", g - 2) in cp:
            wl.append(cp[("gmulB", g - 2)])
        lsl = p.add("sync", lambda e, g=g, fSL=fSL: e.dma_start(
            out=AFr[:, fSL:fSL + 4], in_=E["SLCV"][g * 128:(g + 1) * 128, :]), waits=wl, dma="win")
        for rk in range(2):
            wa = [lsl]
            if ("gat", g - 2, rk) in cp:
                wa.append(cp[("gat", g - 2, rk)])
            elif g == 0:
                wa.append(cp[("scx", 31, rk)])
            aic = p.add("vector", lambda e, rk=rk, fSL=fSL, aco=aco: e.tensor_copy(
                AI[:, aco + rk:aco + rk + 1], af(fSL + 2 * rk, 1)), waits=wa)
            gslot = bGT4[2 * (g % 2) + rk]
            wg_ = [aic] + (all_scats if g == 0 else [])
            if ("gmul", g - 2, rk) in cp:
                wg_.append(cp[("gmul", g - 2, rk)])
            cp[("gat", g, rk)] = p.add("gpsimd", lambda e, rk=rk, gslot=gslot, aco=aco: e.indirect_dma_start(
                AB[:, gslot:gslot + 1024], None,
                E["XOUT"], bass.IndirectOffsetOnAxis(AI[:, aco + rk:aco + rk + 1], 0)),
                waits=wg_, dma="wsc")

    gat_prep(0)
    for g in range(32):
        if g + 1 < 32:
            gat_prep(g + 1)
        fSL = fSLC + 8 * (g % 2)
        lx1 = p.add("sync", lambda e, g=g: e.dma_start(
            out=AFr[:, fX1Q:fX1Q + 1024].rearrange("p (c t) -> p c t", c=8),
            in_=x1f_r2[:, :, 128 * g:128 * g + 128]), dma="win",
            waits=[cp[("ydma", g - 2)]] if ("ydma", g - 2) in cp else [])
        lsh = p.add("sync", lambda e, g=g: e.dma_start(
            out=AB[:, bSHF:bSHF + 1024].rearrange("p (c t) -> p c t", c=8),
            in_=shf_r[:, :, 128 * g:128 * g + 128]), dma="win",
            waits=sh_dmas if g == 0 else [])
        cA = p.add("scalar", lambda e, g=g, fSL=fSL: e.activation(
            af(fRAC, 1024), AB[:, bGT4[2 * (g % 2)]:bGT4[2 * (g % 2)] + 1024], AF.Copy, scale=af(fSL + 1, 1)),
            waits=[cp[("gat", g, 0)], cp.get(("ra1_prev",))])
        cB = p.add("scalar", lambda e, g=g, fSL=fSL: e.activation(
            af(fGB, 1024), AB[:, bGT4[2 * (g % 2) + 1]:bGT4[2 * (g % 2) + 1] + 1024], AF.Copy, scale=af(fSL + 3, 1)),
            waits=[cp[("gat", g, 1)], cp.get(("rb_prev",))])
        cp[("gmul", g, 0)] = cA
        cp[("gmul", g, 1)] = cB
        cp[("gmulB", g)] = cB
        cmb = p.add("vector", lambda e: e.tensor_tensor(af(fCMB, 1024), af(fX1Q, 1024), ab(bSHF, 1024), ALU.add),
                    waits=[lx1, lsh, cp.get(("yts_last",))], tag=f"cmb{g}")
        radd = []
        for half in range(2):
            trs = []
            for c4 in range(4):
                c = 4 * half + c4
                w = [cmb]
                if half == 0 and c4 == 0 and ("yev", g - 1) in cp:
                    w.append(cp[("yev", g - 1)])
                if c4 == 0 and half == 1:
                    w.append(radd[0])
                trs.append(p.add("tensor", lambda e, c=c, c4=c4: e.transpose(
                    ovp[:, 128 * c4:128 * c4 + 128], af(fCMB + 128 * c, 128), af(fIDENT, 128)), waits=w))
            ra = p.add("vector", lambda e, half=half: e.tensor_tensor(
                af(fR2 + 512 * half, 512), ovp[:, 0:512], af(fRAC + 512 * half, 512), ALU.add),
                waits=[trs[-1], cA], tag=f"ra{g}.{half}")
            radd.append(ra)
        cp[("ra1_prev",)] = radd[-1]
        rb = p.add("vector", lambda e: e.tensor_tensor(af(fR2, 1024), af(fR2, 1024), af(fGB, 1024), ALU.add),
                   waits=[radd[-1], cB])
        cp[("rb_prev",)] = rb
        cp[("yev", g)] = rb
        fLO = fLN2[g % 2]
        s1 = p.add("scalar", lambda e, fLO=fLO: e.activation(af(fLO, 1024), af(fR2, 1024), AF.Identity, accum_out=af(fT2, 1)),
                   waits=[rb] + ([cp[("ydma", g - 2)]] if ("ydma", g - 2) in cp else []))
        s2 = p.add("scalar", lambda e, fLO=fLO: e.activation(af(fLO, 1024), af(fR2, 1024), AF.Square, accum_out=af(fT2 + 1, 1)))
        mb2 = p.add("vector", lambda e: e.tensor_scalar(af(fT2 + 2, 1), af(fT2, 1), 1.0 / D, None, ALU.mult), waits=[s1, s2])
        tb2 = p.add("vector", lambda e: e.tensor_scalar(af(fT2 + 3, 1), af(fT2 + 1, 1), 1.0 / D, EPS, ALU.mult, ALU.add))
        m2b = p.add("vector", lambda e: e.tensor_tensor(af(fT2 + 4, 1), af(fT2 + 2, 1), af(fT2 + 2, 1), ALU.mult))
        vrb = p.add("vector", lambda e: e.tensor_tensor(af(fT2 + 4, 1), af(fT2 + 3, 1), af(fT2 + 4, 1), ALU.subtract))
        lnb2 = p.add("scalar", lambda e: e.activation(af(fT2 + 5, 1), af(fT2 + 4, 1), AF.Ln), waits=[vrb])
        rsb2 = p.add("scalar", lambda e: e.activation(af(fT2 + 6, 1), af(fT2 + 5, 1), AF.Exp, scale=-0.5))
        yts = p.add("vector", lambda e, fLO=fLO: e.tensor_scalar(af(fLO, 1024), af(fR2, 1024), af(fT2 + 2, 1), af(fT2 + 6, 1), ALU.subtract, ALU.mult), waits=[rsb2], tag=f"yts{g}")
        cp[("yts_last",)] = yts
        yd = p.add("sync", lambda e, g=g, fLO=fLO: e.dma_start(
            out=E["YT"][128 * g:128 * (g + 1), :], in_=af(fLO, 1024)), waits=[yts], dma="out")
        cp[("ydma", g)] = yd
        y_dmas.append(yd)

    p.add("sync", lambda e: e.nop(), waits=y_dmas)


def swz(m):
    m = np.asarray(m, dtype=np.float32)
    c = m.shape[0] // 128
    return np.ascontiguousarray(m.reshape(c, 128, m.shape[1]).transpose(1, 0, 2).reshape(128, -1))


def make_inputs_for_core(inputs, b):
    f32 = lambda x: np.ascontiguousarray(np.asarray(x, dtype=np.float32))
    d = {}
    d["XT"] = f32(np.asarray(inputs["L0_attn_X"][b]).T)
    d["WqT"] = swz(np.asarray(inputs["L0_attn_Wq"]).T)
    d["WkT"] = swz(np.asarray(inputs["L0_attn_Wk"]).T)
    d["WvT"] = swz(np.asarray(inputs["L0_attn_Wv"]).T)
    d["WoT"] = swz(np.asarray(inputs["L0_attn_Wo"]).T)
    d["lq1"] = f32(inputs["L0_attn_lambda_q1"])
    d["lk1"] = f32(inputs["L0_attn_lambda_k1"])
    d["lq2"] = f32(inputs["L0_attn_lambda_q2"])
    d["lk2"] = f32(inputs["L0_attn_lambda_k2"])
    d["rms_scale"] = f32(np.asarray(inputs["L0_attn_rms_scale"])[None, :])
    d["rw"] = swz(np.asarray(inputs["L0_ffn_router_w"]))
    d["gw"] = np.stack([swz(np.asarray(inputs["L0_ffn_gate_w"])[e]) for e in range(NE)])
    d["uw"] = np.stack([swz(np.asarray(inputs["L0_ffn_up_w"])[e]) for e in range(NE)])
    d["dw"] = np.stack([swz(np.asarray(inputs["L0_ffn_down_w"])[e]) for e in range(NE)])
    d["sgw"] = swz(np.asarray(inputs["L0_ffn_shared_gate_w"]))
    d["suw"] = swz(np.asarray(inputs["L0_ffn_shared_up_w"]))
    d["sdw"] = swz(np.asarray(inputs["L0_ffn_shared_down_w"]))
    d["identf"] = np.eye(128, dtype=np.float32)
    d["onesf"] = np.ones((1, 128), dtype=np.float32)
    d["iotap"] = np.arange(128, dtype=np.float32).reshape(128, 1)
    d["eids"] = np.arange(NE, dtype=np.float32).reshape(1, NE)
    d["ltri"] = np.triu(np.ones((128, 128), dtype=np.float32), 1)
    return d


def kernel(**inputs):
    nc = build_program()
    in_maps = [make_inputs_for_core(inputs, b) for b in range(8)]
    res = run_bass_kernel_spmd(nc, in_maps, core_ids=list(range(8)))
    out = np.stack([np.asarray(r["YT"]) for r in res.results], axis=0)
    return np.ascontiguousarray(out).astype(np.float32)


# version 17
# speedup vs baseline: 1.2563x; 1.0169x over previous
import math
import numpy as np
import ml_dtypes
import concourse.bass as bass
import concourse.mybir as mybir
from concourse.bass_utils import run_bass_kernel_spmd

F32 = mybir.dt.float32
BF16 = mybir.dt.bfloat16
AF = mybir.ActivationFunctionType
ALU = mybir.AluOpType
AX = mybir.AxisListType

T = 4096
D = 1024
H = 8
DV = 256
NE = 16
FE = 512
FS = 1024
SCALE = 1.0 / math.sqrt(128.0)
LAMBDA_INIT = 0.8 - 0.6 * math.exp(-0.3)
EPS = 1e-5

ENGINES = ("tensor", "scalar", "vector", "gpsimd", "sync")
SER_SC = False


DMA_POOL = 8


class Op:
    __slots__ = ("eng", "fn", "waits", "dma", "marked", "sem", "val", "pre", "tag")

    def __init__(self, eng, fn, waits, dma, tag):
        self.eng, self.fn, self.dma, self.tag = eng, fn, dma, tag
        self.waits = [w for w in waits if w is not None]
        self.marked = False
        self.sem = None
        self.val = None
        self.pre = None


class Prog:
    def __init__(self):
        self.ops = []

    def add(self, eng, fn, waits=(), dma=None, tag=""):
        op = Op(eng, fn, waits, dma, tag)
        self.ops.append(op)
        return op

    def resolve(self, nc, ctx):
        groups = sorted({op.dma for op in self.ops if op.dma is not None})
        sems = {}
        for e in ENGINES:
            sems["e_" + e] = ctx.enter_context(nc.semaphore("e_" + e))
        pool = {}
        for g in groups:
            pool[g] = [ctx.enter_context(nc.semaphore(f"d_{g}{i}")) for i in range(DMA_POOL)]
        CHAIN = ("vector", "scalar", "gpsimd")
        for op in self.ops:
            kept = []
            for w in op.waits:
                if w.dma is None and w.eng == op.eng and op.dma is None:
                    continue
                w.marked = True
                kept.append(w)
            op.waits = kept
        for op in self.ops:
            if op.eng in CHAIN and op.dma is None:
                op.marked = True
        counts = {("e_" + e): 0 for e in ENGINES}
        rr = {g: 0 for g in groups}
        dma_counts = {}
        for op in self.ops:
            if op.dma is not None:
                op.marked = True
                i = rr[op.dma]
                rr[op.dma] = (i + 1) % DMA_POOL
                sem = pool[op.dma][i]
                prevv = dma_counts.get(id(sem), 0)
                if prevv:
                    op.pre = (sem, prevv)
                op.sem = sem
                op.val = prevv + 16
                dma_counts[id(sem)] = op.val
            elif op.marked:
                cnt = counts["e_" + op.eng]
                if op.eng in CHAIN and cnt > 0:
                    op.pre = (sems["e_" + op.eng], cnt)
                counts["e_" + op.eng] = cnt + 1
                op.sem = sems["e_" + op.eng]
                op.val = cnt + 1
            elif op.eng in CHAIN:
                pass
        self.sems = sems
        self.drain_waits = []
        seen = set()
        for op in self.ops:
            if op.dma is not None and id(op.sem) not in seen:
                seen.add(id(op.sem))
        finals = {}
        for op in self.ops:
            if op.dma is not None:
                finals[id(op.sem)] = (op.sem, op.val)
        self.drain_waits = list(finals.values())

    def emit(self, eng_name, eng):
        watermark = {}
        for op in self.ops:
            if op.eng != eng_name:
                continue
            pend = list(op.waits)
            for w in pend:
                key = id(w.sem)
                if watermark.get(key, -1) >= w.val:
                    continue
                eng.wait_ge(w.sem, w.val)
                watermark[key] = w.val
            if op.pre is not None:
                sem, v = op.pre
                if watermark.get(id(sem), -1) < v:
                    eng.wait_ge(sem, v)
                    watermark[id(sem)] = v
            inst = op.fn(eng)
            if op.marked:
                inst.then_inc(op.sem, 16 if op.dma is not None else 1)
        if eng_name == "sync":
            for sem, v in self.drain_waits:
                if watermark.get(id(sem), -1) < v:
                    eng.wait_ge(sem, v)


def build_program(dbg=()):
    nc = bass.Bass("TRN2", target_bir_lowering=False)
    dt = nc.dram_tensor
    E = {}

    for n, s in (("XT", [D, T]), ("WqT", [128, 16384]), ("WkT", [128, 16384]),
                 ("WvT", [128, 16384]), ("WoT", [128, 16384]),
                 ("lq1", [H, 128]), ("lk1", [H, 128]), ("lq2", [H, 128]), ("lk2", [H, 128]),
                 ("rms_scale", [1, DV]), ("rw", [128, 128]),
                 ("gw", [NE, 128, 4096]), ("uw", [NE, 128, 4096]), ("dw", [NE, 128, 4096]),
                 ("sgw", [128, 8192]), ("suw", [128, 8192]), ("sdw", [128, 8192]),
                 ("identf", [128, 128]), ("onesf", [1, 128])):
        E[n] = dt(n, s, F32, kind="ExternalInput").ap()
    E["iotap"] = dt("iotap", [128, 1], F32, kind="ExternalInput").ap()
    E["eids"] = dt("eids", [1, NE], F32, kind="ExternalInput").ap()
    E["ltri"] = dt("ltri", [128, 128], F32, kind="ExternalInput").ap()
    E["YT"] = dt("YT", [T, D], F32, kind="ExternalOutput").ap()

    def scratch(name, shape, dtype):
        kind = "ExternalOutput" if name in dbg else "Internal"
        return dt(name, shape, dtype, kind=kind).ap()

    if "stop_cmp0" in dbg:
        E["DBG_CMP"] = dt("DBG_CMP", [128, 256], F32, kind="ExternalOutput").ap()
        E["DBG_AI"] = dt("DBG_AI", [128, 32], mybir.dt.int32, kind="ExternalOutput").ap()
    if "probe_u0" in dbg:
        E["DBG_P"] = dt("DBG_P", [128, 4096], BF16, kind="ExternalOutput").ap()
        E["DBG_E2"] = dt("DBG_E2", [128, 4096], BF16, kind="ExternalOutput").ap()
        E["DBG_DEN"] = dt("DBG_DEN", [128, 32], F32, kind="ExternalOutput").ap()
        E["DBG_PT"] = dt("DBG_PT", [128, 4096], BF16, kind="ExternalOutput").ap()
        E["DBG_ON"] = dt("DBG_ON", [128, 256], F32, kind="ExternalOutput").ap()
        E["DBG_OTM"] = dt("DBG_OTM", [128, 256], BF16, kind="ExternalOutput").ap()
        E["DBG_Q"] = dt("DBG_Q", [128, 8192], BF16, kind="ExternalOutput").ap()
        E["DBG_K"] = dt("DBG_K", [128, 8192], BF16, kind="ExternalOutput").ap()
        E["DBG_V"] = dt("DBG_V", [128, 8192], BF16, kind="ExternalOutput").ap()
    E["QTh"] = scratch("QTh", [2048, T], BF16)
    E["KTh"] = scratch("KTh", [2048, T], BF16)
    E["VH"] = scratch("VH", [T, 2048], BF16)
    E["OT"] = scratch("OT", [2048, T], BF16)
    E["X1F"] = scratch("X1F", [D, T], F32)
    E["X1B"] = scratch("X1B", [D, T], BF16)
    E["X1TM"] = scratch("X1TM", [T, D], BF16)
    E["XCOMP"] = scratch("XCOMP", [NE * 2048, D], BF16)
    E["XOUT"] = scratch("XOUT", [NE * 2048, D], BF16)
    E["SLCV"] = scratch("SLCV", [T, 4], F32)
    E["SHFM"] = scratch("SHFM", [D, T], BF16)

    from contextlib import ExitStack
    with ExitStack() as ctx:
        E["AB"] = ctx.enter_context(nc.sbuf_tensor("AB", [128, 73728], BF16))
        E["AFr"] = ctx.enter_context(nc.sbuf_tensor("AFr", [128, 15488], F32))
        E["AI"] = ctx.enter_context(nc.sbuf_tensor("AI", [128, 32], mybir.dt.int32))
        E["S1p"] = ctx.enter_context(nc.psum_tensor("S1p", [128, 1024], F32))
        E["S2p"] = ctx.enter_context(nc.psum_tensor("S2p", [128, 1024], F32))
        E["trp0"] = ctx.enter_context(nc.psum_tensor("trp0", [128, 1024], BF16))
        E["trp1"] = ctx.enter_context(nc.psum_tensor("trp1", [128, 1024], BF16))
        E["ovp"] = ctx.enter_context(nc.psum_tensor("ovp", [128, 512], F32))
        E["pob"] = ctx.enter_context(nc.psum_tensor("pob", [128, 1024], BF16))

        p = Prog()
        emit_all(p, E, dbg)
        p.resolve(nc, ctx)

        with nc.Block() as block:
            @block.tensor
            def _(e):
                p.emit("tensor", e)

            @block.scalar
            def _(e):
                p.emit("scalar", e)

            @block.vector
            def _(e):
                p.emit("vector", e)

            @block.gpsimd
            def _(e):
                p.emit("gpsimd", e)

            @block.sync
            def _(e):
                p.emit("sync", e)
    return nc


def emit_all(p, E, dbg):
    AB, AFr, AI = E["AB"], E["AFr"], E["AI"]
    S1p, S2p, trp0, trp1, ovp, pob = E["S1p"], E["S2p"], E["trp0"], E["trp1"], E["ovp"], E["pob"]

    def ab(off, n, p0=0, np_=128):
        return AB[p0:p0 + np_, off:off + n]

    def af(off, n, p0=0, np_=128):
        return AFr[p0:p0 + np_, off:off + n]

    fIDENT, fONES, fRMSB, fLAM = 0, 128, 256, 512
    fBIG = 640
    bIDENT, bONES, bSEL = 0, 128, 256
    bBIG = bSEL + 2048

    fIOTA, fONC, fEIDS, fEIDB = 520, 521, 524, 544
    fLTRI = 15232
    ld_id = p.add("sync", lambda e: e.dma_start(out=af(fIDENT, 128), in_=E["identf"]), dma="win")
    ld_on = p.add("sync", lambda e: e.dma_start(out=af(fONES, 128, 0, 1), in_=E["onesf"]), dma="win")
    ld_io = p.add("sync", lambda e: e.dma_start(out=af(fIOTA, 1), in_=E["iotap"]), dma="win")
    ld_ei = p.add("sync", lambda e: e.dma_start(out=af(fEIDS, NE, 0, 1), in_=E["eids"]), dma="win")
    ld_idb = p.add("gpsimd", lambda e: e.dma_start(out=ab(bIDENT, 128), in_=E["identf"]), dma="wing")
    ms_ones = p.add("gpsimd", lambda e: e.memset(ab(bONES, 128), 1.0))
    ms_onc = p.add("vector", lambda e: e.memset(af(fONC, 1), 1.0))
    mm_ei = p.add("tensor", lambda e: e.matmul(ovp[:, 272:288], af(fONES, 128, 0, 1), af(fEIDS, NE, 0, 1), start=True, stop=True), waits=[ld_on, ld_ei])
    ev_ei = p.add("vector", lambda e: e.tensor_copy(af(fEIDB, NE), ovp[:, 272:288]), waits=[mm_ei])

    fSM = fBIG
    ld_rs = p.add("sync", lambda e: e.dma_start(out=af(fSM, DV, 0, 1), in_=E["rms_scale"]), dma="win")
    mm_rs = p.add("tensor", lambda e: e.matmul(ovp[:, 0:256], af(fONES, 128, 0, 1), af(fSM, DV, 0, 1), start=True, stop=True), waits=[ld_on, ld_rs])
    ev_rs = p.add("vector", lambda e: e.tensor_scalar(af(fRMSB, DV), ovp[:, 0:256], 1.0 - LAMBDA_INIT, None, ALU.mult), waits=[mm_rs])

    fL = fBIG + 256
    fLs = fL + 512
    ldl = []
    for i, t in enumerate(("lq1", "lk1", "lq2", "lk2")):
        ldl.append(p.add("sync", lambda e, t=t, i=i: e.dma_start(out=af(fL + 128 * i, 128, 0, 8), in_=E[t]), dma="win"))
    p.add("vector", lambda e: e.tensor_tensor(af(fL, 128, 0, 8), af(fL, 128, 0, 8), af(fL + 128, 128, 0, 8), ALU.mult), waits=ldl[:2])
    p.add("vector", lambda e: e.tensor_tensor(af(fL + 256, 128, 0, 8), af(fL + 256, 128, 0, 8), af(fL + 384, 128, 0, 8), ALU.mult), waits=ldl[2:])
    rs1 = p.add("vector", lambda e: e.tensor_reduce(af(fLs, 1, 0, 8), af(fL, 128, 0, 8), AX.X, ALU.add))
    rs2 = p.add("vector", lambda e: e.tensor_reduce(af(fLs + 1, 1, 0, 8), af(fL + 256, 128, 0, 8), AX.X, ALU.add))
    ex12 = p.add("scalar", lambda e: e.activation(af(fLs + 2, 2, 0, 8), af(fLs, 2, 0, 8), AF.Exp), waits=[rs1, rs2])
    p.add("vector", lambda e: e.tensor_tensor(af(fLs + 4, 1, 0, 8), af(fLs + 3, 1, 0, 8), af(fLs + 2, 1, 0, 8), ALU.subtract), waits=[ex12])
    nl2 = p.add("vector", lambda e: e.tensor_scalar(af(fLs + 5, 1, 0, 8), af(fLs + 4, 1, 0, 8), -LAMBDA_INIT, None, ALU.add))
    trl = p.add("tensor", lambda e: e.transpose(ovp[0:1, 256:264], af(fLs + 5, 1, 0, 8), af(fIDENT, 8, 0, 8)), waits=[nl2, ev_rs, ld_id])
    evl = p.add("vector", lambda e: e.tensor_copy(af(fLs + 6, 8, 0, 1), ovp[0:1, 256:264]), waits=[trl])
    mml = p.add("tensor", lambda e: e.matmul(ovp[:, 264:272], af(fONES, 128, 0, 1), af(fLs + 6, 8, 0, 1), start=True, stop=True), waits=[evl])
    nlam = p.add("vector", lambda e: e.tensor_copy(af(fLAM, 8), ovp[:, 264:272]), waits=[mml])

    bXTB = bBIG
    bW = bXTB + 32768
    bW2 = bW + 16384
    bSTG = bW2 + 16384
    assert bSTG + 4096 <= 73728

    xchunks = [p.add("gpsimd",
                     lambda e, c=c: e.dma_start(out=ab(bXTB + 4096 * c, 4096), in_=E["XT"][c * 128:(c + 1) * 128, :]),
                     dma="wing") for c in range(8)]
    ldwq = p.add("gpsimd", lambda e: e.dma_start(out=ab(bW, 16384), in_=E["WqT"]), dma="wing")
    ldwk = p.add("gpsimd", lambda e: e.dma_start(out=ab(bW2, 16384), in_=E["WkT"]), dma="wing")

    vh_r = E["VH"].rearrange("(t p) o -> p t o", p=128)

    def proj_fm(dst, woff, wait_w, tag, stg_prev=None, ev_seed=None):
        dmas = []
        prev_stg = stg_prev
        prev_ev = dict(ev_seed) if ev_seed else {}
        for m in range(16):
            evs = []
            for n in range(8):
                bank = (m * 8 + n) % 2
                psl = (S1p[:, 0:512], S1p[:, 512:1024])[bank]
                mms = []
                for c in range(8):
                    w = []
                    if c == 0:
                        w = [wait_w] + xchunks
                        if prev_ev.get(bank) is not None:
                            w.append(prev_ev[bank])
                    mms.append(p.add(
                        "tensor",
                        lambda e, m=m, n=n, c=c, psl=psl, woff=woff: e.matmul(
                            psl, ab(woff + 2048 * c + 128 * m, 128),
                            ab(bXTB + 4096 * c + 512 * n, 512),
                            start=(c == 0), stop=(c == 7)),
                        waits=w, tag=f"{tag}mm"))
                w2 = [mms[-1]]
                if n == 0:
                    if dmas:
                        w2.append(dmas[-1])
                    elif prev_stg is not None:
                        w2.append(prev_stg)
                ev = p.add("scalar", lambda e, n=n, psl=psl: e.copy(ab(bSTG + 512 * n, 512), psl), waits=w2)
                prev_ev[bank] = ev
                evs.append(ev)
            dmas.append(p.add(
                "sync",
                lambda e, m=m, dst=dst: e.dma_start(out=dst[m * 128:(m + 1) * 128, :], in_=ab(bSTG, 4096)),
                waits=evs, dma="out", tag=f"{tag}dma"))
        return dmas, prev_ev

    qdmas, qev = proj_fm(E["QTh"], bW, ldwq, "q")
    ldwv = p.add("gpsimd", lambda e: e.dma_start(out=ab(bW, 16384), in_=E["WvT"]), waits=[qdmas[-1]], dma="wing")
    kdmas, _ = proj_fm(E["KTh"], bW2, ldwk, "k", stg_prev=qdmas[-1], ev_seed=qev)

    vdmas = []
    prev_ev = {}
    for m in range(32):
        evs = []
        for gi in range(4):
            bank = (m * 4 + gi) % 2
            psl = (S2p[:, 0:512], S2p[:, 512:1024])[bank]
            mms = []
            for c in range(8):
                w = []
                if c == 0:
                    w = [ldwv]
                    if prev_ev.get(bank) is not None:
                        w.append(prev_ev[bank])
                mms.append(p.add(
                    "tensor",
                    lambda e, m=m, gi=gi, c=c, psl=psl: e.matmul(
                        psl, ab(bXTB + 4096 * c + 128 * m, 128),
                        ab(bW + 2048 * c + 512 * gi, 512),
                        start=(c == 0), stop=(c == 7)),
                    waits=w, tag="vmm"))
            w2 = [mms[-1]]
            if gi == 0:
                w2.append(vdmas[-1] if vdmas else kdmas[-1])
            ev = p.add("scalar", lambda e, gi=gi, psl=psl: e.copy(ab(bSTG + 512 * gi, 512), psl), waits=w2)
            prev_ev[bank] = ev
            evs.append(ev)
        vdmas.append(p.add(
            "sync",
            lambda e, m=m: e.dma_start(out=vh_r[:, m, :], in_=ab(bSTG, 2048)),
            waits=evs, dma="out"))

    if "stop_a0" in dbg:
        p.add("sync", lambda e: e.nop(), waits=qdmas + kdmas + vdmas)
        return

    bQ = bBIG
    bK = [bBIG + 8192, bBIG + 16384]
    bV = [bBIG + 24576, bBIG + 32768]
    bE1 = [bBIG + 40960, bBIG + 45056, bBIG + 49152]
    bE2 = [bBIG + 53248, bBIG + 57344]
    bPT = bBIG + 61440
    bOST = bPT + 4096
    bOTM = bOST + 1024
    assert bOTM + 512 <= 73728

    fD1 = [fBIG + 1024, fBIG + 1040]
    fD2 = [fBIG + 1028, fBIG + 1044]
    fDEN = [fBIG + 1056, fBIG + 1072, fBIG + 1088]
    fON = [fBIG + 1100, fBIG + 1100 + 256]
    fSQ = fBIG + 1100 + 512

    qt_r = E["QTh"].rearrange("(c p) t -> p c t", p=128)
    kt_r = E["KTh"].rearrange("(c p) t -> p c t", p=128)

    def load_head(h, buf, waits):
        lq = p.add("sync", lambda e, h=h: e.dma_start(
            out=AB[:, bQ:bQ + 8192].rearrange("p (c t) -> p c t", c=2),
            in_=qt_r[:, 2 * h:2 * h + 2, :]), dma="win",
            waits=waits + (qdmas if h == 0 else []))
        lk = p.add("sync", lambda e, h=h, buf=buf: e.dma_start(
            out=AB[:, bK[buf]:bK[buf] + 8192].rearrange("p (c t) -> p c t", c=2),
            in_=kt_r[:, 2 * h:2 * h + 2, :]), dma="win",
            waits=(kdmas if h == 0 else []))
        lv = p.add("sync", lambda e, h=h, buf=buf: e.dma_start(
            out=AB[:, bV[buf]:bV[buf] + 8192].rearrange("p (t o) -> p t o", t=32),
            in_=vh_r[:, :, 256 * h:256 * h + 256]), dma="win",
            waits=(vdmas if h == 0 else []))
        return [lq, lk, lv]

    NU = H * 32
    st = {
        "loads": {0: load_head(0, 0, [])},
        "exp_last": [None, None],
        "comb": {}, "exps": {}, "dens": {}, "tr_last": {}, "pv": {},
        "epi": {}, "sq": {}, "otm": {}, "otr": {}, "ose": {}, "otdma": {},
        "head_last_exp": {},
        "trev_last": [None, None],
    }
    ot_dmas = []

    def emit_head_prefetch(u):
        h, qt = u // 32, u % 32
        if qt == 0 and h >= 1:
            lq = p.add("sync", lambda e, h=h: e.dma_start(
                out=AB[:, bQ:bQ + 8192].rearrange("p (c t) -> p c t", c=2),
                in_=qt_r[:, 2 * h:2 * h + 2, :]), dma="win",
                waits=[st["head_last_exp"][h - 1]])
            st["loads"][h].append(lq)
        if qt == 2 and h + 1 < H:
            w = []
            if h >= 1:
                w.append(st["pv"][(h - 1) * 32 + 31])
            lk = p.add("sync", lambda e, h1=h + 1, buf=1 - (h % 2): e.dma_start(
                out=AB[:, bK[buf]:bK[buf] + 8192].rearrange("p (c t) -> p c t", c=2),
                in_=kt_r[:, 2 * h1:2 * h1 + 2, :]), dma="win",
                waits=w)
            lv = p.add("sync", lambda e, h1=h + 1, buf=1 - (h % 2): e.dma_start(
                out=AB[:, bV[buf]:bV[buf] + 8192].rearrange("p (t o) -> p t o", t=32),
                in_=vh_r[:, :, 256 * h1:256 * h1 + 256]), dma="win")
            st["loads"][h + 1] = [lk, lv]


    def emit_S_span(u, j):
        h, qt = u // 32, u % 32
        buf = h % 2
        sl = u % 2
        loads = st["loads"][h]
        oQ, oK = bQ, bK[buf]
        exps = st["exps"].setdefault(u, {1: [], 2: []})
        for br, Sp, qoff, koff, bEo, fDx in (
                (1, S1p, oQ, oK, bE1[u % 3], fD1[sl]),
                (2, S2p, oQ + 4096, oK + 4096, bE2[sl], fD2[sl])):
            mms = []
            for half in range(2):
                kb = 2 * j + half
                w = []
                if kb == 0:
                    w = list(loads)
                    if st["exp_last"][br - 1] is not None:
                        w.append(st["exp_last"][br - 1])
                elif half == 0:
                    w.append(exps[br][j - 1])
                mms.append(p.add(
                    "tensor",
                    lambda e, qt=qt, kb=kb, Sp=Sp, qoff=qoff, koff=koff, half=half: e.matmul(
                        Sp[:, 512 * half:512 * half + 512],
                        ab(qoff + 128 * qt, 128), ab(koff + 512 * kb, 512),
                        start=True, stop=True),
                    waits=w, tag=f"s{br}u{u}"))
            ew = list(mms)
            if j == 0:
                if br == 1:
                    if (u - 3) in st["tr_last"]:
                        ew.append(st["tr_last"][u - 3])
                    if (u - 2) in st["dens"]:
                        ew.append(st["dens"][u - 2][0])
                if br == 2 and (u - 2) in st["comb"]:
                    ew.append(st["comb"][u - 2])
            exps[br].append(p.add(
                "scalar",
                lambda e, j=j, Sp=Sp, bEo=bEo, fDx=fDx: e.activation(
                    ab(bEo + 1024 * j, 1024), Sp[:, :], AF.Exp, scale=SCALE,
                    accum_out=af(fDx + j, 1)),
                waits=ew, tag=f"e{br}u{u}"))
            if j == 3:
                st["exp_last"][br - 1] = exps[br][-1]
                if qt == 31 and br == 2:
                    st["head_last_exp"][h] = exps[2][-1]

    def emit_TR(u, ktg_lo, ktg_hi):
        sl = u % 3
        c2 = st["comb"][u]
        for ktg in range(ktg_lo, ktg_hi + 1):
            dst = (trp0, trp1)[ktg % 2]
            trs = []
            for i in range(8):
                kt = 8 * ktg + i
                w = []
                if i == 0:
                    w = [c2, ld_idb]
                    if st["trev_last"][ktg % 2] is not None:
                        w.append(st["trev_last"][ktg % 2])
                trs.append(p.add(
                    "tensor",
                    lambda e, kt=kt, dst=dst, sl=sl: e.transpose(
                        dst[:, 128 * (kt % 8):128 * (kt % 8) + 128],
                        ab(bE1[sl] + 128 * kt, 128), ab(bIDENT, 128)),
                    waits=w, tag=f"tru{u}"))
            w2 = [trs[-1]]
            if (u - 1) in st["pv"]:
                w2.append(st["pv"][u - 1])
            tev = p.add(
                "vector",
                lambda e, ktg=ktg, dst=dst: e.tensor_copy(ab(bPT + 1024 * ktg, 1024), dst[:, :]),
                waits=w2, tag=f"trevu{u}")
            st["trev_last"][ktg % 2] = tev
            st.setdefault("trevs", {}).setdefault(u, []).append(tev)
            if ktg == 3:
                st["tr_last"][u] = trs[-1]

    def emit_PV(u, half):
        h = u // 32
        buf = h % 2
        sl = u % 2
        oV = bV[buf]
        trevs = st["trevs"][u]
        pvs = st.setdefault("pvs", {}).setdefault(u, [])
        for kt in range(16 * half, 16 * half + 16):
            w = []
            if kt % 8 == 0:
                w.append(trevs[kt // 8])
            if kt == 0:
                if (u - 1) in st["epi"]:
                    w.append(st["epi"][u - 1])
                if u == 0:
                    w += [nlam, ev_rs, ev_ei]
            pvs.append(p.add(
                "tensor",
                lambda e, kt=kt, sl=sl, oV=oV: e.matmul(
                    ovp[:, 256 * sl:256 * sl + 256],
                    ab(bPT + 128 * kt, 128), ab(oV + 256 * kt, 256),
                    start=(kt == 0), stop=(kt == 31)),
                waits=w, tag=f"pvu{u}"))
        if half == 1:
            st["pv"][u] = pvs[-1]

    def emit_epilogue(u):
        sl = u % 2
        fDx = fDEN[u % 3]
        fo = fON[sl]
        ms, r1 = st["dens"][u]
        ew = [st["pv"][u], r1]
        if (u - 2) in st["sq"]:
            ew.append(st["sq"][u - 2])
        epi = p.add("vector", lambda e, sl=sl, fo=fo, fDx=fDx: e.tensor_scalar(
            af(fo, 256), ovp[:, 256 * sl:256 * sl + 256], af(fDx + 4, 1), None, ALU.mult),
            waits=ew, tag=f"epiu{u}")
        st["epi"][u] = epi
        sqm = p.add("vector", lambda e, fo=fo: e.tensor_tensor(
            af(fSQ, 256), af(fo, 256), af(fo, 256), ALU.mult), tag=f"squ{u}")
        st["sq"][u] = sqm
        t2 = p.add("vector", lambda e, fDx=fDx: e.tensor_scalar(
            af(fSQ, 256), af(fSQ, 256), 1.0 / DV, EPS, ALU.mult, ALU.add,
            accum_out=af(fDx + 7, 1)), tag=f"t2u{u}")
        ln_ = p.add("scalar", lambda e, fDx=fDx: e.activation(af(fDx + 8, 1), af(fDx + 7, 1), AF.Ln), waits=[t2])
        rinv = p.add("scalar", lambda e, fDx=fDx: e.activation(af(fDx + 9, 1), af(fDx + 8, 1), AF.Exp, scale=-0.5), tag=f"rinvu{u}")
        p.add("vector", lambda e, fo=fo, fDx=fDx: e.tensor_scalar(af(fo, 256), af(fo, 256), af(fDx + 9, 1), None, ALU.mult), waits=[rinv])
        otm_w = [ev_rs]
        if (u - 2) in st["otr"]:
            otm_w.append(st["otr"][u - 2][-1])
        otm = p.add("vector", lambda e, sl=sl, fo=fo: e.tensor_tensor(
            ab(bOTM + 256 * sl, 256), af(fo, 256), af(fRMSB, 256), ALU.mult), waits=otm_w, tag=f"otmu{u}")
        st["otm"][u] = otm

    def emit_S_dve(u):
        h, qt = u // 32, u % 32
        sl = u % 2
        fDx = fDEN[u % 3]
        exps = st["exps"][u]
        p.add("vector", lambda e, sl=sl, fDx=fDx: e.tensor_reduce(
            af(fDx, 2), AFr[:, fD1[sl]:fD1[sl] + 8].rearrange("p (b x) -> p b x", b=2),
            AX.X, ALU.add), waits=exps[1] + exps[2])
        rd = p.add("vector", lambda e, fDx=fDx: e.reciprocal(af(fDx + 4, 2), af(fDx, 2)))
        p.add("vector", lambda e, fDx=fDx: e.tensor_tensor(af(fDx + 3, 1), af(fDx, 1), af(fDx + 5, 1), ALU.mult))
        ms = p.add("vector", lambda e, h=h, fDx=fDx: e.tensor_tensor(af(fDx + 6, 1), af(fDx + 3, 1), af(fLAM + h, 1), ALU.mult), waits=[nlam])
        st["dens"][u] = (ms, rd)
        c1 = p.add("vector", lambda e, sl=sl, fDx=fDx: e.tensor_scalar(ab(bE2[sl], 4096), ab(bE2[sl], 4096), af(fDx + 6, 1), None, ALU.mult), tag=f"c1u{u}")
        c2 = p.add("gpsimd", lambda e, sl=sl, s3=u % 3: e.tensor_tensor(ab(bE1[s3], 4096), ab(bE1[s3], 4096), ab(bE2[sl], 4096), ALU.add),
                   waits=[c1, exps[1][-1]], tag=f"c2u{u}")
        st["comb"][u] = c2

    def stage_OTR(u):
        h, qt = u // 32, u % 32
        sl = u % 2
        otrs = []
        for chv in range(2):
            w = [st["otm"][u]]
            if qt % 4 == 0 and chv == 0 and (u - 4) in st["ose"]:
                w += st["ose"][u - 4]
            otrs.append(p.add(
                "tensor",
                lambda e, qt=qt, chv=chv, sl=sl: e.transpose(
                    pob[:, 256 * (qt % 4) + 128 * chv:256 * (qt % 4) + 128 * chv + 128],
                    ab(bOTM + 256 * sl + 128 * chv, 128), ab(bIDENT, 128)),
                waits=w, tag=f"otru{u}"))
        st["otr"][u] = otrs
        if qt % 4 == 3:
            qb = u - 3
            grp = (qt - 3) // 4
            pob_v = pob[:, :].rearrange("p (q x) -> p q x", q=4)
            oses = []
            all_otrs = [o for i in range(4) for o in st["otr"][u - 3 + i]]
            for ch in range(2):
                w = list(all_otrs)
                prevd = st["otdma"].get((h, grp - 1), st["otdma"].get((h - 1, 7)))
                if prevd is not None:
                    w.append(prevd[ch])
                oses.append(p.add(
                    "vector",
                    lambda e, ch=ch: e.tensor_copy(
                        AB[:, bOST + 512 * ch:bOST + 512 * ch + 512].rearrange("p (q x) -> p q x", q=4),
                        pob_v[:, :, 128 * ch:128 * ch + 128]),
                    waits=w, tag=f"oseu{u}"))
            st["ose"][qb] = oses
            pair = []
            for ch in range(2):
                dma = p.add(
                    "sync",
                    lambda e, h=h, grp=grp, ch=ch: e.dma_start(
                        out=E["OT"][h * 256 + ch * 128:h * 256 + ch * 128 + 128, grp * 512:grp * 512 + 512],
                        in_=ab(bOST + 512 * ch, 512)),
                    waits=[oses[ch]], dma="out", tag=f"otdma{h}.{grp}")
                ot_dmas.append(dma)
                pair.append(dma)
            st["otdma"][(h, grp)] = pair

    nheads = H
    for d in dbg:
        if isinstance(d, str) and d.startswith("nh"):
            nheads = int(d[2:])
    NU = nheads * 32

    for u in range(NU + 4):
        cur = u < NU
        prv = 2 <= u <= NU + 1
        if cur:
            emit_head_prefetch(u)
            emit_S_span(u, 0)
        if 4 <= u <= NU + 3:
            stage_OTR(u - 4)
        if prv:
            emit_TR(u - 2, 0, 0)
        if cur:
            emit_S_span(u, 1)
        if prv:
            emit_TR(u - 2, 1, 2)
        if cur:
            emit_S_span(u, 2)
        if prv:
            emit_TR(u - 2, 3, 3)
            emit_PV(u - 2, 0)
        if cur:
            emit_S_span(u, 3)
        if prv:
            emit_PV(u - 2, 1)
            emit_epilogue(u - 2)
        if cur:
            emit_S_dve(u)

    if "stop_a1" in dbg:
        p.add("sync", lambda e: e.nop(), waits=ot_dmas)
        return

    bWO = bBIG
    bOB = [bWO + 16384, bWO + 24576]
    bRB = bWO + 32768
    bRSQ = bRB + 4096
    bX1B = bRSQ + 4096
    bTM = bX1B + 4096
    assert bTM + 2048 <= 73728
    fXT = fBIG + 1024
    fR = fXT + 4096
    fX1 = fR + 4096
    fT = fX1 + 4096
    assert fT + 1536 <= 15488

    ldwo = p.add("gpsimd", lambda e: e.dma_start(out=ab(bWO, 16384), in_=E["WoT"]),
        dma="wing", waits=[st["head_last_exp"][H - 1], st["pv"][NU - 1]])
    ot_r = E["OT"].rearrange("(c p) t -> p c t", p=128)
    xt_r = E["XT"].rearrange("(c p) t -> p c t", p=128)
    x1f_r = E["X1F"].rearrange("(c p) t -> p c t", p=128)
    x1b_r = E["X1B"].rearrange("(c p) t -> p c t", p=128)

    prev = {}
    x1_dmas = []
    for n in range(8):
        bufn = n % 2
        lob_w = list(ot_dmas) if n == 0 else ([prev[("womm", n - 2)]] if ("womm", n - 2) in prev else [])
        lob = p.add("sync", lambda e, n=n, bufn=bufn: e.dma_start(
            out=AB[:, bOB[bufn]:bOB[bufn] + 8192].rearrange("p (c t) -> p c t", c=16),
            in_=ot_r[:, :, 512 * n:512 * n + 512]), dma="win", waits=lob_w)
        lxt = p.add("sync", lambda e, n=n: e.dma_start(
            out=AFr[:, fXT:fXT + 4096].rearrange("p (c t) -> p c t", c=8),
            in_=xt_r[:, :, 512 * n:512 * n + 512]), dma="win",
            waits=[prev[("rev_last", n - 1)]] if ("rev_last", n - 1) in prev else [])
        revs = []
        for m in range(8):
            psl = (S1p[:, 0:512], S1p[:, 512:1024])[m % 2]
            mms = []
            for ec in range(16):
                w = []
                if ec == 0:
                    w = [lob, ldwo]
                    if ("rev", n, m - 2) in prev:
                        w.append(prev[("rev", n, m - 2)])
                    if m < 2 and ("rev", n - 1, 6 + m) in prev:
                        w.append(prev[("rev", n - 1, 6 + m)])
                mms.append(p.add(
                    "tensor",
                    lambda e, m=m, ec=ec, psl=psl, bufn=bufn: e.matmul(
                        psl, ab(bWO + 1024 * ec + 128 * m, 128), ab(bOB[bufn] + 512 * ec, 512),
                        start=(ec == 0), stop=(ec == 15)),
                    waits=w, tag=f"wo{n}.{m}"))
            prev[("womm", n)] = mms[-1]
            rev = p.add("vector", lambda e, m=m, psl=psl: e.tensor_tensor(
                af(fR + 512 * m, 512), psl, af(fXT + 512 * m, 512), ALU.add),
                waits=[mms[-1], lxt], tag=f"rev{n}.{m}")
            prev[("rev", n, m)] = rev
            revs.append(rev)
        rbfs, rsqs = [], []
        for m in range(8):
            rbfs.append(p.add("scalar", lambda e, m=m: e.copy(ab(bRB + 512 * m, 512), af(fR + 512 * m, 512)), waits=[revs[m]]))
            rsqs.append(p.add("scalar", lambda e, m=m: e.activation(ab(bRSQ + 512 * m, 512), af(fR + 512 * m, 512), AF.Square), waits=[revs[m]]))
        summ, sqmm = [], []
        for c in range(8):
            w1 = [rbfs[c], ms_ones]
            w2 = [rsqs[c]]
            if c == 0 and ("stat", n - 1) in prev:
                w1.append(prev[("stat", n - 1)])
                w2.append(prev[("stat", n - 1)])
            summ.append(p.add("tensor", lambda e, c=c: e.matmul(
                S2p[:, 0:512], ab(bONES, 128), ab(bRB + 512 * c, 512), start=(c == 0), stop=(c == 7)), waits=w1))
            sqmm.append(p.add("tensor", lambda e, c=c: e.matmul(
                S2p[:, 512:1024], ab(bONES, 128), ab(bRSQ + 512 * c, 512), start=(c == 0), stop=(c == 7)), waits=w2))
        mb = p.add("vector", lambda e: e.tensor_scalar(af(fT, 512), S2p[:, 0:512], 1.0 / D, None, ALU.mult), waits=[summ[-1]])
        t = p.add("vector", lambda e: e.tensor_scalar(af(fT + 512, 512), S2p[:, 512:1024], 1.0 / D, EPS, ALU.mult, ALU.add), waits=[sqmm[-1]])
        prev[("stat", n)] = t
        p.add("vector", lambda e: e.tensor_tensor(af(fT + 1024, 512), af(fT, 512), af(fT, 512), ALU.mult))
        var = p.add("vector", lambda e: e.tensor_tensor(af(fT + 1024, 512), af(fT + 512, 512), af(fT + 1024, 512), ALU.subtract))
        lnv = p.add("scalar", lambda e: e.activation(af(fT + 512, 512), af(fT + 1024, 512), AF.Ln), waits=[var])
        rstd = p.add("scalar", lambda e: e.activation(af(fT + 1024, 512), af(fT + 512, 512), AF.Exp, scale=-0.5), waits=[lnv])
        x1f_ops, x1b_ops = [], []
        for m in range(8):
            wv = [rstd]
            if ("x1dma", n - 1) in prev and m == 0:
                wv.append(prev[("x1dma", n - 1)])
            p.add("vector", lambda e, m=m: e.tensor_tensor(af(fX1 + 512 * m, 512), af(fR + 512 * m, 512), af(fT, 512), ALU.subtract), waits=wv)
            x1c = p.add("vector", lambda e, m=m: e.tensor_tensor(af(fX1 + 512 * m, 512), af(fX1 + 512 * m, 512), af(fT + 1024, 512), ALU.mult))
            wb = [x1c]
            if m == 0 and ("x1dmab", n - 1) in prev:
                wb.append(prev[("x1dmab", n - 1)])
            x1b = p.add("scalar", lambda e, m=m: e.copy(ab(bX1B + 512 * m, 512), af(fX1 + 512 * m, 512)), waits=wb)
            x1f_ops.append(x1c)
            x1b_ops.append(x1b)
        rl = p.add("vector", lambda e: e.tensor_copy(af(fLAM, 1), af(fLAM, 1)))
        prev[("rev_last", n)] = rl
        d1 = p.add("sync", lambda e, n=n: e.dma_start(
            out=x1f_r[:, :, 512 * n:512 * n + 512],
            in_=AFr[:, fX1:fX1 + 4096].rearrange("p (c t) -> p c t", c=8)),
            waits=x1f_ops, dma="out")
        d2 = p.add("sync", lambda e, n=n: e.dma_start(
            out=x1b_r[:, :, 512 * n:512 * n + 512],
            in_=AB[:, bX1B:bX1B + 4096].rearrange("p (c t) -> p c t", c=8)),
            waits=x1b_ops, dma="out")
        prev[("x1dma", n)] = d1
        prev[("x1dmab", n)] = d2
        x1_dmas += [d1, d2]
        for tp in range(4):
            dstp = (trp0, trp1)[tp % 2]
            trsb = []
            for c in range(8):
                w = [x1b_ops[c]]
                if ("x1tmev", n, tp - 2) in prev:
                    w.append(prev[("x1tmev", n, tp - 2)])
                elif tp < 2 and ("x1tmev", n - 1, tp + 2) in prev:
                    w.append(prev[("x1tmev", n - 1, tp + 2)])
                trsb.append(p.add("tensor", lambda e, c=c, tp=tp, dstp=dstp: e.transpose(
                    dstp[:, 128 * c:128 * c + 128], ab(bX1B + 512 * c + 128 * tp, 128), ab(bIDENT, 128)),
                    waits=w))
            w2 = [trsb[-1]]
            if ("x1tmdma", n, tp - 2) in prev:
                w2.append(prev[("x1tmdma", n, tp - 2)])
            elif tp < 2 and ("x1tmdma", n - 1, tp + 2) in prev:
                w2.append(prev[("x1tmdma", n - 1, tp + 2)])
            tev = p.add("vector", lambda e, tp=tp, dstp=dstp: e.tensor_copy(
                ab(bTM + 1024 * (tp % 2), 1024), dstp[:, :]), waits=w2)
            prev[("x1tmev", n, tp)] = tev
            dtm = p.add("sync", lambda e, n=n, tp=tp: e.dma_start(
                out=E["X1TM"][512 * n + 128 * tp:512 * n + 128 * tp + 128, :],
                in_=ab(bTM + 1024 * (tp % 2), 1024)), waits=[tev], dma="out")
            prev[("x1tmdma", n, tp)] = dtm
            x1_dmas.append(dtm)

    if "stop_b" in dbg:
        p.add("sync", lambda e: e.nop(), waits=x1_dmas)
        return

    C = 1024
    CC = 2048
    bXB = bBIG
    bSH = bXB + 8192
    bINT = bSH + 8192
    bDWN = bINT + 8192
    bWG = [bDWN + 8192, bDWN + 20480, bDWN + 32768]
    bG = bDWN + 45056
    bRW = bG + 512
    bSHF = bRW + 128
    bZ = bXB
    bXO = bINT + 4096
    assert bSHF + 1024 <= 73728
    fLG = fBIG + 1024
    fLGT = fLG + 1024
    fPRB = fLGT + 128
    fPM = fPRB + 112
    fMB = fPM + 16
    fCW = fMB + 16
    fMET = fCW + 64
    fROFF = fMET + 4
    fMETA = fROFF + 16
    fSC = fMETA + 32
    fX1Q = fSC + 2048
    fR2 = fX1Q + 1024
    fCMB = fR2 + 1024
    fLNS = fCMB + 1024
    fRAC = fLNS + 1024
    fT2 = fRAC + 1024
    fGB = fT2 + 64
    fSLC = fGB + 1024
    fLN2 = [fLNS, fSLC + 16]
    assert fSLC + 16 + 1024 <= 15232

    ldrw = p.add("gpsimd", lambda e: e.dma_start(out=ab(bRW, 128), in_=E["rw"]), dma="wing",
        waits=[prev[("womm", 7)]])
    ld_lt = p.add("sync", lambda e: e.dma_start(out=af(fLTRI, 128), in_=E["ltri"]), dma="win",
        waits=[prev[("x1dma", 7)], prev[("x1dmab", 7)]])

    msz = p.add("gpsimd", lambda e: e.memset(ab(bZ, 4096), 0.0), waits=[prev[("womm", 7)]])
    zdmas = []
    for blk in range(64):
        zdmas.append(p.add("sync", lambda e, blk=blk: e.dma_start(
            out=E["XCOMP"][512 * blk:512 * (blk + 1), :].rearrange("(b p) c -> p b c", p=128),
            in_=AB[:, bZ:bZ + 4096].rearrange("p (b c) -> p b c", b=4)), waits=[msz], dma="out"))
    zro = p.add("vector", lambda e: e.memset(af(fROFF, NE, 0, 1), 0.0))

    y_dmas = []
    cp = {}
    sc_dmas = []
    sh_dmas = []
    x1b_r2 = x1b_r
    for Q in range(4):
        tq = 1024 * Q
        lxb = p.add("sync", lambda e, tq=tq: e.dma_start(
            out=AB[:, bXB:bXB + 8192].rearrange("p (c t) -> p c t", c=8),
            in_=x1b_r2[:, :, tq:tq + 1024]), dma="win",
            waits=([prev[("x1dmab", n)] for n in range(8)] + zdmas if Q == 0 else [cp[("shdma", Q - 1)]]))
        lgevs = []
        for tt in range(2):
            mms = []
            for c in range(8):
                w = []
                if c == 0:
                    w = [lxb, ldrw]
                    if tt == 1:
                        w.append(lgevs[0])
                    elif ("ovp_free",) in cp:
                        w.append(cp[("ovp_free",)])
                    elif Q == 0:
                        w.append(st["epi"].get(NU - 1))
                mms.append(p.add("tensor", lambda e, tt=tt, c=c: e.matmul(
                    ovp[0:16, 0:512], ab(bRW + 16 * c, 16, 0, 128), ab(bXB + 1024 * c + 512 * tt, 512),
                    start=(c == 0), stop=(c == 7)), waits=w))
            lgevs.append(p.add("vector", lambda e, tt=tt: e.tensor_copy(af(fLG + 512 * tt, 512, 0, 16), ovp[0:16, 0:512]), waits=[mms[-1]]))
        trls = []
        for j in range(8):
            w = [lgevs[-1]] if j == 0 else []
            trls.append(p.add("tensor", lambda e, j=j: e.transpose(
                ovp[:, 16 * j:16 * j + 16], af(fLG + 128 * j, 128, 0, 16), af(fIDENT, 16, 0, 16)), waits=w))
        lgt = p.add("vector", lambda e: e.tensor_copy(af(fLGT, 128), ovp[:, 0:128]), waits=[trls[-1]])
        for j in range(8):
            g = Q * 8 + j
            p.add("vector", lambda e, j=j: e.reduce_max(af(fPRB, 1), af(fLGT + 16 * j, 16), AX.X))
            nm1 = p.add("vector", lambda e: e.tensor_scalar(af(fPRB + 5, 1), af(fPRB, 1), -1.0, None, ALU.mult))
            ex = p.add("scalar", lambda e, j=j: e.activation(af(fPRB + 16, 16), af(fLGT + 16 * j, 16), AF.Exp, bias=af(fPRB + 5, 1), accum_out=af(fPRB + 1, 1)), waits=[nm1])
            p.add("vector", lambda e: e.reciprocal(af(fPRB + 2, 1), af(fPRB + 1, 1)), waits=[ex])
            p.add("vector", lambda e: e.tensor_scalar(af(fPRB + 32, 16), af(fPRB + 16, 16), af(fPRB + 2, 1), None, ALU.mult))
            p.add("vector", lambda e: e.reduce_max(af(fPRB + 3, 1), af(fPRB + 32, 16), AX.X))
            eqo = p.add("vector", lambda e: e.tensor_scalar(af(fPRB + 48, 16), af(fPRB + 32, 16), af(fPRB + 3, 1), None, ALU.is_equal))
            p.add("vector", lambda e: e.tensor_tensor(af(fPRB + 64, 16), af(fPRB + 32, 16), af(fPRB + 48, 16), ALU.mult))
            p.add("vector", lambda e: e.tensor_tensor(af(fPRB + 64, 16), af(fPRB + 32, 16), af(fPRB + 64, 16), ALU.subtract))
            p.add("vector", lambda e: e.reduce_max(af(fPRB + 4, 1), af(fPRB + 64, 16), AX.X))
            geo = p.add("vector", lambda e: e.tensor_scalar(af(fMB, 16), af(fPRB + 32, 16), af(fPRB + 4, 1), None, ALU.is_ge))
            w0 = [geo, ld_lt, ld_on, ms_onc]
            if ("cmp_mm",) in cp:
                w0.append(cp[("cmp_mm",)])
            mmro = p.add("tensor", lambda e: e.matmul(ovp[:, 0:16], af(fONES, NE * 0 + 128, 0, 1), af(fROFF, NE, 0, 1), start=True, stop=False), waits=w0 + [zro])
            mmpf = p.add("tensor", lambda e: e.matmul(ovp[:, 0:16], af(fLTRI, 128), af(fMB, 16), start=False, stop=True))
            mmtot = p.add("tensor", lambda e: e.matmul(ovp[0:1, 16:32], af(fONC, 1), af(fMB, 16), start=True, stop=True))
            evpm = p.add("vector", lambda e: e.tensor_copy(af(fPM, 16), ovp[:, 0:16]), waits=[mmpf])
            roff = p.add("vector", lambda e: e.tensor_tensor(af(fROFF, NE, 0, 1), af(fROFF, NE, 0, 1), ovp[0:1, 16:32], ALU.add), waits=[mmtot])
            cp[("cmp_mm",)] = roff
            scs = []
            for rk, (mname, moff, cvoff) in enumerate((("A", fPRB + 48, fPRB + 3), ("B", fPRB + 64, fPRB + 4))):
                if rk == 1:
                    p.add("vector", lambda e: e.tensor_tensor(af(fPRB + 64, 16), af(fMB, 16), af(fPRB + 48, 16), ALU.subtract))
                p.add("vector", lambda e, moff=moff: e.tensor_tensor(af(fCW, 16), af(moff, 16), af(fEIDB, 16), ALU.mult), waits=[ev_ei])
                p.add("vector", lambda e: e.tensor_reduce(af(fCW + 32, 1), af(fCW, 16), AX.X, ALU.add))
                p.add("vector", lambda e, moff=moff: e.tensor_tensor(af(fCW + 16, 16), af(moff, 16), af(fPM, 16), ALU.mult), waits=[evpm])
                p.add("vector", lambda e: e.tensor_reduce(af(fCW + 33, 1), af(fCW + 16, 16), AX.X, ALU.add))
                p.add("vector", lambda e: e.tensor_scalar(af(fCW + 33, 1), af(fCW + 33, 1), float(CC - 1), None, ALU.min))
                slf = p.add("vector", lambda e: e.tensor_scalar(af(fCW + 34, 1), af(fCW + 32, 1), float(CC), af(fCW + 33, 1), ALU.mult, ALU.add))
                wsli = [slf]
                if ("scx", g - 1, rk) in cp:
                    wsli.append(cp[("scx", g - 1, rk)])
                sli = p.add("vector", lambda e, rk=rk: e.tensor_copy(AI[:, rk:rk + 1], af(fCW + 34, 1)), waits=wsli)
                wmv = [cp[("slcv", g - 1)]] if rk == 0 and ("slcv", g - 1) in cp else []
                p.add("vector", lambda e, rk=rk: e.tensor_copy(af(fMET + 2 * rk, 1), af(fCW + 34, 1)), waits=wmv)
                mcv = p.add("vector", lambda e, rk=rk, cvoff=cvoff: e.tensor_copy(af(fMET + 2 * rk + 1, 1), af(cvoff, 1)))
                wsc = [sli] + (zdmas if g == 0 else [])
                if ("scx", g - 2, rk) in cp:
                    wsc.append(cp[("scx", g - 2, rk)])
                if rk == 0:
                    wst = [prev[("x1tmdma", g // 4, g % 4)]]
                    if ("scx", g - 2, 1) in cp:
                        wst.append(cp[("scx", g - 2, 1)])
                    cp[("xst", g)] = p.add("sync", lambda e, g=g: e.dma_start(
                        out=ab(bDWN + 1024 * (g % 2), 1024),
                        in_=E["X1TM"][g * 128:(g + 1) * 128, :]), waits=wst, dma="win")
                if SER_SC and ("ser_sc",) in cp:
                    wsc.append(cp[("ser_sc",)])
                scx = p.add("gpsimd", lambda e, rk=rk, g=g: e.indirect_dma_start(
                    E["XCOMP"], bass.IndirectOffsetOnAxis(AI[:, rk:rk + 1], 0),
                    ab(bDWN + 1024 * (g % 2), 1024), None), waits=wsc + [cp[("xst", g)]], dma="wsc")
                if SER_SC:
                    cp[("ser_sc",)] = scx
                cp[("scx", g, rk)] = scx
                scs.append(scx)
            slcv = p.add("sync", lambda e, g=g: e.dma_start(
                out=E["SLCV"][g * 128:(g + 1) * 128, :], in_=AFr[:, fMET:fMET + 4]),
                waits=[mcv], dma="out")
            cp[("slcv", g)] = slcv
            sc_dmas += scs
            if "stop_cmp0" in dbg and g == 0:
                dws = scs + [roff]
                dd = []
                dd.append(p.add("sync", lambda e: e.dma_start(out=E["DBG_CMP"][:, 0:112], in_=af(fPRB, 112)), waits=dws, dma="out"))
                dd.append(p.add("sync", lambda e: e.dma_start(out=E["DBG_CMP"][:, 112:128], in_=af(fPM, 16)), waits=dws, dma="out"))
                dd.append(p.add("sync", lambda e: e.dma_start(out=E["DBG_CMP"][:, 128:144], in_=af(fMB, 16)), waits=dws, dma="out"))
                dd.append(p.add("sync", lambda e: e.dma_start(out=E["DBG_CMP"][:, 144:208], in_=af(fCW, 64)), waits=dws, dma="out"))
                dd.append(p.add("sync", lambda e: e.dma_start(out=E["DBG_CMP"][0:1, 208:224], in_=af(fROFF, NE, 0, 1)), waits=dws, dma="out"))
                dd.append(p.add("sync", lambda e: e.dma_start(out=E["DBG_AI"][:, 0:2], in_=AI[:, 0:2]), waits=dws, dma="out"))
                dd.append(p.add("sync", lambda e: e.dma_start(out=E["DBG_CMP"][:, 224:240], in_=af(fLTRI, 16)), waits=dws, dma="out"))
                dd.append(p.add("sync", lambda e: e.dma_start(out=E["DBG_CMP"][:, 240:241], in_=AFr[:, fLTRI + 127:fLTRI + 128], allow_slow_non_contiguous=True), waits=dws, dma="out"))
                dd.append(p.add("sync", lambda e: e.dma_start(out=E["DBG_CMP"][0:1, 241:242], in_=af(fONC, 1, 0, 1)), waits=dws, dma="out"))
                dd.append(p.add("sync", lambda e: e.dma_start(out=E["DBG_CMP"][0:1, 242:250], in_=af(fONES, 8, 0, 1)), waits=dws, dma="out"))
                p.add("sync", lambda e: e.nop(), waits=dd)
                return
        lsg = p.add("gpsimd", lambda e: e.dma_start(out=ab(bWG[0], 8192), in_=E["sgw"]), dma="wing",
            waits=([ldrw, prev[("x1dmab", 7)], prev[("x1dma", 7)]] if Q == 0 else [cp[("shdn_last",)]]))
        lsu = p.add("gpsimd", lambda e: e.dma_start(out=ab(bWG[1], 8192), in_=E["suw"]), dma="wing")
        last_sil = cp.get(("lnb_pe_done",))
        last_mi = cp.get(("sh_mi_last",))
        gu_done = []
        for fm in range(8):
            for tt in range(2):
                gm = []
                for c in range(8):
                    w = []
                    if c == 0:
                        w = [lsg, lxb]
                        if last_sil is not None:
                            w.append(last_sil)
                    gm.append(p.add("tensor", lambda e, fm=fm, tt=tt, c=c: e.matmul(
                        S1p[:, 0:512], ab(bWG[0] + 1024 * c + 128 * fm, 128), ab(bXB + 1024 * c + 512 * tt, 512),
                        start=(c == 0), stop=(c == 7)), waits=w))
                sil = p.add("scalar", lambda e: e.activation(ab(bG, 512), S1p[:, 0:512], AF.Silu), waits=[gm[-1], (last_mi if last_mi is not None else None)], tag=f"sil{Q}.{fm}.{tt}")
                last_sil = sil
                um = []
                for c in range(8):
                    w = []
                    if c == 0:
                        w = [lsu]
                        if last_mi is not None:
                            w.append(last_mi)
                    um.append(p.add("tensor", lambda e, fm=fm, tt=tt, c=c: e.matmul(
                        S1p[:, 512:1024], ab(bWG[1] + 1024 * c + 128 * fm, 128), ab(bXB + 1024 * c + 512 * tt, 512),
                        start=(c == 0), stop=(c == 7)), waits=w))
                mi = p.add("vector", lambda e, fm=fm, tt=tt: e.tensor_tensor(
                    ab(bINT + 1024 * fm + 512 * tt, 512), ab(bG, 512), S1p[:, 512:1024], ALU.mult),
                    waits=[um[-1], sil], tag=f"smi{Q}.{fm}.{tt}")
                last_mi = mi
                gu_done.append(mi)
        cp[("sh_mi_last",)] = last_mi
        lsd = p.add("gpsimd", lambda e: e.dma_start(out=ab(bWG[0], 8192), in_=E["sdw"]), dma="wing", waits=[gu_done[-1], last_sil])
        last_rc = [cp.get(("lnb_pe_done",)), cp.get(("lnb_pe_done",))]
        dn_ops = []
        for dm in range(8):
            for tt in range(2):
                bank = tt
                psl = (S2p[:, 0:512], S2p[:, 512:1024])[bank]
                mm = []
                for c in range(8):
                    w = []
                    if c == 0:
                        w = [lsd] + gu_done[tt::2]
                        if last_rc[bank] is not None:
                            w.append(last_rc[bank])
                    mm.append(p.add("tensor", lambda e, dm=dm, tt=tt, c=c, psl=psl: e.matmul(
                        psl, ab(bWG[0] + 1024 * c + 128 * dm, 128), ab(bINT + 1024 * c + 512 * tt, 512),
                        start=(c == 0), stop=(c == 7)), waits=w))
                w2 = [mm[-1]]
                if ("shdma", Q - 1) in cp and dm == 0 and tt == 0:
                    w2.append(cp[("shdma", Q - 1)])
                rc = p.add("vector", lambda e, dm=dm, tt=tt, psl=psl: e.tensor_copy(
                    ab(bSH + 1024 * dm + 512 * tt, 512), psl), waits=w2, tag=f"sracc{Q}.{dm}.{tt}")
                last_rc[bank] = rc
                dn_ops.append(rc)
        cp[("shdn_last",)] = dn_ops[-1]
        cp[("lnb_pe_done",)] = dn_ops[-1]
        shd = p.add("sync", lambda e, tq=tq: e.dma_start(
            out=E["SHFM"].rearrange("(c p) t -> p c t", p=128)[:, :, tq:tq + 1024],
            in_=AB[:, bSH:bSH + 8192].rearrange("p (c t) -> p c t", c=8)), waits=dn_ops, dma="out")
        sh_dmas.append(shd)
        cp[("shdma", Q)] = shd
        cp[("ovp_free",)] = cp[("cmp_mm",)]

    for ei in range(NE * 2):
        wi = ei // 2
        wbuf = (wi + 2) % 3
        if ei % 2 == 0:
            if wi == 0:
                wld_w = [sh_dmas[-1]]
            elif wi == 1:
                wld_w = [cp[("shdn_last",)]]
            elif wi == 2:
                wld_w = [cp[("sh_mi_last",)]]
            else:
                wld_w = [cp[("emi_last", ei - 3)], cp[("edn_mm_last", ei - 3)]]
            lg_ = p.add("gpsimd", lambda e, wi=wi, wbuf=wbuf: e.dma_start(
                out=ab(bWG[wbuf], 4096), in_=E["gw"][wi]), dma="wing", waits=wld_w)
            lu_ = p.add("gpsimd", lambda e, wi=wi, wbuf=wbuf: e.dma_start(
                out=ab(bWG[wbuf] + 4096, 4096), in_=E["uw"][wi]), dma="wing")
            ld_ = p.add("gpsimd", lambda e, wi=wi, wbuf=wbuf: e.dma_start(
                out=ab(bWG[wbuf] + 8192, 4096), in_=E["dw"][wi]), dma="wing")
        wload = list(sc_dmas) if ei == 0 else [cp[("etr_last", ei - 1)]]
        lxc = []
        for t in range(8):
            lxc.append(p.add("sync", lambda e, ei=ei, t=t: e.dma_start(
                out=ab(bXB + 1024 * t, 1024),
                in_=E["XCOMP"][ei * C + 128 * t:ei * C + 128 * (t + 1), :]),
                waits=wload if t == 0 else [], dma="win"))
        xtrev = []
        for c in range(8):
            dstp = (trp0, trp1)[c % 2]
            trs = []
            for t in range(8):
                w = [lxc[t]]
                if t == 0:
                    if c >= 2:
                        w.append(cp[("xtrev", ei, c - 2)])
                    elif ("odtrev", ei - 1, c + 6) in cp:
                        w.append(cp[("odtrev", ei - 1, c + 6)])
                    elif ei == 0:
                        w.append(st["trev_last"][c % 2])
                trs.append(p.add("tensor", lambda e, c=c, t=t, dstp=dstp: e.transpose(
                    dstp[:, 128 * t:128 * t + 128], ab(bXB + 1024 * t + 128 * c, 128), ab(bIDENT, 128)), waits=w))
            w2 = [trs[-1]]
            if ("xgate", ei - 1) in cp and c < 2:
                w2.append(cp[("xgate", ei - 1)])
            if ei == 0:
                w2.append(sh_dmas[-1])
            tev = p.add("vector", lambda e, c=c, dstp=dstp: e.tensor_copy(ab(bSH + 1024 * c, 1024), dstp[:, :]), waits=w2)
            cp[("xtrev", ei, c)] = tev
            xtrev.append(tev)
        cp[("etr_last", ei)] = xtrev[-1]
        last_esil = cp.get(("esil_last",))
        last_emi = cp.get(("emi_bank_last",))
        emi = []
        for fm in range(4):
            for tt in range(2):
                gm = []
                for c in range(8):
                    w = []
                    if c == 0:
                        w = [lg_, xtrev[c]] if fm == 0 and tt == 0 else [lg_]
                        w += [x for x in (last_esil,) if x is not None]
                        if fm == 0 and tt == 0:
                            w += xtrev
                    gm.append(p.add("tensor", lambda e, fm=fm, tt=tt, c=c, wbuf=wbuf: e.matmul(
                        S1p[:, 0:512], ab(bWG[wbuf] + 512 * c + 128 * fm, 128), ab(bSH + 1024 * c + 512 * tt, 512),
                        start=(c == 0), stop=(c == 7)), waits=w))
                cp[("xgate", ei)] = gm[-1]
                sil = p.add("scalar", lambda e: e.activation(ab(bG, 512), S1p[:, 0:512], AF.Silu),
                            waits=[gm[-1]] + ([last_emi] if last_emi is not None else []))
                last_esil = sil
                um = []
                for c in range(8):
                    w = []
                    if c == 0:
                        w = [lu_]
                        if last_emi is not None:
                            w.append(last_emi)
                    um.append(p.add("tensor", lambda e, fm=fm, tt=tt, c=c, wbuf=wbuf: e.matmul(
                        S1p[:, 512:1024], ab(bWG[wbuf] + 4096 + 512 * c + 128 * fm, 128), ab(bSH + 1024 * c + 512 * tt, 512),
                        start=(c == 0), stop=(c == 7)), waits=w))
                mi1 = p.add("vector", lambda e, fm=fm, tt=tt: e.tensor_tensor(
                    ab(bINT + 1024 * fm + 512 * tt, 512), ab(bG, 512), S1p[:, 512:1024], ALU.mult),
                    waits=[um[-1], sil])
                last_emi = mi1
                emi.append(mi1)
        cp[("esil_last",)] = last_esil
        cp[("emi_bank_last",)] = last_emi
        cp[("emi_last", ei)] = emi[-1]
        last_rc2 = cp.get(("edn_rc_last",), [None, None])
        edn = []
        for dm in range(8):
            for tt in range(2):
                bank = tt
                psl = (S2p[:, 0:512], S2p[:, 512:1024])[bank]
                mm = []
                for c in range(4):
                    w = []
                    if c == 0:
                        w = [ld_] + emi[tt::2]
                        if last_rc2[bank] is not None:
                            w.append(last_rc2[bank])
                    mm.append(p.add("tensor", lambda e, dm=dm, tt=tt, c=c, psl=psl, wbuf=wbuf: e.matmul(
                        psl, ab(bWG[wbuf] + 8192 + 1024 * c + 128 * dm, 128), ab(bINT + 1024 * c + 512 * tt, 512),
                        start=(c == 0), stop=(c == 3)), waits=w))
                w2 = [mm[-1]]
                if ("odtr_last", ei - 1) in cp and dm == 0 and tt == 0:
                    w2.append(cp[("odtr_last", ei - 1)])
                rc = p.add("vector", lambda e, dm=dm, tt=tt, psl=psl: e.tensor_copy(
                    ab(bDWN + 1024 * dm + 512 * tt, 512), psl), waits=w2)
                last_rc2[bank] = rc
                edn.append(rc)
        cp[("edn_rc_last",)] = last_rc2
        cp[("edn_mm_last", ei)] = mm[-1]
        oglast = None
        for t in range(8):
            dstp = (trp0, trp1)[t % 2]
            trs = []
            for dm in range(8):
                w = [edn[2 * dm + (t // 4)]]
                if dm == 0 and ("odtrev", ei, t - 2) in cp:
                    w.append(cp[("odtrev", ei, t - 2)])
                elif dm == 0 and t < 2:
                    w.append(cp[("xtrev", ei, t + 6)])
                trs.append(p.add("tensor", lambda e, t=t, dm=dm, dstp=dstp: e.transpose(
                    dstp[:, 128 * dm:128 * dm + 128], ab(bDWN + 1024 * dm + 128 * t, 128), ab(bIDENT, 128)), waits=w))
            slot = t % 2
            wev = [trs[-1]]
            if ("xod", ei, t - 2) in cp:
                wev.append(cp[("xod", ei, t - 2)])
            elif t < 2 and ("xod", ei - 1, t + 6) in cp:
                wev.append(cp[("xod", ei - 1, t + 6)])
            oev = p.add("vector", lambda e, t=t, slot=slot, dstp=dstp: e.tensor_copy(
                ab(bXO + 1024 * slot, 1024), dstp[:, :]), waits=wev)
            cp[("odtrev", ei, t)] = oev
            xod = p.add("sync", lambda e, ei=ei, t=t, slot=slot: e.dma_start(
                out=E["XOUT"][ei * C + 128 * t:ei * C + 128 * (t + 1), :],
                in_=ab(bXO + 1024 * slot, 1024)), waits=[oev], dma="out")
            cp[("xod", ei, t)] = xod
            oglast = xod
        cp[("odtr_last", ei)] = cp[("odtrev", ei, 7)]
        cp[("escat_last", ei)] = oglast

    all_scats = [cp[("escat_last", e_)] for e_ in range(NE * 2)] + [cp[("xod", NE * 2 - 1, t)] for t in range(8)]
    slcv_all = [cp[("slcv", gg)] for gg in range(32)]
    bGT4 = [bXO, bXO + 1024, bDWN, bDWN + 1024]
    x1f_r2 = x1f_r
    shf_r = E["SHFM"].rearrange("(c p) t -> p c t", p=128)

    def gat_prep(g):
        fSL = fSLC + 8 * (g % 2)
        aco = 2 * (g % 2)
        wl = list(slcv_all) if g == 0 else []
        if ("gmulB", g - 2) in cp:
            wl.append(cp[("gmulB", g - 2)])
        lsl = p.add("sync", lambda e, g=g, fSL=fSL: e.dma_start(
            out=AFr[:, fSL:fSL + 4], in_=E["SLCV"][g * 128:(g + 1) * 128, :]), waits=wl, dma="win")
        for rk in range(2):
            wa = [lsl]
            if ("gat", g - 2, rk) in cp:
                wa.append(cp[("gat", g - 2, rk)])
            elif g == 0:
                wa.append(cp[("scx", 31, rk)])
            aic = p.add("vector", lambda e, rk=rk, fSL=fSL, aco=aco: e.tensor_copy(
                AI[:, aco + rk:aco + rk + 1], af(fSL + 2 * rk, 1)), waits=wa)
            gslot = bGT4[2 * (g % 2) + rk]
            wg_ = [aic] + (all_scats if g == 0 else [])
            if ("gmul", g - 2, rk) in cp:
                wg_.append(cp[("gmul", g - 2, rk)])
            cp[("gat", g, rk)] = p.add("gpsimd", lambda e, rk=rk, gslot=gslot, aco=aco: e.indirect_dma_start(
                AB[:, gslot:gslot + 1024], None,
                E["XOUT"], bass.IndirectOffsetOnAxis(AI[:, aco + rk:aco + rk + 1], 0)),
                waits=wg_, dma="wsc")

    gat_prep(0)
    for g in range(32):
        if g + 1 < 32:
            gat_prep(g + 1)
        fSL = fSLC + 8 * (g % 2)
        lx1 = p.add("sync", lambda e, g=g: e.dma_start(
            out=AFr[:, fX1Q:fX1Q + 1024].rearrange("p (c t) -> p c t", c=8),
            in_=x1f_r2[:, :, 128 * g:128 * g + 128]), dma="win",
            waits=[cp[("ydma", g - 2)]] if ("ydma", g - 2) in cp else [])
        lsh = p.add("sync", lambda e, g=g: e.dma_start(
            out=AB[:, bSHF:bSHF + 1024].rearrange("p (c t) -> p c t", c=8),
            in_=shf_r[:, :, 128 * g:128 * g + 128]), dma="win",
            waits=sh_dmas if g == 0 else [])
        cA = p.add("scalar", lambda e, g=g, fSL=fSL: e.activation(
            af(fRAC, 1024), AB[:, bGT4[2 * (g % 2)]:bGT4[2 * (g % 2)] + 1024], AF.Copy, scale=af(fSL + 1, 1)),
            waits=[cp[("gat", g, 0)], cp.get(("ra1_prev",))])
        cB = p.add("scalar", lambda e, g=g, fSL=fSL: e.activation(
            af(fGB, 1024), AB[:, bGT4[2 * (g % 2) + 1]:bGT4[2 * (g % 2) + 1] + 1024], AF.Copy, scale=af(fSL + 3, 1)),
            waits=[cp[("gat", g, 1)], cp.get(("rb_prev",))])
        cp[("gmul", g, 0)] = cA
        cp[("gmul", g, 1)] = cB
        cp[("gmulB", g)] = cB
        cmb = p.add("vector", lambda e: e.tensor_tensor(af(fCMB, 1024), af(fX1Q, 1024), ab(bSHF, 1024), ALU.add),
                    waits=[lx1, lsh, cp.get(("yts_last",))], tag=f"cmb{g}")
        radd = []
        for half in range(2):
            trs = []
            for c4 in range(4):
                c = 4 * half + c4
                w = [cmb]
                if half == 0 and c4 == 0 and ("yev", g - 1) in cp:
                    w.append(cp[("yev", g - 1)])
                if c4 == 0 and half == 1:
                    w.append(radd[0])
                trs.append(p.add("tensor", lambda e, c=c, c4=c4: e.transpose(
                    ovp[:, 128 * c4:128 * c4 + 128], af(fCMB + 128 * c, 128), af(fIDENT, 128)), waits=w))
            ra = p.add("vector", lambda e, half=half: e.tensor_tensor(
                af(fR2 + 512 * half, 512), ovp[:, 0:512], af(fRAC + 512 * half, 512), ALU.add),
                waits=[trs[-1], cA], tag=f"ra{g}.{half}")
            radd.append(ra)
        cp[("ra1_prev",)] = radd[-1]
        rb = p.add("vector", lambda e: e.tensor_tensor(af(fR2, 1024), af(fR2, 1024), af(fGB, 1024), ALU.add),
                   waits=[radd[-1], cB])
        cp[("rb_prev",)] = rb
        cp[("yev", g)] = rb
        fLO = fLN2[g % 2]
        s1 = p.add("scalar", lambda e, fLO=fLO: e.activation(af(fLO, 1024), af(fR2, 1024), AF.Identity, accum_out=af(fT2, 1)),
                   waits=[rb] + ([cp[("ydma", g - 2)]] if ("ydma", g - 2) in cp else []))
        s2 = p.add("scalar", lambda e, fLO=fLO: e.activation(af(fLO, 1024), af(fR2, 1024), AF.Square, accum_out=af(fT2 + 1, 1)))
        mb2 = p.add("vector", lambda e: e.tensor_scalar(af(fT2 + 2, 1), af(fT2, 1), 1.0 / D, None, ALU.mult), waits=[s1, s2])
        tb2 = p.add("vector", lambda e: e.tensor_scalar(af(fT2 + 3, 1), af(fT2 + 1, 1), 1.0 / D, EPS, ALU.mult, ALU.add))
        m2b = p.add("vector", lambda e: e.tensor_tensor(af(fT2 + 4, 1), af(fT2 + 2, 1), af(fT2 + 2, 1), ALU.mult))
        vrb = p.add("vector", lambda e: e.tensor_tensor(af(fT2 + 4, 1), af(fT2 + 3, 1), af(fT2 + 4, 1), ALU.subtract))
        lnb2 = p.add("scalar", lambda e: e.activation(af(fT2 + 5, 1), af(fT2 + 4, 1), AF.Ln), waits=[vrb])
        rsb2 = p.add("scalar", lambda e: e.activation(af(fT2 + 6, 1), af(fT2 + 5, 1), AF.Exp, scale=-0.5))
        yts = p.add("vector", lambda e, fLO=fLO: e.tensor_scalar(af(fLO, 1024), af(fR2, 1024), af(fT2 + 2, 1), af(fT2 + 6, 1), ALU.subtract, ALU.mult), waits=[rsb2], tag=f"yts{g}")
        cp[("yts_last",)] = yts
        yd = p.add("sync", lambda e, g=g, fLO=fLO: e.dma_start(
            out=E["YT"][128 * g:128 * (g + 1), :], in_=af(fLO, 1024)), waits=[yts], dma="out")
        cp[("ydma", g)] = yd
        y_dmas.append(yd)

    p.add("sync", lambda e: e.nop(), waits=y_dmas)


def swz(m):
    m = np.asarray(m, dtype=np.float32)
    c = m.shape[0] // 128
    return np.ascontiguousarray(m.reshape(c, 128, m.shape[1]).transpose(1, 0, 2).reshape(128, -1))


def make_inputs_for_core(inputs, b):
    f32 = lambda x: np.ascontiguousarray(np.asarray(x, dtype=np.float32))
    d = {}
    d["XT"] = f32(np.asarray(inputs["L0_attn_X"][b]).T)
    d["WqT"] = swz(np.asarray(inputs["L0_attn_Wq"]).T)
    d["WkT"] = swz(np.asarray(inputs["L0_attn_Wk"]).T)
    d["WvT"] = swz(np.asarray(inputs["L0_attn_Wv"]).T)
    d["WoT"] = swz(np.asarray(inputs["L0_attn_Wo"]).T)
    d["lq1"] = f32(inputs["L0_attn_lambda_q1"])
    d["lk1"] = f32(inputs["L0_attn_lambda_k1"])
    d["lq2"] = f32(inputs["L0_attn_lambda_q2"])
    d["lk2"] = f32(inputs["L0_attn_lambda_k2"])
    d["rms_scale"] = f32(np.asarray(inputs["L0_attn_rms_scale"])[None, :])
    d["rw"] = swz(np.asarray(inputs["L0_ffn_router_w"]))
    d["gw"] = np.stack([swz(np.asarray(inputs["L0_ffn_gate_w"])[e]) for e in range(NE)])
    d["uw"] = np.stack([swz(np.asarray(inputs["L0_ffn_up_w"])[e]) for e in range(NE)])
    d["dw"] = np.stack([swz(np.asarray(inputs["L0_ffn_down_w"])[e]) for e in range(NE)])
    d["sgw"] = swz(np.asarray(inputs["L0_ffn_shared_gate_w"]))
    d["suw"] = swz(np.asarray(inputs["L0_ffn_shared_up_w"]))
    d["sdw"] = swz(np.asarray(inputs["L0_ffn_shared_down_w"]))
    d["identf"] = np.eye(128, dtype=np.float32)
    d["onesf"] = np.ones((1, 128), dtype=np.float32)
    d["iotap"] = np.arange(128, dtype=np.float32).reshape(128, 1)
    d["eids"] = np.arange(NE, dtype=np.float32).reshape(1, NE)
    d["ltri"] = np.triu(np.ones((128, 128), dtype=np.float32), 1)
    return d


def kernel(**inputs):
    nc = build_program()
    in_maps = [make_inputs_for_core(inputs, b) for b in range(8)]
    res = run_bass_kernel_spmd(nc, in_maps, core_ids=list(range(8)))
    out = np.stack([np.asarray(r["YT"]) for r in res.results], axis=0)
    return np.ascontiguousarray(out).astype(np.float32)


# version 18
# speedup vs baseline: 1.2912x; 1.0278x over previous
import math
import numpy as np
import ml_dtypes
import concourse.bass as bass
import concourse.mybir as mybir
from concourse.bass_utils import run_bass_kernel_spmd

F32 = mybir.dt.float32
BF16 = mybir.dt.bfloat16
AF = mybir.ActivationFunctionType
ALU = mybir.AluOpType
AX = mybir.AxisListType

T = 4096
D = 1024
H = 8
DV = 256
NE = 16
FE = 512
FS = 1024
SCALE = 1.0 / math.sqrt(128.0)
LAMBDA_INIT = 0.8 - 0.6 * math.exp(-0.3)
EPS = 1e-5

ENGINES = ("tensor", "scalar", "vector", "gpsimd", "sync")
SER_SC = False


DMA_POOL = 8


class Op:
    __slots__ = ("eng", "fn", "waits", "dma", "marked", "sem", "val", "pre", "tag")

    def __init__(self, eng, fn, waits, dma, tag):
        self.eng, self.fn, self.dma, self.tag = eng, fn, dma, tag
        self.waits = [w for w in waits if w is not None]
        self.marked = False
        self.sem = None
        self.val = None
        self.pre = None


class Prog:
    def __init__(self):
        self.ops = []

    def add(self, eng, fn, waits=(), dma=None, tag=""):
        op = Op(eng, fn, waits, dma, tag)
        self.ops.append(op)
        return op

    def resolve(self, nc, ctx):
        groups = sorted({op.dma for op in self.ops if op.dma is not None})
        sems = {}
        for e in ENGINES:
            sems["e_" + e] = ctx.enter_context(nc.semaphore("e_" + e))
        pool = {}
        for g in groups:
            pool[g] = [ctx.enter_context(nc.semaphore(f"d_{g}{i}")) for i in range(DMA_POOL)]
        CHAIN = ("vector", "scalar", "gpsimd")
        for op in self.ops:
            kept = []
            for w in op.waits:
                if w.dma is None and w.eng == op.eng and op.dma is None:
                    continue
                w.marked = True
                kept.append(w)
            op.waits = kept
        for op in self.ops:
            if op.eng in CHAIN and op.dma is None:
                op.marked = True
        counts = {("e_" + e): 0 for e in ENGINES}
        rr = {g: 0 for g in groups}
        dma_counts = {}
        for op in self.ops:
            if op.dma is not None:
                op.marked = True
                i = rr[op.dma]
                rr[op.dma] = (i + 1) % DMA_POOL
                sem = pool[op.dma][i]
                prevv = dma_counts.get(id(sem), 0)
                if prevv:
                    op.pre = (sem, prevv)
                op.sem = sem
                op.val = prevv + 16
                dma_counts[id(sem)] = op.val
            elif op.marked:
                cnt = counts["e_" + op.eng]
                if op.eng in CHAIN and cnt > 0:
                    op.pre = (sems["e_" + op.eng], cnt)
                counts["e_" + op.eng] = cnt + 1
                op.sem = sems["e_" + op.eng]
                op.val = cnt + 1
            elif op.eng in CHAIN:
                pass
        self.sems = sems
        self.drain_waits = []
        seen = set()
        for op in self.ops:
            if op.dma is not None and id(op.sem) not in seen:
                seen.add(id(op.sem))
        finals = {}
        for op in self.ops:
            if op.dma is not None:
                finals[id(op.sem)] = (op.sem, op.val)
        self.drain_waits = list(finals.values())

    def emit(self, eng_name, eng):
        watermark = {}
        for op in self.ops:
            if op.eng != eng_name:
                continue
            pend = list(op.waits)
            for w in pend:
                key = id(w.sem)
                if watermark.get(key, -1) >= w.val:
                    continue
                eng.wait_ge(w.sem, w.val)
                watermark[key] = w.val
            if op.pre is not None:
                sem, v = op.pre
                if watermark.get(id(sem), -1) < v:
                    eng.wait_ge(sem, v)
                    watermark[id(sem)] = v
            inst = op.fn(eng)
            if op.marked:
                inst.then_inc(op.sem, 16 if op.dma is not None else 1)
        if eng_name == "sync":
            for sem, v in self.drain_waits:
                if watermark.get(id(sem), -1) < v:
                    eng.wait_ge(sem, v)


def build_program(dbg=()):
    nc = bass.Bass("TRN2", target_bir_lowering=False)
    dt = nc.dram_tensor
    E = {}

    for n, s in (("XT", [D, T]), ("WqT", [128, 16384]), ("WkT", [128, 16384]),
                 ("WvT", [128, 16384]), ("WoT", [128, 16384]),
                 ("lq1", [H, 128]), ("lk1", [H, 128]), ("lq2", [H, 128]), ("lk2", [H, 128]),
                 ("rms_scale", [1, DV]), ("rw", [128, 128]),
                 ("gw", [NE, 128, 4096]), ("uw", [NE, 128, 4096]), ("dw", [NE, 128, 4096]),
                 ("sgw", [128, 8192]), ("suw", [128, 8192]), ("sdw", [128, 8192]),
                 ("identf", [128, 128]), ("onesf", [1, 128])):
        E[n] = dt(n, s, F32, kind="ExternalInput").ap()
    E["iotap"] = dt("iotap", [128, 1], F32, kind="ExternalInput").ap()
    E["eids"] = dt("eids", [1, NE], F32, kind="ExternalInput").ap()
    E["ltri"] = dt("ltri", [128, 128], F32, kind="ExternalInput").ap()
    E["YT"] = dt("YT", [T, D], F32, kind="ExternalOutput").ap()

    def scratch(name, shape, dtype):
        kind = "ExternalOutput" if name in dbg else "Internal"
        return dt(name, shape, dtype, kind=kind).ap()

    if "stop_cmp0" in dbg:
        E["DBG_CMP"] = dt("DBG_CMP", [128, 256], F32, kind="ExternalOutput").ap()
        E["DBG_AI"] = dt("DBG_AI", [128, 32], mybir.dt.int32, kind="ExternalOutput").ap()
    if "probe_u0" in dbg:
        E["DBG_P"] = dt("DBG_P", [128, 4096], BF16, kind="ExternalOutput").ap()
        E["DBG_E2"] = dt("DBG_E2", [128, 4096], BF16, kind="ExternalOutput").ap()
        E["DBG_DEN"] = dt("DBG_DEN", [128, 32], F32, kind="ExternalOutput").ap()
        E["DBG_PT"] = dt("DBG_PT", [128, 4096], BF16, kind="ExternalOutput").ap()
        E["DBG_ON"] = dt("DBG_ON", [128, 256], F32, kind="ExternalOutput").ap()
        E["DBG_OTM"] = dt("DBG_OTM", [128, 256], BF16, kind="ExternalOutput").ap()
        E["DBG_Q"] = dt("DBG_Q", [128, 8192], BF16, kind="ExternalOutput").ap()
        E["DBG_K"] = dt("DBG_K", [128, 8192], BF16, kind="ExternalOutput").ap()
        E["DBG_V"] = dt("DBG_V", [128, 8192], BF16, kind="ExternalOutput").ap()
    E["QTh"] = scratch("QTh", [2048, T], BF16)
    E["KTh"] = scratch("KTh", [2048, T], BF16)
    E["VH"] = scratch("VH", [T, 2048], BF16)
    E["OT"] = scratch("OT", [2048, T], BF16)
    E["X1F"] = scratch("X1F", [D, T], F32)
    E["X1B"] = scratch("X1B", [D, T], BF16)
    E["X1TM"] = scratch("X1TM", [T, D], BF16)
    E["XCOMP"] = scratch("XCOMP", [NE * 2048, D], BF16)
    E["XOUT"] = scratch("XOUT", [NE * 2048, D], BF16)
    E["SLCV"] = scratch("SLCV", [T, 4], F32)
    E["SHFM"] = scratch("SHFM", [D, T], BF16)

    from contextlib import ExitStack
    with ExitStack() as ctx:
        E["AB"] = ctx.enter_context(nc.sbuf_tensor("AB", [128, 73728], BF16))
        E["AFr"] = ctx.enter_context(nc.sbuf_tensor("AFr", [128, 15488], F32))
        E["AI"] = ctx.enter_context(nc.sbuf_tensor("AI", [128, 32], mybir.dt.int32))
        E["S1p"] = ctx.enter_context(nc.psum_tensor("S1p", [128, 1024], F32))
        E["S2p"] = ctx.enter_context(nc.psum_tensor("S2p", [128, 1024], F32))
        E["trp0"] = ctx.enter_context(nc.psum_tensor("trp0", [128, 1024], BF16))
        E["trp1"] = ctx.enter_context(nc.psum_tensor("trp1", [128, 1024], BF16))
        E["ovp"] = ctx.enter_context(nc.psum_tensor("ovp", [128, 512], F32))
        E["pob"] = ctx.enter_context(nc.psum_tensor("pob", [128, 1024], BF16))

        p = Prog()
        emit_all(p, E, dbg)
        p.resolve(nc, ctx)

        with nc.Block() as block:
            @block.tensor
            def _(e):
                p.emit("tensor", e)

            @block.scalar
            def _(e):
                p.emit("scalar", e)

            @block.vector
            def _(e):
                p.emit("vector", e)

            @block.gpsimd
            def _(e):
                p.emit("gpsimd", e)

            @block.sync
            def _(e):
                p.emit("sync", e)
    return nc


def emit_all(p, E, dbg):
    AB, AFr, AI = E["AB"], E["AFr"], E["AI"]
    S1p, S2p, trp0, trp1, ovp, pob = E["S1p"], E["S2p"], E["trp0"], E["trp1"], E["ovp"], E["pob"]

    def ab(off, n, p0=0, np_=128):
        return AB[p0:p0 + np_, off:off + n]

    def af(off, n, p0=0, np_=128):
        return AFr[p0:p0 + np_, off:off + n]

    fIDENT, fONES, fRMSB, fLAM = 0, 128, 256, 512
    fBIG = 640
    bIDENT, bONES, bSEL = 0, 128, 256
    bBIG = bSEL + 2048

    fIOTA, fONC, fEIDS, fEIDB = 520, 521, 524, 544
    fEPSC = 522
    fLTRI = 15232
    ld_id = p.add("sync", lambda e: e.dma_start(out=af(fIDENT, 128), in_=E["identf"]), dma="win")
    ld_on = p.add("sync", lambda e: e.dma_start(out=af(fONES, 128, 0, 1), in_=E["onesf"]), dma="win")
    ld_io = p.add("sync", lambda e: e.dma_start(out=af(fIOTA, 1), in_=E["iotap"]), dma="win")
    ld_ei = p.add("sync", lambda e: e.dma_start(out=af(fEIDS, NE, 0, 1), in_=E["eids"]), dma="win")
    ld_idb = p.add("gpsimd", lambda e: e.dma_start(out=ab(bIDENT, 128), in_=E["identf"]), dma="wing")
    ms_ones = p.add("gpsimd", lambda e: e.memset(ab(bONES, 128), 1.0))
    ms_onc = p.add("vector", lambda e: e.memset(af(fONC, 1), 1.0))
    ms_eps = p.add("vector", lambda e: e.memset(af(fEPSC, 1), EPS))
    mm_ei = p.add("tensor", lambda e: e.matmul(ovp[:, 272:288], af(fONES, 128, 0, 1), af(fEIDS, NE, 0, 1), start=True, stop=True), waits=[ld_on, ld_ei])
    ev_ei = p.add("vector", lambda e: e.tensor_copy(af(fEIDB, NE), ovp[:, 272:288]), waits=[mm_ei])

    fSM = fBIG
    ld_rs = p.add("sync", lambda e: e.dma_start(out=af(fSM, DV, 0, 1), in_=E["rms_scale"]), dma="win")
    mm_rs = p.add("tensor", lambda e: e.matmul(ovp[:, 0:256], af(fONES, 128, 0, 1), af(fSM, DV, 0, 1), start=True, stop=True), waits=[ld_on, ld_rs])
    ev_rs = p.add("vector", lambda e: e.tensor_scalar(af(fRMSB, DV), ovp[:, 0:256], 1.0 - LAMBDA_INIT, None, ALU.mult), waits=[mm_rs])

    fL = fBIG + 256
    fLs = fL + 512
    ldl = []
    for i, t in enumerate(("lq1", "lk1", "lq2", "lk2")):
        ldl.append(p.add("sync", lambda e, t=t, i=i: e.dma_start(out=af(fL + 128 * i, 128, 0, 8), in_=E[t]), dma="win"))
    p.add("vector", lambda e: e.tensor_tensor(af(fL, 128, 0, 8), af(fL, 128, 0, 8), af(fL + 128, 128, 0, 8), ALU.mult), waits=ldl[:2])
    p.add("vector", lambda e: e.tensor_tensor(af(fL + 256, 128, 0, 8), af(fL + 256, 128, 0, 8), af(fL + 384, 128, 0, 8), ALU.mult), waits=ldl[2:])
    rs1 = p.add("vector", lambda e: e.tensor_reduce(af(fLs, 1, 0, 8), af(fL, 128, 0, 8), AX.X, ALU.add))
    rs2 = p.add("vector", lambda e: e.tensor_reduce(af(fLs + 1, 1, 0, 8), af(fL + 256, 128, 0, 8), AX.X, ALU.add))
    ex12 = p.add("scalar", lambda e: e.activation(af(fLs + 2, 2, 0, 8), af(fLs, 2, 0, 8), AF.Exp), waits=[rs1, rs2])
    p.add("vector", lambda e: e.tensor_tensor(af(fLs + 4, 1, 0, 8), af(fLs + 3, 1, 0, 8), af(fLs + 2, 1, 0, 8), ALU.subtract), waits=[ex12])
    nl2 = p.add("vector", lambda e: e.tensor_scalar(af(fLs + 5, 1, 0, 8), af(fLs + 4, 1, 0, 8), -LAMBDA_INIT, None, ALU.add))
    trl = p.add("tensor", lambda e: e.transpose(ovp[0:1, 256:264], af(fLs + 5, 1, 0, 8), af(fIDENT, 8, 0, 8)), waits=[nl2, ev_rs, ld_id])
    evl = p.add("vector", lambda e: e.tensor_copy(af(fLs + 6, 8, 0, 1), ovp[0:1, 256:264]), waits=[trl])
    mml = p.add("tensor", lambda e: e.matmul(ovp[:, 264:272], af(fONES, 128, 0, 1), af(fLs + 6, 8, 0, 1), start=True, stop=True), waits=[evl])
    nlam = p.add("vector", lambda e: e.tensor_copy(af(fLAM, 8), ovp[:, 264:272]), waits=[mml])

    bXTB = bBIG
    bW = bXTB + 32768
    bW2 = bW + 16384
    bSTG = bW2 + 16384
    assert bSTG + 4096 <= 73728

    xchunks = [p.add("gpsimd",
                     lambda e, c=c: e.dma_start(out=ab(bXTB + 4096 * c, 4096), in_=E["XT"][c * 128:(c + 1) * 128, :]),
                     dma="wing") for c in range(8)]
    ldwq = p.add("gpsimd", lambda e: e.dma_start(out=ab(bW, 16384), in_=E["WqT"]), dma="wing")
    ldwk = p.add("gpsimd", lambda e: e.dma_start(out=ab(bW2, 16384), in_=E["WkT"]), dma="wing")

    vh_r = E["VH"].rearrange("(t p) o -> p t o", p=128)

    def proj_fm(dst, woff, wait_w, tag, stg_prev=None, ev_seed=None):
        dmas = []
        prev_stg = stg_prev
        prev_ev = dict(ev_seed) if ev_seed else {}
        for m in range(16):
            evs = []
            for n in range(8):
                bank = (m * 8 + n) % 2
                psl = (S1p[:, 0:512], S1p[:, 512:1024])[bank]
                mms = []
                for c in range(8):
                    w = []
                    if c == 0:
                        w = [wait_w] + xchunks
                        if prev_ev.get(bank) is not None:
                            w.append(prev_ev[bank])
                    mms.append(p.add(
                        "tensor",
                        lambda e, m=m, n=n, c=c, psl=psl, woff=woff: e.matmul(
                            psl, ab(woff + 2048 * c + 128 * m, 128),
                            ab(bXTB + 4096 * c + 512 * n, 512),
                            start=(c == 0), stop=(c == 7)),
                        waits=w, tag=f"{tag}mm"))
                w2 = [mms[-1]]
                if n == 0:
                    if dmas:
                        w2.append(dmas[-1])
                    elif prev_stg is not None:
                        w2.append(prev_stg)
                ev = p.add("scalar", lambda e, n=n, psl=psl: e.copy(ab(bSTG + 512 * n, 512), psl), waits=w2)
                prev_ev[bank] = ev
                evs.append(ev)
            dmas.append(p.add(
                "sync",
                lambda e, m=m, dst=dst: e.dma_start(out=dst[m * 128:(m + 1) * 128, :], in_=ab(bSTG, 4096)),
                waits=evs, dma="out", tag=f"{tag}dma"))
        return dmas, prev_ev

    qdmas, qev = proj_fm(E["QTh"], bW, ldwq, "q")
    ldwv = p.add("gpsimd", lambda e: e.dma_start(out=ab(bW, 16384), in_=E["WvT"]), waits=[qdmas[-1]], dma="wing")
    kdmas, _ = proj_fm(E["KTh"], bW2, ldwk, "k", stg_prev=qdmas[-1], ev_seed=qev)

    vdmas = []
    prev_ev = {}
    for m in range(32):
        evs = []
        for gi in range(4):
            bank = (m * 4 + gi) % 2
            psl = (S2p[:, 0:512], S2p[:, 512:1024])[bank]
            mms = []
            for c in range(8):
                w = []
                if c == 0:
                    w = [ldwv]
                    if prev_ev.get(bank) is not None:
                        w.append(prev_ev[bank])
                mms.append(p.add(
                    "tensor",
                    lambda e, m=m, gi=gi, c=c, psl=psl: e.matmul(
                        psl, ab(bXTB + 4096 * c + 128 * m, 128),
                        ab(bW + 2048 * c + 512 * gi, 512),
                        start=(c == 0), stop=(c == 7)),
                    waits=w, tag="vmm"))
            w2 = [mms[-1]]
            if gi == 0:
                w2.append(vdmas[-1] if vdmas else kdmas[-1])
            ev = p.add("scalar", lambda e, gi=gi, psl=psl: e.copy(ab(bSTG + 512 * gi, 512), psl), waits=w2)
            prev_ev[bank] = ev
            evs.append(ev)
        vdmas.append(p.add(
            "sync",
            lambda e, m=m: e.dma_start(out=vh_r[:, m, :], in_=ab(bSTG, 2048)),
            waits=evs, dma="out"))

    if "stop_a0" in dbg:
        p.add("sync", lambda e: e.nop(), waits=qdmas + kdmas + vdmas)
        return

    bQ = bBIG
    bK = [bBIG + 8192, bBIG + 16384]
    bV = [bBIG + 24576, bBIG + 32768]
    bE1 = [bBIG + 40960, bBIG + 45056, bBIG + 49152]
    bE2 = [bBIG + 53248, bBIG + 57344]
    bPT = bBIG + 61440
    bOST = bPT + 4096
    bOTM = bOST + 1024
    assert bOTM + 512 <= 73728

    fD1 = [fBIG + 1024, fBIG + 1040]
    fD2 = [fBIG + 1028, fBIG + 1044]
    fDEN = [fBIG + 1056, fBIG + 1072, fBIG + 1088]
    fON = [fBIG + 1100, fBIG + 1100 + 256]
    fSQ = fBIG + 1100 + 512

    qt_r = E["QTh"].rearrange("(c p) t -> p c t", p=128)
    kt_r = E["KTh"].rearrange("(c p) t -> p c t", p=128)

    def load_head(h, buf, waits):
        lq = p.add("sync", lambda e, h=h: e.dma_start(
            out=AB[:, bQ:bQ + 8192].rearrange("p (c t) -> p c t", c=2),
            in_=qt_r[:, 2 * h:2 * h + 2, :]), dma="win",
            waits=waits + (qdmas if h == 0 else []))
        lk = p.add("sync", lambda e, h=h, buf=buf: e.dma_start(
            out=AB[:, bK[buf]:bK[buf] + 8192].rearrange("p (c t) -> p c t", c=2),
            in_=kt_r[:, 2 * h:2 * h + 2, :]), dma="win",
            waits=(kdmas if h == 0 else []))
        lv = p.add("sync", lambda e, h=h, buf=buf: e.dma_start(
            out=AB[:, bV[buf]:bV[buf] + 8192].rearrange("p (t o) -> p t o", t=32),
            in_=vh_r[:, :, 256 * h:256 * h + 256]), dma="win",
            waits=(vdmas if h == 0 else []))
        return [lq, lk, lv]

    NU = H * 32
    st = {
        "loads": {0: load_head(0, 0, [])},
        "exp_last": [None, None],
        "comb": {}, "exps": {}, "dens": {}, "tr_last": {}, "pv": {},
        "epi": {}, "sq": {}, "otm": {}, "otr": {}, "ose": {}, "otdma": {},
        "head_last_exp": {},
        "trev_last": [None, None],
    }
    ot_dmas = []

    def emit_head_prefetch(u):
        h, qt = u // 32, u % 32
        if qt == 0 and h >= 1:
            lq = p.add("sync", lambda e, h=h: e.dma_start(
                out=AB[:, bQ:bQ + 8192].rearrange("p (c t) -> p c t", c=2),
                in_=qt_r[:, 2 * h:2 * h + 2, :]), dma="win",
                waits=[st["head_last_exp"][h - 1]])
            st["loads"][h].append(lq)
        if qt == 2 and h + 1 < H:
            w = []
            if h >= 1:
                w.append(st["pv"][(h - 1) * 32 + 31])
            lk = p.add("sync", lambda e, h1=h + 1, buf=1 - (h % 2): e.dma_start(
                out=AB[:, bK[buf]:bK[buf] + 8192].rearrange("p (c t) -> p c t", c=2),
                in_=kt_r[:, 2 * h1:2 * h1 + 2, :]), dma="win",
                waits=w)
            lv = p.add("sync", lambda e, h1=h + 1, buf=1 - (h % 2): e.dma_start(
                out=AB[:, bV[buf]:bV[buf] + 8192].rearrange("p (t o) -> p t o", t=32),
                in_=vh_r[:, :, 256 * h1:256 * h1 + 256]), dma="win")
            st["loads"][h + 1] = [lk, lv]


    def emit_S_span(u, j):
        h, qt = u // 32, u % 32
        buf = h % 2
        sl = u % 2
        loads = st["loads"][h]
        oQ, oK = bQ, bK[buf]
        exps = st["exps"].setdefault(u, {1: [], 2: []})
        for br, Sp, qoff, koff, bEo, fDx in (
                (1, S1p, oQ, oK, bE1[u % 3], fD1[sl]),
                (2, S2p, oQ + 4096, oK + 4096, bE2[sl], fD2[sl])):
            mms = []
            for half in range(2):
                kb = 2 * j + half
                w = []
                if kb == 0:
                    w = list(loads)
                    if st["exp_last"][br - 1] is not None:
                        w.append(st["exp_last"][br - 1])
                elif half == 0:
                    w.append(exps[br][j - 1])
                mms.append(p.add(
                    "tensor",
                    lambda e, qt=qt, kb=kb, Sp=Sp, qoff=qoff, koff=koff, half=half: e.matmul(
                        Sp[:, 512 * half:512 * half + 512],
                        ab(qoff + 128 * qt, 128), ab(koff + 512 * kb, 512),
                        start=True, stop=True),
                    waits=w, tag=f"s{br}u{u}"))
            ew = list(mms)
            if j == 0:
                if br == 1:
                    if (u - 3) in st["tr_last"]:
                        ew.append(st["tr_last"][u - 3])
                    if (u - 2) in st["dens"]:
                        ew.append(st["dens"][u - 2][0])
                if br == 2 and (u - 2) in st["comb"]:
                    ew.append(st["comb"][u - 2])
            exps[br].append(p.add(
                "scalar",
                lambda e, j=j, Sp=Sp, bEo=bEo, fDx=fDx: e.activation(
                    ab(bEo + 1024 * j, 1024), Sp[:, :], AF.Exp, scale=SCALE,
                    accum_out=af(fDx + j, 1)),
                waits=ew, tag=f"e{br}u{u}"))
            if j == 3:
                st["exp_last"][br - 1] = exps[br][-1]
                if qt == 31 and br == 2:
                    st["head_last_exp"][h] = exps[2][-1]

    def emit_TR(u, ktg_lo, ktg_hi):
        sl = u % 3
        c2 = st["comb"][u]
        for ktg in range(ktg_lo, ktg_hi + 1):
            dst = (trp0, trp1)[ktg % 2]
            trs = []
            for i in range(8):
                kt = 8 * ktg + i
                w = []
                if i == 0:
                    w = [c2, ld_idb]
                    if st["trev_last"][ktg % 2] is not None:
                        w.append(st["trev_last"][ktg % 2])
                trs.append(p.add(
                    "tensor",
                    lambda e, kt=kt, dst=dst, sl=sl: e.transpose(
                        dst[:, 128 * (kt % 8):128 * (kt % 8) + 128],
                        ab(bE1[sl] + 128 * kt, 128), ab(bIDENT, 128)),
                    waits=w, tag=f"tru{u}"))
            w2 = [trs[-1]]
            if (u - 1) in st["pv"]:
                w2.append(st["pv"][u - 1])
            tev = p.add(
                "vector",
                lambda e, ktg=ktg, dst=dst: e.tensor_copy(ab(bPT + 1024 * ktg, 1024), dst[:, :]),
                waits=w2, tag=f"trevu{u}")
            st["trev_last"][ktg % 2] = tev
            st.setdefault("trevs", {}).setdefault(u, []).append(tev)
            if ktg == 3:
                st["tr_last"][u] = trs[-1]

    def emit_PV(u, half):
        h = u // 32
        buf = h % 2
        sl = u % 2
        oV = bV[buf]
        trevs = st["trevs"][u]
        pvs = st.setdefault("pvs", {}).setdefault(u, [])
        for kt in range(16 * half, 16 * half + 16):
            w = []
            if kt % 8 == 0:
                w.append(trevs[kt // 8])
            if kt == 0:
                if (u - 1) in st["epi"]:
                    w.append(st["epi"][u - 1])
                if u == 0:
                    w += [nlam, ev_rs, ev_ei]
            pvs.append(p.add(
                "tensor",
                lambda e, kt=kt, sl=sl, oV=oV: e.matmul(
                    ovp[:, 256 * sl:256 * sl + 256],
                    ab(bPT + 128 * kt, 128), ab(oV + 256 * kt, 256),
                    start=(kt == 0), stop=(kt == 31)),
                waits=w, tag=f"pvu{u}"))
        if half == 1:
            st["pv"][u] = pvs[-1]

    def emit_epilogue(u):
        sl = u % 2
        fDx = fDEN[u % 3]
        fo = fON[sl]
        ms, r1 = st["dens"][u]
        ew = [st["pv"][u], r1]
        if (u - 2) in st["sq"]:
            ew.append(st["sq"][u - 2])
        epi = p.add("vector", lambda e, sl=sl, fo=fo, fDx=fDx: e.tensor_scalar(
            af(fo, 256), ovp[:, 256 * sl:256 * sl + 256], af(fDx + 4, 1), None, ALU.mult),
            waits=ew, tag=f"epiu{u}")
        st["epi"][u] = epi
        sqm = p.add("scalar", lambda e, fo=fo, fDx=fDx: e.activation(
            af(fSQ, 256), af(fo, 256), AF.Square, accum_out=af(fDx + 7, 1)),
            waits=[epi], tag=f"squ{u}")
        st["sq"][u] = sqm
        ln_ = p.add("scalar", lambda e, fDx=fDx: e.activation(
            af(fDx + 8, 1), af(fDx + 7, 1), AF.Ln, bias=af(fEPSC, 1), scale=1.0 / DV), waits=[ms_eps])
        rinv = p.add("scalar", lambda e, fDx=fDx: e.activation(af(fDx + 9, 1), af(fDx + 8, 1), AF.Exp, scale=-0.5), tag=f"rinvu{u}")
        p.add("vector", lambda e, fo=fo, fDx=fDx: e.tensor_scalar(af(fo, 256), af(fo, 256), af(fDx + 9, 1), None, ALU.mult), waits=[rinv])
        otm_w = [ev_rs]
        if (u - 2) in st["otr"]:
            otm_w.append(st["otr"][u - 2][-1])
        otm = p.add("vector", lambda e, sl=sl, fo=fo: e.tensor_tensor(
            ab(bOTM + 256 * sl, 256), af(fo, 256), af(fRMSB, 256), ALU.mult), waits=otm_w, tag=f"otmu{u}")
        st["otm"][u] = otm

    def emit_S_dve(u):
        h, qt = u // 32, u % 32
        sl = u % 2
        fDx = fDEN[u % 3]
        exps = st["exps"][u]
        p.add("vector", lambda e, sl=sl, fDx=fDx: e.tensor_reduce(
            af(fDx, 2), AFr[:, fD1[sl]:fD1[sl] + 8].rearrange("p (b x) -> p b x", b=2),
            AX.X, ALU.add), waits=exps[1] + exps[2])
        rd = p.add("vector", lambda e, fDx=fDx: e.reciprocal(af(fDx + 4, 2), af(fDx, 2)))
        p.add("vector", lambda e, fDx=fDx: e.tensor_tensor(af(fDx + 3, 1), af(fDx, 1), af(fDx + 5, 1), ALU.mult))
        ms = p.add("vector", lambda e, h=h, fDx=fDx: e.tensor_tensor(af(fDx + 6, 1), af(fDx + 3, 1), af(fLAM + h, 1), ALU.mult), waits=[nlam])
        st["dens"][u] = (ms, rd)
        c1 = p.add("vector", lambda e, sl=sl, fDx=fDx: e.tensor_scalar(ab(bE2[sl], 4096), ab(bE2[sl], 4096), af(fDx + 6, 1), None, ALU.mult), tag=f"c1u{u}")
        c2 = p.add("gpsimd", lambda e, sl=sl, s3=u % 3: e.tensor_tensor(ab(bE1[s3], 4096), ab(bE1[s3], 4096), ab(bE2[sl], 4096), ALU.add),
                   waits=[c1, exps[1][-1]], tag=f"c2u{u}")
        st["comb"][u] = c2

    def stage_OTR(u):
        h, qt = u // 32, u % 32
        sl = u % 2
        otrs = []
        for chv in range(2):
            w = [st["otm"][u]]
            if qt % 4 == 0 and chv == 0 and (u - 4) in st["ose"]:
                w += st["ose"][u - 4]
            otrs.append(p.add(
                "tensor",
                lambda e, qt=qt, chv=chv, sl=sl: e.transpose(
                    pob[:, 256 * (qt % 4) + 128 * chv:256 * (qt % 4) + 128 * chv + 128],
                    ab(bOTM + 256 * sl + 128 * chv, 128), ab(bIDENT, 128)),
                waits=w, tag=f"otru{u}"))
        st["otr"][u] = otrs
        if qt % 4 == 3:
            qb = u - 3
            grp = (qt - 3) // 4
            pob_v = pob[:, :].rearrange("p (q x) -> p q x", q=4)
            oses = []
            all_otrs = [o for i in range(4) for o in st["otr"][u - 3 + i]]
            for ch in range(2):
                w = list(all_otrs)
                prevd = st["otdma"].get((h, grp - 1), st["otdma"].get((h - 1, 7)))
                if prevd is not None:
                    w.append(prevd[ch])
                oses.append(p.add(
                    "vector",
                    lambda e, ch=ch: e.tensor_copy(
                        AB[:, bOST + 512 * ch:bOST + 512 * ch + 512].rearrange("p (q x) -> p q x", q=4),
                        pob_v[:, :, 128 * ch:128 * ch + 128]),
                    waits=w, tag=f"oseu{u}"))
            st["ose"][qb] = oses
            pair = []
            for ch in range(2):
                dma = p.add(
                    "sync",
                    lambda e, h=h, grp=grp, ch=ch: e.dma_start(
                        out=E["OT"][h * 256 + ch * 128:h * 256 + ch * 128 + 128, grp * 512:grp * 512 + 512],
                        in_=ab(bOST + 512 * ch, 512)),
                    waits=[oses[ch]], dma="out", tag=f"otdma{h}.{grp}")
                ot_dmas.append(dma)
                pair.append(dma)
            st["otdma"][(h, grp)] = pair

    nheads = H
    for d in dbg:
        if isinstance(d, str) and d.startswith("nh"):
            nheads = int(d[2:])
    NU = nheads * 32

    for u in range(NU + 4):
        cur = u < NU
        prv = 2 <= u <= NU + 1
        if cur:
            emit_head_prefetch(u)
            emit_S_span(u, 0)
        if 4 <= u <= NU + 3:
            stage_OTR(u - 4)
        if prv:
            emit_TR(u - 2, 0, 0)
        if cur:
            emit_S_span(u, 1)
        if prv:
            emit_TR(u - 2, 1, 2)
        if cur:
            emit_S_span(u, 2)
        if prv:
            emit_TR(u - 2, 3, 3)
            emit_PV(u - 2, 0)
        if cur:
            emit_S_span(u, 3)
        if prv:
            emit_PV(u - 2, 1)
            emit_epilogue(u - 2)
        if cur:
            emit_S_dve(u)

    if "stop_a1" in dbg:
        p.add("sync", lambda e: e.nop(), waits=ot_dmas)
        return

    bWO = bBIG
    bOB = [bWO + 16384, bWO + 24576]
    bRB = bWO + 32768
    bRSQ = bRB + 4096
    bX1B = bRSQ + 4096
    bTM = bX1B + 4096
    assert bTM + 2048 <= 73728
    fXT = fBIG + 1024
    fR = fXT + 4096
    fX1 = fR + 4096
    fT = fX1 + 4096
    assert fT + 1536 <= 15488

    ldwo = p.add("gpsimd", lambda e: e.dma_start(out=ab(bWO, 16384), in_=E["WoT"]),
        dma="wing", waits=[st["head_last_exp"][H - 1], st["pv"][NU - 1]])
    ot_r = E["OT"].rearrange("(c p) t -> p c t", p=128)
    xt_r = E["XT"].rearrange("(c p) t -> p c t", p=128)
    x1f_r = E["X1F"].rearrange("(c p) t -> p c t", p=128)
    x1b_r = E["X1B"].rearrange("(c p) t -> p c t", p=128)

    prev = {}
    x1_dmas = []
    for n in range(8):
        bufn = n % 2
        lob_w = list(ot_dmas) if n == 0 else ([prev[("womm", n - 2)]] if ("womm", n - 2) in prev else [])
        lob = p.add("sync", lambda e, n=n, bufn=bufn: e.dma_start(
            out=AB[:, bOB[bufn]:bOB[bufn] + 8192].rearrange("p (c t) -> p c t", c=16),
            in_=ot_r[:, :, 512 * n:512 * n + 512]), dma="win", waits=lob_w)
        lxt = p.add("sync", lambda e, n=n: e.dma_start(
            out=AFr[:, fXT:fXT + 4096].rearrange("p (c t) -> p c t", c=8),
            in_=xt_r[:, :, 512 * n:512 * n + 512]), dma="win",
            waits=[prev[("rev_last", n - 1)]] if ("rev_last", n - 1) in prev else [])
        revs = []
        for m in range(8):
            psl = (S1p[:, 0:512], S1p[:, 512:1024])[m % 2]
            mms = []
            for ec in range(16):
                w = []
                if ec == 0:
                    w = [lob, ldwo]
                    if ("rev", n, m - 2) in prev:
                        w.append(prev[("rev", n, m - 2)])
                    if m < 2 and ("rev", n - 1, 6 + m) in prev:
                        w.append(prev[("rev", n - 1, 6 + m)])
                mms.append(p.add(
                    "tensor",
                    lambda e, m=m, ec=ec, psl=psl, bufn=bufn: e.matmul(
                        psl, ab(bWO + 1024 * ec + 128 * m, 128), ab(bOB[bufn] + 512 * ec, 512),
                        start=(ec == 0), stop=(ec == 15)),
                    waits=w, tag=f"wo{n}.{m}"))
            prev[("womm", n)] = mms[-1]
            rev = p.add("vector", lambda e, m=m, psl=psl: e.tensor_tensor(
                af(fR + 512 * m, 512), psl, af(fXT + 512 * m, 512), ALU.add),
                waits=[mms[-1], lxt], tag=f"rev{n}.{m}")
            prev[("rev", n, m)] = rev
            revs.append(rev)
        rbfs, rsqs = [], []
        for m in range(8):
            rbfs.append(p.add("scalar", lambda e, m=m: e.copy(ab(bRB + 512 * m, 512), af(fR + 512 * m, 512)), waits=[revs[m]]))
            rsqs.append(p.add("scalar", lambda e, m=m: e.activation(ab(bRSQ + 512 * m, 512), af(fR + 512 * m, 512), AF.Square), waits=[revs[m]]))
        summ, sqmm = [], []
        for c in range(8):
            w1 = [rbfs[c], ms_ones]
            w2 = [rsqs[c]]
            if c == 0 and ("stat", n - 1) in prev:
                w1.append(prev[("stat", n - 1)])
                w2.append(prev[("stat", n - 1)])
            summ.append(p.add("tensor", lambda e, c=c: e.matmul(
                S2p[:, 0:512], ab(bONES, 128), ab(bRB + 512 * c, 512), start=(c == 0), stop=(c == 7)), waits=w1))
            sqmm.append(p.add("tensor", lambda e, c=c: e.matmul(
                S2p[:, 512:1024], ab(bONES, 128), ab(bRSQ + 512 * c, 512), start=(c == 0), stop=(c == 7)), waits=w2))
        mb = p.add("vector", lambda e: e.tensor_scalar(af(fT, 512), S2p[:, 0:512], 1.0 / D, None, ALU.mult), waits=[summ[-1]])
        t = p.add("vector", lambda e: e.tensor_scalar(af(fT + 512, 512), S2p[:, 512:1024], 1.0 / D, EPS, ALU.mult, ALU.add), waits=[sqmm[-1]])
        prev[("stat", n)] = t
        p.add("vector", lambda e: e.tensor_tensor(af(fT + 1024, 512), af(fT, 512), af(fT, 512), ALU.mult))
        var = p.add("vector", lambda e: e.tensor_tensor(af(fT + 1024, 512), af(fT + 512, 512), af(fT + 1024, 512), ALU.subtract))
        lnv = p.add("scalar", lambda e: e.activation(af(fT + 512, 512), af(fT + 1024, 512), AF.Ln), waits=[var])
        rstd = p.add("scalar", lambda e: e.activation(af(fT + 1024, 512), af(fT + 512, 512), AF.Exp, scale=-0.5), waits=[lnv])
        x1f_ops, x1b_ops = [], []
        for m in range(8):
            wv = [rstd]
            if ("x1dma", n - 1) in prev and m == 0:
                wv.append(prev[("x1dma", n - 1)])
            p.add("vector", lambda e, m=m: e.tensor_tensor(af(fX1 + 512 * m, 512), af(fR + 512 * m, 512), af(fT, 512), ALU.subtract), waits=wv)
            x1c = p.add("vector", lambda e, m=m: e.tensor_tensor(af(fX1 + 512 * m, 512), af(fX1 + 512 * m, 512), af(fT + 1024, 512), ALU.mult))
            wb = [x1c]
            if m == 0 and ("x1dmab", n - 1) in prev:
                wb.append(prev[("x1dmab", n - 1)])
            x1b = p.add("scalar", lambda e, m=m: e.copy(ab(bX1B + 512 * m, 512), af(fX1 + 512 * m, 512)), waits=wb)
            x1f_ops.append(x1c)
            x1b_ops.append(x1b)
        rl = p.add("vector", lambda e: e.tensor_copy(af(fLAM, 1), af(fLAM, 1)))
        prev[("rev_last", n)] = rl
        d1 = p.add("sync", lambda e, n=n: e.dma_start(
            out=x1f_r[:, :, 512 * n:512 * n + 512],
            in_=AFr[:, fX1:fX1 + 4096].rearrange("p (c t) -> p c t", c=8)),
            waits=x1f_ops, dma="out")
        d2 = p.add("sync", lambda e, n=n: e.dma_start(
            out=x1b_r[:, :, 512 * n:512 * n + 512],
            in_=AB[:, bX1B:bX1B + 4096].rearrange("p (c t) -> p c t", c=8)),
            waits=x1b_ops, dma="out")
        prev[("x1dma", n)] = d1
        prev[("x1dmab", n)] = d2
        x1_dmas += [d1, d2]
        for tp in range(4):
            dstp = (trp0, trp1)[tp % 2]
            trsb = []
            for c in range(8):
                w = [x1b_ops[c]]
                if ("x1tmev", n, tp - 2) in prev:
                    w.append(prev[("x1tmev", n, tp - 2)])
                elif tp < 2 and ("x1tmev", n - 1, tp + 2) in prev:
                    w.append(prev[("x1tmev", n - 1, tp + 2)])
                trsb.append(p.add("tensor", lambda e, c=c, tp=tp, dstp=dstp: e.transpose(
                    dstp[:, 128 * c:128 * c + 128], ab(bX1B + 512 * c + 128 * tp, 128), ab(bIDENT, 128)),
                    waits=w))
            w2 = [trsb[-1]]
            if ("x1tmdma", n, tp - 2) in prev:
                w2.append(prev[("x1tmdma", n, tp - 2)])
            elif tp < 2 and ("x1tmdma", n - 1, tp + 2) in prev:
                w2.append(prev[("x1tmdma", n - 1, tp + 2)])
            tev = p.add("vector", lambda e, tp=tp, dstp=dstp: e.tensor_copy(
                ab(bTM + 1024 * (tp % 2), 1024), dstp[:, :]), waits=w2)
            prev[("x1tmev", n, tp)] = tev
            dtm = p.add("sync", lambda e, n=n, tp=tp: e.dma_start(
                out=E["X1TM"][512 * n + 128 * tp:512 * n + 128 * tp + 128, :],
                in_=ab(bTM + 1024 * (tp % 2), 1024)), waits=[tev], dma="out")
            prev[("x1tmdma", n, tp)] = dtm
            x1_dmas.append(dtm)

    if "stop_b" in dbg:
        p.add("sync", lambda e: e.nop(), waits=x1_dmas)
        return

    C = 1024
    CC = 2048
    bXB = bBIG
    bSH = bXB + 8192
    bINT = bSH + 8192
    bDWN = bINT + 8192
    bWG = [bDWN + 8192, bDWN + 20480, bDWN + 32768]
    bG = bDWN + 45056
    bRW = bG + 512
    bSHF = bRW + 128
    bZ = bXB
    bXO = bINT + 4096
    assert bSHF + 1024 <= 73728
    fLG = fBIG + 1024
    fLGT = fLG + 1024
    fPRB = fLGT + 128
    fPM = fPRB + 112
    fMB = fPM + 16
    fCW = fMB + 16
    fMET = fCW + 64
    fROFF = fMET + 4
    fMETA = fROFF + 16
    fSC = fMETA + 32
    fX1Q = fSC + 2048
    fR2 = fX1Q + 1024
    fCMB = fR2 + 1024
    fLNS = fCMB + 1024
    fRAC = fLNS + 1024
    fT2 = fRAC + 1024
    fGB = fT2 + 64
    fSLC = fGB + 1024
    fLN2 = [fLNS, fSLC + 16]
    assert fSLC + 16 + 1024 <= 15232

    ldrw = p.add("gpsimd", lambda e: e.dma_start(out=ab(bRW, 128), in_=E["rw"]), dma="wing",
        waits=[prev[("womm", 7)]])
    ld_lt = p.add("sync", lambda e: e.dma_start(out=af(fLTRI, 128), in_=E["ltri"]), dma="win",
        waits=[prev[("x1dma", 7)], prev[("x1dmab", 7)]])

    msz = p.add("gpsimd", lambda e: e.memset(ab(bZ, 4096), 0.0), waits=[prev[("womm", 7)]])
    zdmas = []
    for blk in range(64):
        zdmas.append(p.add("sync", lambda e, blk=blk: e.dma_start(
            out=E["XCOMP"][512 * blk:512 * (blk + 1), :].rearrange("(b p) c -> p b c", p=128),
            in_=AB[:, bZ:bZ + 4096].rearrange("p (b c) -> p b c", b=4)), waits=[msz], dma="out"))
    zro = p.add("vector", lambda e: e.memset(af(fROFF, NE, 0, 1), 0.0))

    y_dmas = []
    cp = {}
    sc_dmas = []
    sh_dmas = []
    x1b_r2 = x1b_r
    for Q in range(4):
        tq = 1024 * Q
        lxb = p.add("sync", lambda e, tq=tq: e.dma_start(
            out=AB[:, bXB:bXB + 8192].rearrange("p (c t) -> p c t", c=8),
            in_=x1b_r2[:, :, tq:tq + 1024]), dma="win",
            waits=([prev[("x1dmab", n)] for n in range(8)] + zdmas if Q == 0 else [cp[("shdma", Q - 1)]]))
        lgevs = []
        for tt in range(2):
            mms = []
            for c in range(8):
                w = []
                if c == 0:
                    w = [lxb, ldrw]
                    if tt == 1:
                        w.append(lgevs[0])
                    elif ("ovp_free",) in cp:
                        w.append(cp[("ovp_free",)])
                    elif Q == 0:
                        w.append(st["epi"].get(NU - 1))
                mms.append(p.add("tensor", lambda e, tt=tt, c=c: e.matmul(
                    ovp[0:16, 0:512], ab(bRW + 16 * c, 16, 0, 128), ab(bXB + 1024 * c + 512 * tt, 512),
                    start=(c == 0), stop=(c == 7)), waits=w))
            lgevs.append(p.add("vector", lambda e, tt=tt: e.tensor_copy(af(fLG + 512 * tt, 512, 0, 16), ovp[0:16, 0:512]), waits=[mms[-1]]))
        trls = []
        for j in range(8):
            w = [lgevs[-1]] if j == 0 else []
            trls.append(p.add("tensor", lambda e, j=j: e.transpose(
                ovp[:, 16 * j:16 * j + 16], af(fLG + 128 * j, 128, 0, 16), af(fIDENT, 16, 0, 16)), waits=w))
        lgt = p.add("vector", lambda e: e.tensor_copy(af(fLGT, 128), ovp[:, 0:128]), waits=[trls[-1]])
        for j in range(8):
            g = Q * 8 + j
            p.add("vector", lambda e, j=j: e.reduce_max(af(fPRB, 1), af(fLGT + 16 * j, 16), AX.X))
            nm1 = p.add("vector", lambda e: e.tensor_scalar(af(fPRB + 5, 1), af(fPRB, 1), -1.0, None, ALU.mult))
            ex = p.add("scalar", lambda e, j=j: e.activation(af(fPRB + 16, 16), af(fLGT + 16 * j, 16), AF.Exp, bias=af(fPRB + 5, 1), accum_out=af(fPRB + 1, 1)), waits=[nm1])
            p.add("vector", lambda e: e.reciprocal(af(fPRB + 2, 1), af(fPRB + 1, 1)), waits=[ex])
            p.add("vector", lambda e: e.tensor_scalar(af(fPRB + 32, 16), af(fPRB + 16, 16), af(fPRB + 2, 1), None, ALU.mult))
            p.add("vector", lambda e: e.reduce_max(af(fPRB + 3, 1), af(fPRB + 32, 16), AX.X))
            eqo = p.add("vector", lambda e: e.tensor_scalar(af(fPRB + 48, 16), af(fPRB + 32, 16), af(fPRB + 3, 1), None, ALU.is_equal))
            p.add("vector", lambda e: e.tensor_tensor(af(fPRB + 64, 16), af(fPRB + 32, 16), af(fPRB + 48, 16), ALU.mult))
            p.add("vector", lambda e: e.tensor_tensor(af(fPRB + 64, 16), af(fPRB + 32, 16), af(fPRB + 64, 16), ALU.subtract))
            p.add("vector", lambda e: e.reduce_max(af(fPRB + 4, 1), af(fPRB + 64, 16), AX.X))
            geo = p.add("vector", lambda e: e.tensor_scalar(af(fMB, 16), af(fPRB + 32, 16), af(fPRB + 4, 1), None, ALU.is_ge))
            w0 = [geo, ld_lt, ld_on, ms_onc]
            if ("cmp_mm",) in cp:
                w0.append(cp[("cmp_mm",)])
            mmro = p.add("tensor", lambda e: e.matmul(ovp[:, 0:16], af(fONES, NE * 0 + 128, 0, 1), af(fROFF, NE, 0, 1), start=True, stop=False), waits=w0 + [zro])
            mmpf = p.add("tensor", lambda e: e.matmul(ovp[:, 0:16], af(fLTRI, 128), af(fMB, 16), start=False, stop=True))
            mmtot = p.add("tensor", lambda e: e.matmul(ovp[0:1, 16:32], af(fONC, 1), af(fMB, 16), start=True, stop=True))
            evpm = p.add("vector", lambda e: e.tensor_copy(af(fPM, 16), ovp[:, 0:16]), waits=[mmpf])
            roff = p.add("vector", lambda e: e.tensor_tensor(af(fROFF, NE, 0, 1), af(fROFF, NE, 0, 1), ovp[0:1, 16:32], ALU.add), waits=[mmtot])
            cp[("cmp_mm",)] = roff
            scs = []
            for rk, (mname, moff, cvoff) in enumerate((("A", fPRB + 48, fPRB + 3), ("B", fPRB + 64, fPRB + 4))):
                if rk == 1:
                    p.add("vector", lambda e: e.tensor_tensor(af(fPRB + 64, 16), af(fMB, 16), af(fPRB + 48, 16), ALU.subtract))
                p.add("vector", lambda e, moff=moff: e.tensor_tensor(af(fCW, 16), af(moff, 16), af(fEIDB, 16), ALU.mult), waits=[ev_ei])
                p.add("vector", lambda e: e.tensor_reduce(af(fCW + 32, 1), af(fCW, 16), AX.X, ALU.add))
                p.add("vector", lambda e, moff=moff: e.tensor_tensor(af(fCW + 16, 16), af(moff, 16), af(fPM, 16), ALU.mult), waits=[evpm])
                p.add("vector", lambda e: e.tensor_reduce(af(fCW + 33, 1), af(fCW + 16, 16), AX.X, ALU.add))
                p.add("vector", lambda e: e.tensor_scalar(af(fCW + 33, 1), af(fCW + 33, 1), float(CC - 1), None, ALU.min))
                slf = p.add("vector", lambda e: e.tensor_scalar(af(fCW + 34, 1), af(fCW + 32, 1), float(CC), af(fCW + 33, 1), ALU.mult, ALU.add))
                wsli = [slf]
                if ("scx", g - 1, rk) in cp:
                    wsli.append(cp[("scx", g - 1, rk)])
                sli = p.add("vector", lambda e, rk=rk: e.tensor_copy(AI[:, rk:rk + 1], af(fCW + 34, 1)), waits=wsli)
                wmv = [cp[("slcv", g - 1)]] if rk == 0 and ("slcv", g - 1) in cp else []
                p.add("vector", lambda e, rk=rk: e.tensor_copy(af(fMET + 2 * rk, 1), af(fCW + 34, 1)), waits=wmv)
                mcv = p.add("vector", lambda e, rk=rk, cvoff=cvoff: e.tensor_copy(af(fMET + 2 * rk + 1, 1), af(cvoff, 1)))
                wsc = [sli] + (zdmas if g == 0 else [])
                if ("scx", g - 2, rk) in cp:
                    wsc.append(cp[("scx", g - 2, rk)])
                if rk == 0:
                    wst = [prev[("x1tmdma", g // 4, g % 4)]]
                    if ("scx", g - 2, 1) in cp:
                        wst.append(cp[("scx", g - 2, 1)])
                    cp[("xst", g)] = p.add("sync", lambda e, g=g: e.dma_start(
                        out=ab(bDWN + 1024 * (g % 2), 1024),
                        in_=E["X1TM"][g * 128:(g + 1) * 128, :]), waits=wst, dma="win")
                if SER_SC and ("ser_sc",) in cp:
                    wsc.append(cp[("ser_sc",)])
                scx = p.add("gpsimd", lambda e, rk=rk, g=g: e.indirect_dma_start(
                    E["XCOMP"], bass.IndirectOffsetOnAxis(AI[:, rk:rk + 1], 0),
                    ab(bDWN + 1024 * (g % 2), 1024), None), waits=wsc + [cp[("xst", g)]], dma="wsc")
                if SER_SC:
                    cp[("ser_sc",)] = scx
                cp[("scx", g, rk)] = scx
                scs.append(scx)
            slcv = p.add("sync", lambda e, g=g: e.dma_start(
                out=E["SLCV"][g * 128:(g + 1) * 128, :], in_=AFr[:, fMET:fMET + 4]),
                waits=[mcv], dma="out")
            cp[("slcv", g)] = slcv
            sc_dmas += scs
            if "stop_cmp0" in dbg and g == 0:
                dws = scs + [roff]
                dd = []
                dd.append(p.add("sync", lambda e: e.dma_start(out=E["DBG_CMP"][:, 0:112], in_=af(fPRB, 112)), waits=dws, dma="out"))
                dd.append(p.add("sync", lambda e: e.dma_start(out=E["DBG_CMP"][:, 112:128], in_=af(fPM, 16)), waits=dws, dma="out"))
                dd.append(p.add("sync", lambda e: e.dma_start(out=E["DBG_CMP"][:, 128:144], in_=af(fMB, 16)), waits=dws, dma="out"))
                dd.append(p.add("sync", lambda e: e.dma_start(out=E["DBG_CMP"][:, 144:208], in_=af(fCW, 64)), waits=dws, dma="out"))
                dd.append(p.add("sync", lambda e: e.dma_start(out=E["DBG_CMP"][0:1, 208:224], in_=af(fROFF, NE, 0, 1)), waits=dws, dma="out"))
                dd.append(p.add("sync", lambda e: e.dma_start(out=E["DBG_AI"][:, 0:2], in_=AI[:, 0:2]), waits=dws, dma="out"))
                dd.append(p.add("sync", lambda e: e.dma_start(out=E["DBG_CMP"][:, 224:240], in_=af(fLTRI, 16)), waits=dws, dma="out"))
                dd.append(p.add("sync", lambda e: e.dma_start(out=E["DBG_CMP"][:, 240:241], in_=AFr[:, fLTRI + 127:fLTRI + 128], allow_slow_non_contiguous=True), waits=dws, dma="out"))
                dd.append(p.add("sync", lambda e: e.dma_start(out=E["DBG_CMP"][0:1, 241:242], in_=af(fONC, 1, 0, 1)), waits=dws, dma="out"))
                dd.append(p.add("sync", lambda e: e.dma_start(out=E["DBG_CMP"][0:1, 242:250], in_=af(fONES, 8, 0, 1)), waits=dws, dma="out"))
                p.add("sync", lambda e: e.nop(), waits=dd)
                return
        lsg = p.add("gpsimd", lambda e: e.dma_start(out=ab(bWG[0], 8192), in_=E["sgw"]), dma="wing",
            waits=([ldrw, prev[("x1dmab", 7)], prev[("x1dma", 7)]] if Q == 0 else [cp[("shdn_last",)]]))
        lsu = p.add("gpsimd", lambda e: e.dma_start(out=ab(bWG[1], 8192), in_=E["suw"]), dma="wing")
        last_sil = cp.get(("lnb_pe_done",))
        last_mi = cp.get(("sh_mi_last",))
        gu_done = []
        for fm in range(8):
            for tt in range(2):
                gm = []
                for c in range(8):
                    w = []
                    if c == 0:
                        w = [lsg, lxb]
                        if last_sil is not None:
                            w.append(last_sil)
                    gm.append(p.add("tensor", lambda e, fm=fm, tt=tt, c=c: e.matmul(
                        S1p[:, 0:512], ab(bWG[0] + 1024 * c + 128 * fm, 128), ab(bXB + 1024 * c + 512 * tt, 512),
                        start=(c == 0), stop=(c == 7)), waits=w))
                sil = p.add("scalar", lambda e: e.activation(ab(bG, 512), S1p[:, 0:512], AF.Silu), waits=[gm[-1], (last_mi if last_mi is not None else None)], tag=f"sil{Q}.{fm}.{tt}")
                last_sil = sil
                um = []
                for c in range(8):
                    w = []
                    if c == 0:
                        w = [lsu]
                        if last_mi is not None:
                            w.append(last_mi)
                    um.append(p.add("tensor", lambda e, fm=fm, tt=tt, c=c: e.matmul(
                        S1p[:, 512:1024], ab(bWG[1] + 1024 * c + 128 * fm, 128), ab(bXB + 1024 * c + 512 * tt, 512),
                        start=(c == 0), stop=(c == 7)), waits=w))
                mi = p.add("vector", lambda e, fm=fm, tt=tt: e.tensor_tensor(
                    ab(bINT + 1024 * fm + 512 * tt, 512), ab(bG, 512), S1p[:, 512:1024], ALU.mult),
                    waits=[um[-1], sil], tag=f"smi{Q}.{fm}.{tt}")
                last_mi = mi
                gu_done.append(mi)
        cp[("sh_mi_last",)] = last_mi
        lsd = p.add("gpsimd", lambda e: e.dma_start(out=ab(bWG[0], 8192), in_=E["sdw"]), dma="wing", waits=[gu_done[-1], last_sil])
        last_rc = [cp.get(("lnb_pe_done",)), cp.get(("lnb_pe_done",))]
        dn_ops = []
        for dm in range(8):
            for tt in range(2):
                bank = tt
                psl = (S2p[:, 0:512], S2p[:, 512:1024])[bank]
                mm = []
                for c in range(8):
                    w = []
                    if c == 0:
                        w = [lsd] + gu_done[tt::2]
                        if last_rc[bank] is not None:
                            w.append(last_rc[bank])
                    mm.append(p.add("tensor", lambda e, dm=dm, tt=tt, c=c, psl=psl: e.matmul(
                        psl, ab(bWG[0] + 1024 * c + 128 * dm, 128), ab(bINT + 1024 * c + 512 * tt, 512),
                        start=(c == 0), stop=(c == 7)), waits=w))
                w2 = [mm[-1]]
                if ("shdma", Q - 1) in cp and dm == 0 and tt == 0:
                    w2.append(cp[("shdma", Q - 1)])
                rc = p.add("vector", lambda e, dm=dm, tt=tt, psl=psl: e.tensor_copy(
                    ab(bSH + 1024 * dm + 512 * tt, 512), psl), waits=w2, tag=f"sracc{Q}.{dm}.{tt}")
                last_rc[bank] = rc
                dn_ops.append(rc)
        cp[("shdn_last",)] = dn_ops[-1]
        cp[("lnb_pe_done",)] = dn_ops[-1]
        shd = p.add("sync", lambda e, tq=tq: e.dma_start(
            out=E["SHFM"].rearrange("(c p) t -> p c t", p=128)[:, :, tq:tq + 1024],
            in_=AB[:, bSH:bSH + 8192].rearrange("p (c t) -> p c t", c=8)), waits=dn_ops, dma="out")
        sh_dmas.append(shd)
        cp[("shdma", Q)] = shd
        cp[("ovp_free",)] = cp[("cmp_mm",)]

    for ei in range(NE * 2):
        wi = ei // 2
        wbuf = (wi + 2) % 3
        if ei % 2 == 0:
            if wi == 0:
                wld_w = [sh_dmas[-1]]
            elif wi == 1:
                wld_w = [cp[("shdn_last",)]]
            elif wi == 2:
                wld_w = [cp[("sh_mi_last",)]]
            else:
                wld_w = [cp[("emi_last", ei - 3)], cp[("edn_mm_last", ei - 3)]]
            lg_ = p.add("gpsimd", lambda e, wi=wi, wbuf=wbuf: e.dma_start(
                out=ab(bWG[wbuf], 4096), in_=E["gw"][wi]), dma="wing", waits=wld_w)
            lu_ = p.add("gpsimd", lambda e, wi=wi, wbuf=wbuf: e.dma_start(
                out=ab(bWG[wbuf] + 4096, 4096), in_=E["uw"][wi]), dma="wing")
            ld_ = p.add("gpsimd", lambda e, wi=wi, wbuf=wbuf: e.dma_start(
                out=ab(bWG[wbuf] + 8192, 4096), in_=E["dw"][wi]), dma="wing")
        wload = list(sc_dmas) if ei == 0 else [cp[("etr_last", ei - 1)]]
        lxc = []
        for t in range(8):
            lxc.append(p.add("sync", lambda e, ei=ei, t=t: e.dma_start(
                out=ab(bXB + 1024 * t, 1024),
                in_=E["XCOMP"][ei * C + 128 * t:ei * C + 128 * (t + 1), :]),
                waits=wload if t == 0 else [], dma="win"))
        xtrev = []
        for c in range(8):
            dstp = (trp0, trp1)[c % 2]
            trs = []
            for t in range(8):
                w = [lxc[t]]
                if t == 0:
                    if c >= 2:
                        w.append(cp[("xtrev", ei, c - 2)])
                    elif ("odtrev", ei - 1, c + 6) in cp:
                        w.append(cp[("odtrev", ei - 1, c + 6)])
                    elif ei == 0:
                        w.append(st["trev_last"][c % 2])
                trs.append(p.add("tensor", lambda e, c=c, t=t, dstp=dstp: e.transpose(
                    dstp[:, 128 * t:128 * t + 128], ab(bXB + 1024 * t + 128 * c, 128), ab(bIDENT, 128)), waits=w))
            w2 = [trs[-1]]
            if ("xgate", ei - 1) in cp and c < 2:
                w2.append(cp[("xgate", ei - 1)])
            if ei == 0:
                w2.append(sh_dmas[-1])
            tev = p.add("vector", lambda e, c=c, dstp=dstp: e.tensor_copy(ab(bSH + 1024 * c, 1024), dstp[:, :]), waits=w2)
            cp[("xtrev", ei, c)] = tev
            xtrev.append(tev)
        cp[("etr_last", ei)] = xtrev[-1]
        last_esil = cp.get(("esil_last",))
        last_emi = cp.get(("emi_bank_last",))
        emi = []
        for fm in range(4):
            for tt in range(2):
                gm = []
                for c in range(8):
                    w = []
                    if c == 0:
                        w = [lg_, xtrev[c]] if fm == 0 and tt == 0 else [lg_]
                        w += [x for x in (last_esil,) if x is not None]
                        if fm == 0 and tt == 0:
                            w += xtrev
                    gm.append(p.add("tensor", lambda e, fm=fm, tt=tt, c=c, wbuf=wbuf: e.matmul(
                        S1p[:, 0:512], ab(bWG[wbuf] + 512 * c + 128 * fm, 128), ab(bSH + 1024 * c + 512 * tt, 512),
                        start=(c == 0), stop=(c == 7)), waits=w))
                cp[("xgate", ei)] = gm[-1]
                sil = p.add("scalar", lambda e: e.activation(ab(bG, 512), S1p[:, 0:512], AF.Silu),
                            waits=[gm[-1]] + ([last_emi] if last_emi is not None else []))
                last_esil = sil
                um = []
                for c in range(8):
                    w = []
                    if c == 0:
                        w = [lu_]
                        if last_emi is not None:
                            w.append(last_emi)
                    um.append(p.add("tensor", lambda e, fm=fm, tt=tt, c=c, wbuf=wbuf: e.matmul(
                        S1p[:, 512:1024], ab(bWG[wbuf] + 4096 + 512 * c + 128 * fm, 128), ab(bSH + 1024 * c + 512 * tt, 512),
                        start=(c == 0), stop=(c == 7)), waits=w))
                mi1 = p.add("vector", lambda e, fm=fm, tt=tt: e.tensor_tensor(
                    ab(bINT + 1024 * fm + 512 * tt, 512), ab(bG, 512), S1p[:, 512:1024], ALU.mult),
                    waits=[um[-1], sil])
                last_emi = mi1
                emi.append(mi1)
        cp[("esil_last",)] = last_esil
        cp[("emi_bank_last",)] = last_emi
        cp[("emi_last", ei)] = emi[-1]
        last_rc2 = cp.get(("edn_rc_last",), [None, None])
        edn = []
        for dm in range(8):
            for tt in range(2):
                bank = tt
                psl = (S2p[:, 0:512], S2p[:, 512:1024])[bank]
                mm = []
                for c in range(4):
                    w = []
                    if c == 0:
                        w = [ld_] + emi[tt::2]
                        if last_rc2[bank] is not None:
                            w.append(last_rc2[bank])
                    mm.append(p.add("tensor", lambda e, dm=dm, tt=tt, c=c, psl=psl, wbuf=wbuf: e.matmul(
                        psl, ab(bWG[wbuf] + 8192 + 1024 * c + 128 * dm, 128), ab(bINT + 1024 * c + 512 * tt, 512),
                        start=(c == 0), stop=(c == 3)), waits=w))
                w2 = [mm[-1]]
                if ("odtr_last", ei - 1) in cp and dm == 0 and tt == 0:
                    w2.append(cp[("odtr_last", ei - 1)])
                rc = p.add("vector", lambda e, dm=dm, tt=tt, psl=psl: e.tensor_copy(
                    ab(bDWN + 1024 * dm + 512 * tt, 512), psl), waits=w2)
                last_rc2[bank] = rc
                edn.append(rc)
        cp[("edn_rc_last",)] = last_rc2
        cp[("edn_mm_last", ei)] = mm[-1]
        oglast = None
        for t in range(8):
            dstp = (trp0, trp1)[t % 2]
            trs = []
            for dm in range(8):
                w = [edn[2 * dm + (t // 4)]]
                if dm == 0 and ("odtrev", ei, t - 2) in cp:
                    w.append(cp[("odtrev", ei, t - 2)])
                elif dm == 0 and t < 2:
                    w.append(cp[("xtrev", ei, t + 6)])
                trs.append(p.add("tensor", lambda e, t=t, dm=dm, dstp=dstp: e.transpose(
                    dstp[:, 128 * dm:128 * dm + 128], ab(bDWN + 1024 * dm + 128 * t, 128), ab(bIDENT, 128)), waits=w))
            slot = t % 2
            wev = [trs[-1]]
            if ("xod", ei, t - 2) in cp:
                wev.append(cp[("xod", ei, t - 2)])
            elif t < 2 and ("xod", ei - 1, t + 6) in cp:
                wev.append(cp[("xod", ei - 1, t + 6)])
            oev = p.add("vector", lambda e, t=t, slot=slot, dstp=dstp: e.tensor_copy(
                ab(bXO + 1024 * slot, 1024), dstp[:, :]), waits=wev)
            cp[("odtrev", ei, t)] = oev
            xod = p.add("sync", lambda e, ei=ei, t=t, slot=slot: e.dma_start(
                out=E["XOUT"][ei * C + 128 * t:ei * C + 128 * (t + 1), :],
                in_=ab(bXO + 1024 * slot, 1024)), waits=[oev], dma="out")
            cp[("xod", ei, t)] = xod
            oglast = xod
        cp[("odtr_last", ei)] = cp[("odtrev", ei, 7)]
        cp[("escat_last", ei)] = oglast

    all_scats = [cp[("escat_last", e_)] for e_ in range(NE * 2)] + [cp[("xod", NE * 2 - 1, t)] for t in range(8)]
    slcv_all = [cp[("slcv", gg)] for gg in range(32)]
    bGT4 = [bXO, bXO + 1024, bDWN, bDWN + 1024]
    x1f_r2 = x1f_r
    shf_r = E["SHFM"].rearrange("(c p) t -> p c t", p=128)

    def gat_prep(g):
        fSL = fSLC + 8 * (g % 2)
        aco = 2 * (g % 2)
        wl = list(slcv_all) if g == 0 else []
        if ("gmulB", g - 2) in cp:
            wl.append(cp[("gmulB", g - 2)])
        lsl = p.add("sync", lambda e, g=g, fSL=fSL: e.dma_start(
            out=AFr[:, fSL:fSL + 4], in_=E["SLCV"][g * 128:(g + 1) * 128, :]), waits=wl, dma="win")
        for rk in range(2):
            wa = [lsl]
            if ("gat", g - 2, rk) in cp:
                wa.append(cp[("gat", g - 2, rk)])
            elif g == 0:
                wa.append(cp[("scx", 31, rk)])
            aic = p.add("vector", lambda e, rk=rk, fSL=fSL, aco=aco: e.tensor_copy(
                AI[:, aco + rk:aco + rk + 1], af(fSL + 2 * rk, 1)), waits=wa)
            gslot = bGT4[2 * (g % 2) + rk]
            wg_ = [aic] + (all_scats if g == 0 else [])
            if ("gmul", g - 2, rk) in cp:
                wg_.append(cp[("gmul", g - 2, rk)])
            cp[("gat", g, rk)] = p.add("gpsimd", lambda e, rk=rk, gslot=gslot, aco=aco: e.indirect_dma_start(
                AB[:, gslot:gslot + 1024], None,
                E["XOUT"], bass.IndirectOffsetOnAxis(AI[:, aco + rk:aco + rk + 1], 0)),
                waits=wg_, dma="wsc")

    gat_prep(0)
    for g in range(32):
        if g + 1 < 32:
            gat_prep(g + 1)
        fSL = fSLC + 8 * (g % 2)
        lx1 = p.add("sync", lambda e, g=g: e.dma_start(
            out=AFr[:, fX1Q:fX1Q + 1024].rearrange("p (c t) -> p c t", c=8),
            in_=x1f_r2[:, :, 128 * g:128 * g + 128]), dma="win",
            waits=[cp[("ydma", g - 2)]] if ("ydma", g - 2) in cp else [])
        lsh = p.add("sync", lambda e, g=g: e.dma_start(
            out=AB[:, bSHF:bSHF + 1024].rearrange("p (c t) -> p c t", c=8),
            in_=shf_r[:, :, 128 * g:128 * g + 128]), dma="win",
            waits=sh_dmas if g == 0 else [])
        cA = p.add("scalar", lambda e, g=g, fSL=fSL: e.activation(
            af(fRAC, 1024), AB[:, bGT4[2 * (g % 2)]:bGT4[2 * (g % 2)] + 1024], AF.Copy, scale=af(fSL + 1, 1)),
            waits=[cp[("gat", g, 0)], cp.get(("ra1_prev",))])
        cB = p.add("scalar", lambda e, g=g, fSL=fSL: e.activation(
            af(fGB, 1024), AB[:, bGT4[2 * (g % 2) + 1]:bGT4[2 * (g % 2) + 1] + 1024], AF.Copy, scale=af(fSL + 3, 1)),
            waits=[cp[("gat", g, 1)], cp.get(("rb_prev",))])
        cp[("gmul", g, 0)] = cA
        cp[("gmul", g, 1)] = cB
        cp[("gmulB", g)] = cB
        cmb = p.add("vector", lambda e: e.tensor_tensor(af(fCMB, 1024), af(fX1Q, 1024), ab(bSHF, 1024), ALU.add),
                    waits=[lx1, lsh, cp.get(("yts_last",))], tag=f"cmb{g}")
        radd = []
        for half in range(2):
            trs = []
            for c4 in range(4):
                c = 4 * half + c4
                w = [cmb]
                if half == 0 and c4 == 0 and ("yev", g - 1) in cp:
                    w.append(cp[("yev", g - 1)])
                if c4 == 0 and half == 1:
                    w.append(radd[0])
                trs.append(p.add("tensor", lambda e, c=c, c4=c4: e.transpose(
                    ovp[:, 128 * c4:128 * c4 + 128], af(fCMB + 128 * c, 128), af(fIDENT, 128)), waits=w))
            ra = p.add("vector", lambda e, half=half: e.tensor_tensor(
                af(fR2 + 512 * half, 512), ovp[:, 0:512], af(fRAC + 512 * half, 512), ALU.add),
                waits=[trs[-1], cA], tag=f"ra{g}.{half}")
            radd.append(ra)
        cp[("ra1_prev",)] = radd[-1]
        rb = p.add("vector", lambda e: e.tensor_tensor(af(fR2, 1024), af(fR2, 1024), af(fGB, 1024), ALU.add),
                   waits=[radd[-1], cB])
        cp[("rb_prev",)] = rb
        cp[("yev", g)] = rb
        fLO = fLN2[g % 2]
        s1 = p.add("scalar", lambda e, fLO=fLO: e.activation(af(fLO, 1024), af(fR2, 1024), AF.Identity, accum_out=af(fT2, 1)),
                   waits=[rb] + ([cp[("ydma", g - 2)]] if ("ydma", g - 2) in cp else []))
        s2 = p.add("scalar", lambda e, fLO=fLO: e.activation(af(fLO, 1024), af(fR2, 1024), AF.Square, accum_out=af(fT2 + 1, 1)))
        mb2 = p.add("vector", lambda e: e.tensor_scalar(af(fT2 + 2, 1), af(fT2, 1), 1.0 / D, None, ALU.mult), waits=[s1, s2])
        tb2 = p.add("vector", lambda e: e.tensor_scalar(af(fT2 + 3, 1), af(fT2 + 1, 1), 1.0 / D, EPS, ALU.mult, ALU.add))
        m2b = p.add("vector", lambda e: e.tensor_tensor(af(fT2 + 4, 1), af(fT2 + 2, 1), af(fT2 + 2, 1), ALU.mult))
        vrb = p.add("vector", lambda e: e.tensor_tensor(af(fT2 + 4, 1), af(fT2 + 3, 1), af(fT2 + 4, 1), ALU.subtract))
        lnb2 = p.add("scalar", lambda e: e.activation(af(fT2 + 5, 1), af(fT2 + 4, 1), AF.Ln), waits=[vrb])
        rsb2 = p.add("scalar", lambda e: e.activation(af(fT2 + 6, 1), af(fT2 + 5, 1), AF.Exp, scale=-0.5))
        yts = p.add("vector", lambda e, fLO=fLO: e.tensor_scalar(af(fLO, 1024), af(fR2, 1024), af(fT2 + 2, 1), af(fT2 + 6, 1), ALU.subtract, ALU.mult), waits=[rsb2], tag=f"yts{g}")
        cp[("yts_last",)] = yts
        yd = p.add("sync", lambda e, g=g, fLO=fLO: e.dma_start(
            out=E["YT"][128 * g:128 * (g + 1), :], in_=af(fLO, 1024)), waits=[yts], dma="out")
        cp[("ydma", g)] = yd
        y_dmas.append(yd)

    p.add("sync", lambda e: e.nop(), waits=y_dmas)


def swz(m):
    m = np.asarray(m, dtype=np.float32)
    c = m.shape[0] // 128
    return np.ascontiguousarray(m.reshape(c, 128, m.shape[1]).transpose(1, 0, 2).reshape(128, -1))


def make_inputs_for_core(inputs, b):
    f32 = lambda x: np.ascontiguousarray(np.asarray(x, dtype=np.float32))
    d = {}
    d["XT"] = f32(np.asarray(inputs["L0_attn_X"][b]).T)
    d["WqT"] = swz(np.asarray(inputs["L0_attn_Wq"]).T)
    d["WkT"] = swz(np.asarray(inputs["L0_attn_Wk"]).T)
    d["WvT"] = swz(np.asarray(inputs["L0_attn_Wv"]).T)
    d["WoT"] = swz(np.asarray(inputs["L0_attn_Wo"]).T)
    d["lq1"] = f32(inputs["L0_attn_lambda_q1"])
    d["lk1"] = f32(inputs["L0_attn_lambda_k1"])
    d["lq2"] = f32(inputs["L0_attn_lambda_q2"])
    d["lk2"] = f32(inputs["L0_attn_lambda_k2"])
    d["rms_scale"] = f32(np.asarray(inputs["L0_attn_rms_scale"])[None, :])
    d["rw"] = swz(np.asarray(inputs["L0_ffn_router_w"]))
    d["gw"] = np.stack([swz(np.asarray(inputs["L0_ffn_gate_w"])[e]) for e in range(NE)])
    d["uw"] = np.stack([swz(np.asarray(inputs["L0_ffn_up_w"])[e]) for e in range(NE)])
    d["dw"] = np.stack([swz(np.asarray(inputs["L0_ffn_down_w"])[e]) for e in range(NE)])
    d["sgw"] = swz(np.asarray(inputs["L0_ffn_shared_gate_w"]))
    d["suw"] = swz(np.asarray(inputs["L0_ffn_shared_up_w"]))
    d["sdw"] = swz(np.asarray(inputs["L0_ffn_shared_down_w"]))
    d["identf"] = np.eye(128, dtype=np.float32)
    d["onesf"] = np.ones((1, 128), dtype=np.float32)
    d["iotap"] = np.arange(128, dtype=np.float32).reshape(128, 1)
    d["eids"] = np.arange(NE, dtype=np.float32).reshape(1, NE)
    d["ltri"] = np.triu(np.ones((128, 128), dtype=np.float32), 1)
    return d


def kernel(**inputs):
    nc = build_program()
    in_maps = [make_inputs_for_core(inputs, b) for b in range(8)]
    res = run_bass_kernel_spmd(nc, in_maps, core_ids=list(range(8)))
    out = np.stack([np.asarray(r["YT"]) for r in res.results], axis=0)
    return np.ascontiguousarray(out).astype(np.float32)
